# version 04
# baseline (speedup 1.0000x reference)
import numpy as np
from contextlib import ExitStack

import concourse.bass as bass
import concourse.mybir as mybir
from concourse.bass_utils import run_bass_kernel_spmd

F32 = mybir.dt.float32
BF16 = mybir.dt.bfloat16
I32 = mybir.dt.int32
AF = mybir.ActivationFunctionType
ALU = mybir.AluOpType
AX = mybir.AxisListType.X

S, D, FF = 4096, 1024, 4096
P1SUB = "ATBC"
LNLVL = 9
P0LVL = 9
NH, HD = 16, 64
CH = 2048
LB = 256
SBK = 1024
EPS_LN = 1e-5
EPS_B = 1e-6
EPS_Q = 1e-8
NCORES = 8

ENGS = ("sync", "vector", "scalar", "tensor", "gpsimd")


class K:

    def __init__(self, nc, es):
        self.nc = nc
        self.es = es
        self.ops = {e: [] for e in ENGS}
        self.sem = {}
        self.cnt = {}
        self.chan_eng = {}
        for e in ENGS:
            self.sem["d_" + e] = es.enter_context(nc.semaphore("d_" + e))
        self.sem["cc"] = es.enter_context(nc.semaphore("cc"))

    def _bump(self, name, by):
        self.cnt[name] = self.cnt.get(name, 0) + by
        return (name, self.cnt[name])

    def op(self, eng, fn, sig=False):
        if not sig:
            self.ops[eng].append(fn)
            return None
        name = "d_" + eng
        tok = self._bump(name, 1)
        s = self.sem[name]
        self.ops[eng].append(lambda e, fn=fn, s=s: fn(e).then_inc(s, 1))
        return tok

    def sig(self, eng):
        nm = "d_" + eng
        v = self.cnt.get(nm, 0)
        if v:
            s = self.sem[nm]
            self.ops[eng].append(lambda e, s=s, v=v: e.wait_ge(s, v))
        return self.op(eng, lambda e: e.nop(nofuse=True), sig=True)

    def dma(self, eng, out, in_, ch, slow=False):
        name = "dc_" + ch
        if name not in self.sem:
            self.sem[name] = self.es.enter_context(
                self.nc.semaphore(name))
        self.chan_eng.setdefault(eng, set()).add(name)
        tok = self._bump(name, 16)
        s = self.sem[name]
        self.ops[eng].append(
            lambda e, o=out, i=in_, s=s, sl=slow: e.dma_start(
                out=o, in_=i, allow_slow_non_contiguous=sl)
            .then_inc(s, 16))
        return tok

    def await_(self, eng, *toks):
        best = {}
        for t in toks:
            if t is None:
                continue
            nm, v = t
            best[nm] = max(best.get(nm, 0), v)
        for nm, v in best.items():
            s = self.sem[nm]
            self.ops[eng].append(lambda e, s=s, v=v: e.wait_ge(s, v))

    def drain(self, eng):
        self.ops[eng].append(lambda e: e.drain())

    def cc_allreduce_max(self, ins_ap, outs_ap):
        tok = self._bump("cc", 1)
        s = self.sem["cc"]
        self.ops["gpsimd"].append(
            lambda e, i=ins_ap, o=outs_ap, s=s: e.collective_compute(
                "AllReduce", ALU.max, replica_groups=[list(range(NCORES))],
                ins=[i], outs=[o]).then_inc(s, 1))
        return tok


def build_nc(dbg=False, nphases=5):
    nc = bass.Bass("TRN2", target_bir_lowering=False)

    def din(name, shape, dt=F32):
        return nc.dram_tensor(name, shape, dt, kind="ExternalInput").ap()

    x_in = din("x", [S, D])
    aWqT, aWkT = din("aWqT", [D, D]), din("aWkT", [D, D])
    aWvT, aWoT = din("aWvT", [D, D]), din("aWoT", [D, D])
    bWqT, bWkT, bWvT = (din("bWqT", [D, D]), din("bWkT", [D, D]),
                        din("bWvT", [D, D]))
    bq_h, bk_h, bv_h = din("bq", [1, D]), din("bk", [1, D]), din("bv", [1, D])
    gam_h = din("gam", [1, 1])
    w1_h, w1T_h = din("w1", [FF, D]), din("w1T", [D, FF])
    w2_h, w2T_h = din("w2", [D, FF]), din("w2T", [FF, D])
    b1_h, b2_h = din("b1", [1, FF]), din("b2", [1, D])
    c_id = din("c_id", [128, 128])
    c_idb = din("c_idb", [128, 128], BF16)
    c_tri = din("c_tri", [128, 128])
    c_tmj = din("c_tmj", [256, 256])
    c_m01 = din("c_m01", [256, 256])
    c_t1 = din("c_t1", [128, LB])
    c_lmj = din("c_lmj", [128, 2])

    y_out = nc.dram_tensor("y", [S, D], F32, kind="ExternalOutput").ap()

    okind = "ExternalOutput" if dbg else "Internal"
    xs1 = nc.dram_tensor("xs1", [S, D], F32, kind=okind).ap()
    xs2 = nc.dram_tensor("xs2", [S, D], F32, kind=okind).ap()
    xs3 = nc.dram_tensor("xs3", [S, D], F32, kind=okind).ap()
    hsp = nc.dram_tensor("hsp", [16, 128, 8192], F32).ap()
    w1q_h = nc.dram_tensor("w1qs", [D, FF], BF16,
                           kind=okind).ap()
    w2q_h = nc.dram_tensor("w2qs", [FF, D], BF16, kind=okind).ap()
    sw1_h = nc.dram_tensor("sw1s", [1, FF], F32, kind=okind).ap()
    sw2_h = nc.dram_tensor("sw2s", [1, D], F32).ap()
    srow_h = nc.dram_tensor("srows", [8, 128], F32).ap()
    vrow_h = nc.dram_tensor("vrows", [4, FF], F32).ap()
    cc_i = nc.dram_tensor("cc_i", [1, 128], F32)
    cc_o = nc.dram_tensor("cc_o", [1, 128], F32)

    es = ExitStack()
    k = K(nc, es)

    def sbuf(name, shape, dt=F32):
        return es.enter_context(nc.sbuf_tensor(name, shape, dt))

    PS = es.enter_context(nc.psum_tensor("PS", [128, 4096], F32))

    def psb(bank, w=512, off=0, p0=0, p1=128):
        return PS[p0:p1, bank * 512 + off: bank * 512 + off + w]

    id_f = sbuf("id_f", [128, 128])
    id_b = sbuf("id_b", [128, 128], BF16)
    tri = sbuf("tri", [128, 128])
    bvrep = sbuf("bvrep", [128, D])
    bqb, bkb = sbuf("bqb", [128, 8]), sbuf("bkb", [128, 8])
    b1b, b2b = sbuf("b1b", [128, 32]), sbuf("b2b", [128, 8])
    sw1b, sw2b = sbuf("sw1b", [128, 32]), sbuf("sw2b", [128, 8])
    Mt = sbuf("Mt", [128, 2 * 256])
    gvrep = sbuf("gvrep", [128, LB])
    gkv = sbuf("gkv", [128, 2])
    gLv = sbuf("gLv", [128, 1])
    dgl = sbuf("dgl", [128, 128])
    lng = sbuf("lng", [128, 1])
    smv = sbuf("smv", [128, 16])
    st_mu = sbuf("st_mu", [128, 32])
    st_rs = sbuf("st_rs", [128, 32])
    st_mx = sbuf("st_mx", [128, 33])
    sq_scr = sbuf("sq_scr", [128, 1024])
    s_tmp = sbuf("s_tmp", [128, 4])

    ones_r = sbuf("ones_r", [1, 128])
    g11 = sbuf("g11", [1, 2])
    swrow = sbuf("swrow", [1, 1024])

    t_cb = {6: None, 7: None}

    def col2row(dst_row, src_sb_col_fn, m, t_pre=None):
        tev = None
        for j in range(m):
            bank = 6 + j % 2
            k.await_("tensor", t_pre, t_cb[bank])
            t_tr = k.op("tensor", lambda e, bank=bank, j=j: e.transpose(
                psb(bank, 128, p1=1), src_sb_col_fn(j), id_f[:, :]), sig=True)
            k.await_("scalar", t_tr, tev)
            tev = k.op("scalar", lambda e, bank=bank, j=j: e.copy(
                dst_row[0:1, j * 128:(j + 1) * 128], psb(bank, 128, p1=1)),
                sig=True)
            t_cb[bank] = tev
        return tev

    def bcastN(dst_sb_fn, src_row, n, t_pre=None, ev_eng="scalar"):
        tev = None
        for nb in range((n + 511) // 512):
            w = min(512, n - nb * 512)
            bank = 6 + nb % 2
            k.await_("tensor", t_pre, t_ones_r, t_cb[bank])
            t_mm = k.op("tensor", lambda e, bank=bank, nb=nb, w=w: e.matmul(
                psb(bank, w), ones_r[0:1, :],
                src_row[0:1, nb * 512:nb * 512 + w],
                start=True, stop=True), sig=True)
            k.await_(ev_eng, t_mm, tev)
            tev = k.op(ev_eng, lambda e, bank=bank, nb=nb, w=w: e.tensor_copy(
                dst_sb_fn(nb * 512, w), psb(bank, w))
                if ev_eng == "vector" else e.copy(
                    dst_sb_fn(nb * 512, w), psb(bank, w)), sig=True)
            t_cb[bank] = tev
        return tev

    k.op("vector", lambda e: e.memset(ones_r[:, :], 1.0))
    k.drain("vector")
    t_ones_r = k.sig("vector")

    t_ld = [
        k.dma("sync", id_f[:, :], c_id, "c0"),
        k.dma("sync", id_b[:, :], c_idb, "c0"),
        k.dma("sync", tri[:, :], c_tri, "c0"),

        k.dma("sync", bqb[:, :], bq_h.rearrange("o (c p) -> (o p) c", p=128), "c0", slow=True),
        k.dma("sync", bkb[:, :], bk_h.rearrange("o (c p) -> (o p) c", p=128), "c0", slow=True),
        k.dma("sync", b1b[:, :], b1_h.rearrange("o (c p) -> (o p) c", p=128), "c0", slow=True),
        k.dma("sync", b2b[:, :], b2_h.rearrange("o (c p) -> (o p) c", p=128), "c0", slow=True),
        k.dma("sync", g11[0:1, 0:1], gam_h, "c0"),
    ]
    t_bvr = k.dma("sync", swrow[0:1, 0:1024], bv_h, "mis0")
    t_bvr2 = bcastN(lambda o, w: bvrep[:, o:o + w], swrow, 1024, t_bvr,
                    "vector")

    k.await_("scalar", *t_ld)
    t_ln0 = k.op("scalar", lambda e: e.activation(g11[0:1, 0:1], g11[0:1, 0:1],
                                                  AF.Ln), sig=True)
    t_ln = bcastN(lambda o, w: lng[:, o:o + w], g11, 1, t_ln0)
    t_m = [k.dma("sync", sq_scr[:, 0:256], c_tmj[0:128, :], "tb0"),
           k.dma("sync", sq_scr[:, 256:512], c_tmj[128:256, :], "tb0")]
    k.await_("vector", t_ln, *t_m)
    k.op("vector", lambda e: e.tensor_scalar(sq_scr[:, 0:512],
                                             sq_scr[:, 0:512],
                                             lng[:, 0:1], None, ALU.mult))
    k.drain("vector")
    t_v = k.sig("vector")
    k.await_("scalar", t_v)
    t_exp = k.op("scalar", lambda e: e.activation(Mt[:, :], sq_scr[:, 0:512],
                                                  AF.Exp), sig=True)
    k.await_("sync", t_exp)
    t_m01 = [k.dma("sync", sq_scr[:, 0:256], c_m01[0:128, :], "tb1"),
             k.dma("sync", sq_scr[:, 256:512], c_m01[128:256, :], "tb1")]
    k.await_("vector", t_exp, *t_m01)
    k.op("vector", lambda e: e.tensor_mul(Mt[:, :], Mt[:, :],
                                          sq_scr[:, 0:512]))
    k.drain("vector")
    t_g = [k.dma("sync", gvrep[:, :], c_t1, "tb2"),
           k.dma("sync", gkv[:, :], c_lmj, "tb2")]
    k.await_("vector", *t_g)
    k.op("vector", lambda e: e.tensor_scalar(gvrep[:, :], gvrep[:, :],
                                             lng[:, 0:1], None, ALU.mult))
    k.op("vector", lambda e: e.tensor_scalar(gkv[:, :], gkv[:, :],
                                             lng[:, 0:1], None, ALU.mult))
    k.op("vector", lambda e: e.tensor_scalar(gLv[:, :], lng[:, 0:1],
                                             float(LB), None, ALU.mult))
    k.drain("vector")
    t_v2 = k.sig("vector")
    k.await_("scalar", t_v2)
    k.op("scalar", lambda e: e.activation(gvrep[:, :], gvrep[:, :], AF.Exp))
    k.op("scalar", lambda e: e.activation(gkv[:, :], gkv[:, :], AF.Exp))
    t_gl = k.op("scalar", lambda e: e.activation(gLv[:, :], gLv[:, :],
                                                 AF.Exp), sig=True)
    k.await_("vector", t_gl)
    k.op("vector", lambda e: e.tensor_scalar(dgl[:, :], id_f[:, :],
                                             gLv[:, 0:1], None, ALU.mult))
    k.drain("vector")
    t_tables = k.sig("vector")

    p0es = ExitStack()

    def sbuf0(name, shape, dt=F32):
        return p0es.enter_context(nc.sbuf_tensor(name, shape, dt))

    wnat = sbuf0("p0_wnat", [128, FF])
    wtb = [sbuf0(f"p0_wt{i}", [128, 512]) for i in range(2)]
    srep = sbuf0("p0_sr", [128, 512])
    wi32 = sbuf0("p0_wi", [128, 512], I32)
    wr0 = sbuf0("p0_wr0", [128, 512])
    wqb = [sbuf0(f"p0_wqb{i}", [128, 512], BF16) for i in range(2)]

    def wquant(w_nat, wT, sw_sb, sw_hbm, wq_hbm, nf, nd):
        nft = nf // 128
        for ft in range(nft):
            tl = k.dma("sync", wnat[:, 0:nd], w_nat[ft * 128:(ft + 1) * 128, :], "wn")
            k.await_("vector", tl)
            tr = k.op("vector", lambda e, ft=ft: e.reduce_max(
                sw_sb[:, ft:ft + 1], wnat[:, 0:nd], axis=AX,
                apply_absolute_value=True), sig=True)
            k.await_("sync", tr)
        k.drain("vector")
        k.op("vector", lambda e: e.tensor_scalar(
            sw_sb[:, 0:nft], sw_sb[:, 0:nft], EPS_Q, 1.0 / 127.0,
            ALU.max, ALU.mult))
        k.drain("vector")
        t_sw = k.sig("vector")
        t_sw_pe = t_sw
        k.await_("gpsimd", t_sw)
        t_st = k.dma("gpsimd", sw_hbm.rearrange("o (c p) -> (o p) c", p=128),
                     sw_sb[:, 0:nft], "wst", slow=True)
        ndt, nfb = nd // 128, nf // 512
        t_wq = wq_state["t_wq"]
        t_wtb_free = wq_state["t_wtb_free"]
        t_srep_free = wq_state["t_srep_free"]
        t_wcast = wq_state["t_srep_free"]
        for fb in range(nfb):
            t_row = col2row(swrow, lambda j, fb=fb: sw_sb[:, fb * 4 + j:
                                                          fb * 4 + j + 1],
                            4, t_sw_pe)
            k.await_("tensor", t_srep_free)
            t_bc = bcastN(lambda o, w: srep[:, o:o + w], swrow, 512, t_row,
                          "scalar")
            k.await_("vector", t_bc, t_srep_free)
            k.op("vector", lambda e: e.reciprocal(srep[:, :], srep[:, :]))
            k.drain("vector")
            for dt in range(ndt):
                b = dt % 2
                k.await_("sync", t_wtb_free[b])
                tw = k.dma("sync", wtb[b][:, :],
                           wT[dt * 128:(dt + 1) * 128,
                              fb * 512:(fb + 1) * 512], f"wt{b}")
                k.await_("vector", tw, t_wq[b])
                k.op("vector", lambda e, b=b: e.tensor_mul(
                    wtb[b][:, :], wtb[b][:, :], srep[:, :]))
                k.drain("vector")
                k.op("vector", lambda e, b=b: e.tensor_scalar(
                    wtb[b][:, :], wtb[b][:, :], 128.0, None, ALU.add))
                k.drain("vector")
                t_wm = k.sig("vector")
                k.await_("scalar", t_wm, t_wcast)
                k.op("scalar", lambda e, b=b: e.copy(wi32[:, :],
                                                     wtb[b][:, :]))
                k.drain("scalar")
                t_wc = k.op("scalar", lambda e: e.copy(wr0[:, :],
                                                       wi32[:, :]), sig=True)
                k.await_("vector", t_wc)
                k.op("vector", lambda e, b=b: e.tensor_sub(
                    wtb[b][:, :], wtb[b][:, :], wr0[:, :]))
                k.drain("vector")
                k.op("vector", lambda e, b=b: e.scalar_tensor_tensor(
                    wr0[:, :], wtb[b][:, :], 0.5, wr0[:, :],
                    ALU.is_ge, ALU.add))
                k.drain("vector")
                k.op("vector", lambda e: e.tensor_scalar(
                    wr0[:, :], wr0[:, :], 128.0, -128.0,
                    ALU.subtract, ALU.max))
                k.drain("vector")
                t_wtb_free[b] = k.sig("vector")
                tq = k.op("vector", lambda e, b=b: e.tensor_scalar(
                    wqb[b][:, :], wr0[:, :], 127.0, None, ALU.min), sig=True)
                t_wcast = tq
                k.await_("gpsimd", tq)
                t_wq[b] = k.dma("gpsimd",
                                wq_hbm[dt * 128:(dt + 1) * 128,
                                       fb * 512:(fb + 1) * 512], wqb[b][:, :],
                                f"ws{b}")
            t_srep_free = t_wtb_free[(ndt - 1) % 2]
            wq_state["t_srep_free"] = t_srep_free

    wq_state = {"t_wq": [None, None], "t_wtb_free": [None, None],
                "t_srep_free": None}
    if P0LVL >= 2 and P0LVL != 4:
        wquant(w1_h, w1T_h, sw1b, sw1_h, w1q_h, FF, D)
    if P0LVL >= 3:
        wquant(w2_h, w2T_h, sw2b, sw2_h, w2q_h, D, FF)

    _ln_prev = [None]
    t_x_global = [None]

    def ln_stats(xtile, idx, want_max, t_x, t_slot_free=None):
        t_x_global[0] = t_x
        if LNLVL < 1:
            return None
        """stats of xtile into st_mu/st_rs[:, idx]; scale/bias form:
        ln(x) = st_rs*x + st_mu  (st_mu = -mu*rstd)."""
        k.await_("scalar", t_x, _ln_prev[0], t_slot_free)
        t_sq = k.op("scalar", lambda e: e.activation(
            sq_scr[:, :], xtile, AF.Square), sig=True)
        if LNLVL < 2:
            return t_sq
        k.await_("vector", t_x, t_slot_free)
        k.op("vector", lambda e: e.reduce_sum(s_tmp[:, 0:1], xtile, axis=AX))
        if want_max:
            k.op("vector", lambda e, i=idx: e.reduce_max(
                st_mx[:, i:i + 1], xtile, axis=AX))
        k.await_("vector", t_sq)
        k.op("vector", lambda e: e.reduce_sum(s_tmp[:, 1:2], sq_scr[:, :],
                                              axis=AX))
        k.drain("vector")
        if LNLVL < 3:
            return k.sig("vector")
        k.op("vector", lambda e, i=idx: e.tensor_scalar(
            st_mu[:, i:i + 1], s_tmp[:, 0:1], 1.0 / D, None, ALU.mult))
        k.drain("vector")
        k.op("vector", lambda e: e.tensor_scalar(
            s_tmp[:, 1:2], s_tmp[:, 1:2], 1.0 / D, None, ALU.mult))
        k.drain("vector")
        k.op("vector", lambda e, i=idx: e.scalar_tensor_tensor(
            s_tmp[:, 2:3], st_mu[:, i:i + 1], 1.0, st_mu[:, i:i + 1],
            ALU.bypass, ALU.mult))
        k.drain("vector")
        k.op("vector", lambda e: e.tensor_sub(s_tmp[:, 2:3], s_tmp[:, 1:2],
                                              s_tmp[:, 2:3]))
        k.drain("vector")
        k.op("vector", lambda e: e.tensor_scalar(
            s_tmp[:, 2:3], s_tmp[:, 2:3], EPS_LN, None, ALU.add))
        k.drain("vector")
        t_var = k.sig("vector")
        if LNLVL < 4:
            return t_var
        k.await_("scalar", t_var)
        t_rs = k.op("scalar", lambda e, i=idx: e.activation(
            st_rs[:, i:i + 1], s_tmp[:, 2:3], AF.Sqrt), sig=True)
        k.await_("vector", t_rs)
        k.op("vector", lambda e, i=idx: e.reciprocal(
            st_rs[:, i:i + 1], st_rs[:, i:i + 1]))
        k.drain("vector")
        k.op("vector", lambda e, i=idx: e.scalar_tensor_tensor(
            st_mu[:, i:i + 1], st_mu[:, i:i + 1], -1.0, st_rs[:, i:i + 1],
            ALU.mult, ALU.mult))
        k.drain("vector")
        _ln_prev[0] = k.sig("vector")
        return _ln_prev[0]

    def ln_apply(dst, xtile, idx, t_stats, sig=False):
        if LNLVL < 0:
            return k.sig("scalar")
        k.await_("scalar", t_stats)
        if LNLVL < 5:
            k.await_("scalar", t_x_global[0])
            return k.op("scalar", lambda e: e.copy(dst, xtile), sig=sig)
        return k.op("scalar", lambda e, i=idx: e.activation(
            dst, xtile, AF.Identity, bias=st_mu[:, i:i + 1],
            scale=st_rs[:, i:i + 1]), sig=sig)

    def barrier():
        for e, chans in k.chan_eng.items():
            for nm in sorted(chans):
                v = k.cnt.get(nm, 0)
                if v:
                    s = k.sem[nm]
                    k.ops[e].append(lambda en, s=s, v=v: en.wait_ge(s, v))
        toks = [k.sig(e) for e in ENGS]
        for e in ENGS:
            k.await_(e, *toks)

    barrier()
    p0es.close()
    if nphases < 1:
        blk = es.enter_context(nc.Block())

        def runner0(eng_name):
            def fn(e):
                for thunk in k.ops[eng_name]:
                    thunk(e)
            return fn
        blk.sync(runner0("sync"))
        blk.vector(runner0("vector"))
        blk.scalar(runner0("scalar"))
        blk.tensor(runner0("tensor"))
        blk.gpsimd(runner0("gpsimd"))
        es.close()
        return nc

    def build_p1():
        p = ExitStack()

        def psbuf(name, shape, dt=F32):
            return p.enter_context(nc.sbuf_tensor(name, shape, dt))

        if "N" in P1SUB:
            p.close()
            return
        u64 = psbuf("a_u64", [128, 8 * CH])
        xlnT = u64[:, :]
        wo_sb = u64[:, 0:8 * 1024]
        yT = u64[:, 8 * 1024:8 * 1024 + 1024]
        x1o = u64[:, 9 * 1024 + 256:11 * 1024 + 256]
        ytm = psbuf("a_ytm", [128, 16 * 1024])
        Qp = psbuf("a_Qp", [128, CH])
        Kp = psbuf("a_Kp", [128, CH])
        vaug = psbuf("a_vaug", [128, 16 * 129 + 1])
        PT = psbuf("a_PT", [128, 4 * 512])
        wq3 = psbuf("a_wq3", [128, 3 * 1024])
        xld = psbuf("a_xld", [128, 2 * 1024])
        xlb = psbuf("a_xlb", [128, 2 * 1024])

        t_chunk_pe = None
        t_ytm_freed = None
        t_pair_pe = None
        t_bank_ev = {i: None for i in range(8)}
        t_avevac = [[None, None] for _ in range(4)]
        t_ptfree = [[None, None], [None, None]]
        t_x1st = {}
        t_xuse = [None, None]
        t_xlb_pe = [None, None]
        t_trd = [None, None]
        t_slot = [None] * 4
        for c in range(2):
            tc = c * CH
            k.await_("scalar", t_x1st.get(0), t_x1st.get(1))
            for tt in range(16 if "A" in P1SUB else 0):
                b = tt % 2
                xs_ = xld[:, b * 1024:(b + 1) * 1024]
                xb_ = xlb[:, b * 1024:(b + 1) * 1024]
                k.await_("sync", t_xuse[b])
                t_x = k.dma("sync", xs_,
                            x_in[tc + tt * 128:tc + tt * 128 + 128, :],
                            f"xa{b}")
                t_st_ = ln_stats(xs_, tt % 4, False, t_x, t_slot[tt % 4])
                k.await_("scalar", t_xlb_pe[b], t_chunk_pe)
                t_ap = ln_apply(xb_, xs_, tt % 4, t_st_, sig=True)
                t_slot[tt % 4] = t_ap
                t_xuse[b] = t_ap
                for dt in range(8 if "T" in P1SUB else 0):
                    j = tt * 8 + dt
                    bank = 6 + (j % 2)
                    k.await_("tensor", t_ap, t_trd[j % 2])
                    t_tr = k.op("tensor", lambda e, bank=bank, dt=dt, xb_=xb_:
                                e.transpose(psb(bank, 128),
                                            xb_[:, dt * 128:(dt + 1) * 128],
                                            id_f[:, :]), sig=True)
                    t_xlb_pe[b] = t_tr
                    k.await_("scalar", t_tr)
                    t_ev = k.op("scalar", lambda e, bank=bank, dt=dt, tt=tt:
                                e.copy(xlnT[:, dt * CH + tt * 128:
                                            dt * CH + tt * 128 + 128],
                                       psb(bank, 128)), sig=True)
                    t_trd[j % 2] = t_ev
            t_A_done = t_trd[(16 * 8 - 1) % 2] if "T" in P1SUB else None
            if "B" not in P1SUB:
                continue

            for hp in range(8):
                k.await_("sync", t_pair_pe, t_chunk_pe)
                t_w = [k.dma("sync",
                             wq3[:, j * 1024:(j + 1) * 1024]
                             .rearrange("p (dt e) -> p dt e", e=128),
                             W[:, hp * 128:(hp + 1) * 128]
                             .rearrange("(dt p) e -> p dt e", p=128), "wp")
                       for j, W in enumerate((aWqT, aWkT, aWvT))]
                t_qk = []
                for (dst, wj) in ((Qp, 0), (Kp, 1)):
                    for nb in range(4):
                        bank = nb % 2
                        k.await_("tensor", *t_w, t_bank_ev[bank], t_A_done,
                                 t_pair_pe)
                        tok = None
                        for dt in range(8):
                            tok = k.op("tensor", lambda e, bank=bank, wj=wj,
                                       dt=dt, nb=nb:
                                       e.matmul(psb(bank),
                                                wq3[:, wj * 1024 + dt * 128:
                                                    wj * 1024 + dt * 128 + 128],
                                                xlnT[:, dt * CH + nb * 512:
                                                     dt * CH + nb * 512 + 512],
                                                start=(dt == 0),
                                                stop=(dt == 7)),
                                       sig=(dt == 7))
                        k.await_("scalar", tok)
                        t_bank_ev[bank] = k.op(
                            "scalar", lambda e, bank=bank, dst=dst, nb=nb:
                            e.copy(dst[:, nb * 512:(nb + 1) * 512], psb(bank)),
                            sig=True)
                        t_qk.append(t_bank_ev[bank])
                k.await_("vector", t_pair_pe)
                k.op("vector", lambda e: e.memset(
                    vaug[:, 0:16 * 129]
                    .rearrange("p (kt c) -> p kt c", c=129)[:, :, 64:65], 1.0))
                k.drain("vector")
                t_ones = k.sig("vector")
                t_ve = None
                for kt in range(16):
                    k.await_("tensor", t_bank_ev[2], t_pair_pe)
                    tokv = None
                    for dt in range(8):
                        tokv = k.op("tensor", lambda e, kt=kt, dt=dt:
                                    e.matmul(psb(2, 128),
                                             xlnT[:, dt * CH + kt * 128:
                                                  dt * CH + kt * 128 + 128],
                                             wq3[:, 2 * 1024 + dt * 128:
                                                 2 * 1024 + dt * 128 + 128],
                                             start=(dt == 0), stop=(dt == 7)),
                                    sig=(dt == 7))
                    k.await_("scalar", tokv, t_ones)
                    t_ve = k.op(
                        "scalar", lambda e, kt=kt:
                        e.copy(vaug[:, kt * 129:kt * 129 + 130]
                               .rearrange("p (a c) -> p a c", c=65)
                               [:, :, 0:64],
                               psb(2, 128).rearrange("p (a c) -> p a c",
                                                     c=64)),
                        sig=True)
                    t_bank_ev[2] = t_ve
                k.await_("tensor", t_ve, t_ones, *t_qk)
                for qb in range(4):
                    ktmax = 4 * qb + 4
                    t_avstop = [[None, None] for _ in range(4)]
                    for kt in range(ktmax):
                        off = max(0, kt - 4 * qb) * 128
                        Nq = 512 - off
                        for g in range(2):
                            sb_ = g * 2 + kt % 2
                            pt0 = (g * 2 + kt % 2) * 512
                            k.await_("tensor", t_bank_ev[sb_])
                            t_s = k.op("tensor", lambda e, sb_=sb_, off=off,
                                       Nq=Nq, g=g, kt=kt, qb=qb:
                                       e.matmul(psb(sb_, Nq, off),
                                                Kp[g * 64:(g + 1) * 64,
                                                   kt * 128:kt * 128 + 128],
                                                Qp[g * 64:(g + 1) * 64,
                                                   qb * 512 + off:
                                                   qb * 512 + 512],
                                                start=True, stop=True),
                                       sig=True)
                            k.await_("scalar", t_s, t_ptfree[g][kt % 2])
                            t_e = k.op("scalar", lambda e, pt0=pt0, off=off,
                                       Nq=Nq, sb_=sb_:
                                       e.activation(
                                           PT[:, pt0 + off:pt0 + 512],
                                           psb(sb_, Nq, off), AF.Exp,
                                           scale=0.125), sig=True)
                            t_bank_ev[sb_] = t_e
                            t_gate = t_e
                            if kt >= 4 * qb:
                                k.await_("gpsimd", t_e)
                                t_gate = k.op(
                                    "gpsimd", lambda e, pt0=pt0, off=off:
                                    e.tensor_mul(
                                        PT[:, pt0 + off:pt0 + off + 128],
                                        PT[:, pt0 + off:pt0 + off + 128],
                                        tri[:, :]), sig=True)
                            qs_lo = off // 128
                            k.await_("tensor", t_gate)
                            if kt == 0:
                                k.await_("tensor",
                                         *[t for qs in range(4)
                                           for t in t_avevac[qs]])
                            for qs in range(qs_lo, 4):
                                avb = 4 + qs // 2
                                avo = (qs % 2) * 256
                                first = (kt == 0 and g == 0 and qs % 2 == 0)
                                last = (kt == 4 * qb + qs)
                                tk = k.op("tensor", lambda e, avb=avb, avo=avo,
                                          g=g, pt0=pt0, qs=qs, kt=kt,
                                          first=first, last=last:
                                          e.matmul(
                                              psb(avb, 65, avo + 65 * g),
                                              PT[:, pt0 + qs * 128:
                                                  pt0 + qs * 128 + 128],
                                              vaug[:, kt * 129 + g * 64:
                                                   kt * 129 + g * 64 + 65],
                                              start=first, stop=last,
                                              skip_group_check=True),
                                          sig=last)
                                if last:
                                    t_avstop[qs][g] = tk
                            t_ptfree[g][kt % 2] = k.sig("tensor")
                    for qs in range(4):
                        tsub = qb * 4 + qs
                        avb = 4 + qs // 2
                        avo = (qs % 2) * 256
                        k.await_("vector", t_avstop[qs][0], t_avstop[qs][1],
                                 t_ytm_freed)
                        k.op("vector", lambda e, avb=avb, avo=avo:
                             e.reciprocal(s_tmp[:, 0:1],
                                          psb(avb, 1, avo + 64)))
                        k.drain("vector")
                        k.op("vector", lambda e, avb=avb, avo=avo:
                             e.reciprocal(s_tmp[:, 1:2],
                                          psb(avb, 1, avo + 65)))
                        k.drain("vector")
                        k.op("vector", lambda e, avb=avb, avo=avo, tsub=tsub,
                             hp=hp: e.tensor_scalar(
                                 ytm[:, tsub * 1024 + hp * 128:
                                     tsub * 1024 + hp * 128 + 64],
                                 psb(avb, 64, avo), s_tmp[:, 0:1], None,
                                 ALU.mult))
                        k.op("vector", lambda e, avb=avb, avo=avo, tsub=tsub,
                             hp=hp: e.tensor_scalar(
                                 ytm[:, tsub * 1024 + hp * 128 + 64:
                                     tsub * 1024 + hp * 128 + 128],
                                 psb(avb, 64, avo + 66), s_tmp[:, 1:2], None,
                                 ALU.mult))
                        k.drain("vector")
                        tv = k.sig("vector")
                        t_avevac[qs] = [tv, tv]
                t_pair_pe = k.sig("tensor")
            t_B_vec = k.sig("vector")
            if "C" not in P1SUB:
                t_chunk_pe = k.sig("tensor")
                t_ytm_freed = t_chunk_pe
                continue

            k.await_("sync", t_pair_pe, t_B_vec)
            t_wo = [k.dma("sync", wo_sb[:, ec * 1024:(ec + 1) * 1024],
                          aWoT[ec * 128:(ec + 1) * 128, :], "wo")
                    for ec in range(8)]
            t_yT_free = None
            t_res_use = list(t_xuse)
            t_trd2 = list(t_trd)
            k.await_("tensor", t_B_vec)
            for tsub in range(16):
                for ec in range(8):
                    bank = 6 + ec % 2
                    k.await_("tensor", t_trd2[ec % 2])
                    t_tr = k.op("tensor", lambda e, bank=bank, tsub=tsub,
                                ec=ec: e.transpose(
                                    psb(bank, 128),
                                    ytm[:, tsub * 1024 + ec * 128:
                                        tsub * 1024 + ec * 128 + 128],
                                    id_f[:, :]), sig=True)
                    k.await_("scalar", t_tr, t_yT_free)
                    t_trd2[ec % 2] = k.op(
                        "scalar", lambda e, bank=bank, ec=ec:
                        e.copy(yT[:, ec * 128:(ec + 1) * 128], psb(bank, 128)),
                        sig=True)
                b = tsub % 2
                k.await_("sync", t_res_use[b])
                t_xr = k.dma("sync", xld[:, b * 1024:(b + 1) * 1024],
                             x_in[tc + tsub * 128:tc + tsub * 128 + 128, :],
                             f"xa{b}")
                t_o = [None, None]
                k.await_("tensor", t_bank_ev[0], t_bank_ev[1], t_trd2[0],
                         t_trd2[1], *t_wo)
                for nb in range(2):
                    bank = nb
                    for ec in range(8):
                        tko = k.op("tensor", lambda e, bank=bank, ec=ec,
                                   nb=nb: e.matmul(
                                       psb(bank),
                                       yT[:, ec * 128:(ec + 1) * 128],
                                       wo_sb[:, ec * 1024 + nb * 512:
                                             ec * 1024 + nb * 512 + 512],
                                       start=(ec == 0), stop=(ec == 7)),
                                   sig=(ec == 7))
                    t_o[nb] = tko
                t_yT_free = k.sig("tensor")
                for nb in range(2):
                    k.await_("vector", t_o[nb], t_xr, t_x1st.get(b))
                    k.op("vector", lambda e, nb=nb, b=b: e.tensor_add(
                        x1o[:, b * 1024 + nb * 512:b * 1024 + nb * 512 + 512],
                        psb(nb),
                        xld[:, b * 1024 + nb * 512:b * 1024 + nb * 512 + 512]))
                    k.drain("vector")
                    t_bank_ev[nb] = k.sig("vector")
                t_res_use[b] = t_bank_ev[1]
                k.await_("gpsimd", t_bank_ev[0], t_bank_ev[1])
                t_x1st[b] = k.dma(
                    "gpsimd", xs1[tc + tsub * 128:tc + tsub * 128 + 128, :],
                    x1o[:, b * 1024:(b + 1) * 1024], f"x1s{b}")
            t_xuse = list(t_res_use)
            t_chunk_pe = k.sig("tensor")
            t_ytm_freed = t_chunk_pe
        if "Z" not in P1SUB:
            barrier()
        p.close()

    if nphases >= 1:
        build_p1()

    sc2 = sbuf("sc2", [1, 4])
    ccs = sbuf("ccs", [1, 128])
    f_ssw1 = sbuf("f_ssw1", [128, 32])
    f_b1q = sbuf("f_b1q", [128, 32])
    f_ssw2 = sbuf("f_ssw2", [128, 8])
    f_b2q = sbuf("f_b2q", [128, 8])
    f_i32s = sbuf("f_i32s", [128, 64], I32)
    hmax = sbuf("hmax", [128, 1])

    def global_max_scale(src_col, slot, t_ready):
        k.await_("tensor", t_ready)
        t_tr = k.op("tensor", lambda e: e.transpose(
            psb(7, 128, p1=1), src_col, id_f[:, :]), sig=True)
        k.await_("vector", t_tr)
        k.op("vector", lambda e: e.reduce_max(sc2[0:1, 0:1],
                                              psb(7, 128, p1=1), axis=AX))
        k.drain("vector")
        k.op("vector", lambda e: e.tensor_scalar(
            ccs[0:1, :], ones_r[0:1, :], sc2[0:1, 0:1], None, ALU.mult))
        k.drain("vector")
        t_v = k.sig("vector")
        k.await_("gpsimd", t_v)
        t_ci = k.dma("gpsimd", cc_i.ap(), ccs[0:1, :], "cc0")
        k.await_("gpsimd", t_ci)
        t_cc = k.cc_allreduce_max(cc_i.ap(), cc_o.ap())
        k.await_("gpsimd", t_cc)
        t_co = k.dma("gpsimd", sc2[0:1, 0:1], cc_o.ap()[0:1, 0:1], "cc0")
        k.await_("vector", t_co)
        k.op("vector", lambda e: e.tensor_scalar(
            sc2[0:1, 0:1], sc2[0:1, 0:1], EPS_Q, 1.0 / 255.0,
            ALU.max, ALU.mult))
        k.drain("vector")
        k.op("vector", lambda e: e.reciprocal(sc2[0:1, 1:2], sc2[0:1, 0:1]))
        k.drain("vector")
        t_v2 = k.sig("vector")
        t_bc = bcastN(lambda o, w, slot=slot: smv[:, slot + o:slot + o + w],
                      sc2, 2, t_v2)
        return t_bc

    def bias_quant(b_sb, sw_sb, ssw_dst, bq_dst, n, slot):
        k.op("vector", lambda e: e.tensor_scalar(
            ssw_dst[:, 0:n], sw_sb[:, 0:n], smv[:, slot:slot + 1], None,
            ALU.mult))
        k.drain("vector")
        k.op("vector", lambda e: e.reciprocal(sq_scr[:, 0:n], ssw_dst[:, 0:n]))
        k.drain("vector")
        k.op("vector", lambda e: e.tensor_mul(sq_scr[:, 32:32 + n],
                                              b_sb[:, 0:n], sq_scr[:, 0:n]))
        k.drain("vector")
        k.op("vector", lambda e: e.tensor_scalar(
            sq_scr[:, 32:32 + n], sq_scr[:, 32:32 + n], 32768.0, None,
            ALU.add))
        k.drain("vector")
        k.op("vector", lambda e: e.tensor_copy(f_i32s[:, 0:n],
                                               sq_scr[:, 32:32 + n]))
        k.drain("vector")
        k.op("vector", lambda e: e.tensor_copy(sq_scr[:, 64:64 + n],
                                               f_i32s[:, 0:n]))
        k.drain("vector")
        k.op("vector", lambda e: e.tensor_sub(sq_scr[:, 32:32 + n],
                                              sq_scr[:, 32:32 + n],
                                              sq_scr[:, 64:64 + n]))
        k.drain("vector")
        k.op("vector", lambda e: e.scalar_tensor_tensor(
            sq_scr[:, 64:64 + n], sq_scr[:, 32:32 + n], 0.5,
            sq_scr[:, 64:64 + n], ALU.is_ge, ALU.add))
        k.drain("vector")
        k.op("vector", lambda e: e.tensor_scalar(
            sq_scr[:, 64:64 + n], sq_scr[:, 64:64 + n], 32768.0, None,
            ALU.subtract))
        k.drain("vector")
        k.op("vector", lambda e: e.tensor_mul(bq_dst[:, 0:n],
                                              sq_scr[:, 64:64 + n],
                                              ssw_dst[:, 0:n]))
        k.drain("vector")
        return k.sig("vector")

    def build_ffn(xr_in, xw_out, tag=""):
        p = ExitStack()

        def psbuf(name, shape, dt=F32):
            return p.enter_context(nc.sbuf_tensor(name + tag, shape, dt))

        xld2 = psbuf("f_xld", [128, 2 * 1024])
        t_xu = [None, None]
        t_mx = None
        for tt in range(32):
            b = tt % 2
            xs_ = xld2[:, b * 1024:(b + 1) * 1024]
            k.await_("sync", t_xu[b])
            t_x = k.dma("sync", xs_, xr_in[tt * 128:tt * 128 + 128, :], f"xa{b}")
            t_st_ = ln_stats(xs_, tt, True, t_x)
            k.op("vector", lambda e, tt=tt: e.tensor_scalar(
                st_mx[:, tt:tt + 1], st_mx[:, tt:tt + 1],
                st_rs[:, tt:tt + 1], st_mu[:, tt:tt + 1],
                ALU.mult, ALU.add))
            k.drain("vector")
            t_xu[b] = t_st_
        k.op("vector", lambda e: e.reduce_max(st_mx[:, 32:33],
                                              st_mx[:, 0:32], axis=AX))
        k.drain("vector")
        t_mx = k.sig("vector")
        t_s1 = global_max_scale(st_mx[:, 32:33], 0, t_mx)
        k.await_("vector", t_s1)
        t_b1q = bias_quant(b1b, sw1b, f_ssw1, f_b1q, 32, 0)

        w1qs = psbuf("f_w1qs", [128, 8 * FF], BF16)
        xqT = psbuf("f_xqT", [128, 8 * 512], BF16)
        xqf = psbuf("f_xqf", [128, 2 * 1024])
        xqi = psbuf("f_xqi", [128, 1024], I32)
        xr0 = psbuf("f_xr0", [128, 1024])
        xqb = psbuf("f_xqb", [128, 2 * 1024])
        hsb = psbuf("f_hsb", [128, 2 * 512])
        t_w1l = [k.dma("sync", w1qs[:, dt * FF:(dt + 1) * FF],
                       w1q_h[dt * 128:(dt + 1) * 128, :], "wq1")
                 for dt in range(8)]
        k.op("vector", lambda e: e.memset(hmax[:, :], 0.0))
        k.drain("vector")
        t_hm = k.sig("vector")
        t_xqT_pe = None
        t_xqb_pe = [None, None]
        t_xqi_free = None
        t_hsb_st = [None, None]
        t_hsb_mx = [None, None]
        t_trq = [None, None]
        t_bank_ev2 = {0: None, 1: None}
        k.await_("scalar", t_mx)
        for tb in range(8):
            for tt4 in range(4):
                tt = tb * 4 + tt4
                b = tt % 2
                xs_ = xld2[:, b * 1024:(b + 1) * 1024]
                xq_ = xqf[:, b * 1024:(b + 1) * 1024]
                xb_ = xqb[:, b * 1024:(b + 1) * 1024]
                k.await_("sync", t_xu[b])
                t_x = k.dma("sync", xs_, xr_in[tt * 128:tt * 128 + 128, :], f"xa{b}")
                t_ap = ln_apply(xq_, xs_, tt, t_x, sig=True)
                t_xu[b] = t_ap
                k.await_("vector", t_ap, t_s1)
                k.op("vector", lambda e, xq_=xq_: e.tensor_scalar(
                    xq_, xq_, smv[:, 1:2], None, ALU.mult))
                k.drain("vector")
                k.await_("vector", t_xqi_free)
                k.op("vector", lambda e, xq_=xq_: e.tensor_copy(
                    xqi[:, :], xq_))
                k.drain("vector")
                k.op("vector", lambda e: e.tensor_copy(xr0[:, :], xqi[:, :]))
                k.drain("vector")
                t_xqi_free = k.sig("vector")
                k.op("vector", lambda e, xq_=xq_: e.tensor_sub(
                    xq_, xq_, xr0[:, :]))
                k.drain("vector")
                k.op("vector", lambda e, xq_=xq_: e.scalar_tensor_tensor(
                    xr0[:, :], xq_, 0.5, xr0[:, :], ALU.is_ge, ALU.add))
                k.drain("vector")
                k.await_("vector", t_xqb_pe[b])
                k.op("vector", lambda e, xb_=xb_: e.tensor_scalar(
                    xb_, xr0[:, :], 0.0, None, ALU.max))
                k.drain("vector")
                t_q = k.sig("vector")
                for dt in range(8):
                    j = tt4 * 8 + dt
                    bank = 6 + j % 2
                    k.await_("tensor", t_q, t_trq[j % 2])
                    t_tr = k.op("tensor", lambda e, bank=bank, xb_=xb_,
                                dt=dt: e.transpose(
                                    psb(bank, 128),
                                    xb_[:, dt * 128:(dt + 1) * 128],
                                    id_f[:, :]), sig=True)
                    t_xqb_pe[b] = t_tr
                    k.await_("scalar", t_tr)
                    t_trq[j % 2] = k.op(
                        "scalar", lambda e, bank=bank, dt=dt, tt4=tt4:
                        e.copy(xqT[:, dt * 512 + tt4 * 128:
                                   dt * 512 + tt4 * 128 + 128],
                               psb(bank, 128)), sig=True)
            t_xq_done = t_trq[31 % 2]
            k.await_("tensor", t_xq_done, *t_w1l, t_b1q)
            for ft in range(32):
                bank = ft % 2
                k.await_("tensor", t_bank_ev2[bank])
                tokm = None
                for dt in range(8):
                    tokm = k.op("tensor", lambda e, bank=bank, ft=ft, dt=dt:
                                e.matmul(psb(bank),
                                         w1qs[:, dt * FF + ft * 128:
                                              dt * FF + ft * 128 + 128],
                                         xqT[:, dt * 512:(dt + 1) * 512],
                                         start=(dt == 0), stop=(dt == 7)),
                                sig=(dt == 7))
                k.await_("scalar", tokm, t_hsb_st[ft % 2], t_hsb_mx[ft % 2])
                t_h = k.op("scalar", lambda e, bank=bank, ft=ft:
                           e.activation(hsb[:, (ft % 2) * 512:
                                            (ft % 2) * 512 + 512],
                                        psb(bank), AF.Relu,
                                        bias=f_b1q[:, ft:ft + 1],
                                        scale=f_ssw1[:, ft:ft + 1]), sig=True)
                t_bank_ev2[bank] = t_h
                k.await_("vector", t_h, t_hm)
                k.op("vector", lambda e, ft=ft: e.reduce_max(
                    s_tmp[:, 3:4], hsb[:, (ft % 2) * 512:(ft % 2) * 512 + 512],
                    axis=AX))
                k.drain("vector")
                k.op("vector", lambda e: e.tensor_max(
                    hmax[:, :], hmax[:, :], s_tmp[:, 3:4]))
                k.drain("vector")
                t_hsb_mx[ft % 2] = k.sig("vector")
                k.await_("gpsimd", t_h)
                for s_ in range(2):
                    t_hsb_st[ft % 2] = k.dma(
                        "gpsimd",
                        hsp[tb * 2 + s_, :, ft * 256:(ft + 1) * 256],
                        hsb[:, (ft % 2) * 512 + s_ * 256:
                            (ft % 2) * 512 + s_ * 256 + 256], f"hs{ft % 2}")
            t_xqT_pe = k.sig("tensor")
        t_mm1 = k.sig("vector")
        barrier()
        p.close()

        t_s2 = global_max_scale(hmax[:, :], 2, None)
        k.await_("vector", t_s2)
        t_b2q = bias_quant(b2b, sw2b, f_ssw2, f_b2q, 8, 2)
        p2 = ExitStack()

        def psbuf2(name, shape, dt=F32):
            return p2.enter_context(nc.sbuf_tensor(name + tag, shape, dt))

        s2rep = psbuf2("f_s2rep", [128, 1024])
        b2rep = psbuf2("f_b2rep", [128, 1024])
        t_row1 = col2row(swrow, lambda j: f_ssw2[:, j:j + 1], 8, t_b2q)
        t_r1 = bcastN(lambda o, w: s2rep[:, o:o + w], swrow, 1024, t_row1)
        t_row2 = col2row(swrow, lambda j: f_b2q[:, j:j + 1], 8, t_r1)
        t_r2 = bcastN(lambda o, w: b2rep[:, o:o + w], swrow, 1024, t_row2)

        w2qs = psbuf2("f_w2qs", [128, 32 * 1024], BF16)
        hld = psbuf2("f_hld", [128, 2 * 2048])
        hqi = psbuf2("f_hqi", [128, 2048], I32)
        hr0 = psbuf2("f_hr0", [128, 2048])
        hqb = psbuf2("f_hqb", [128, 2 * 8192], BF16)
        x2o = psbuf2("f_x2o", [128, 2 * 1024])
        xr2 = psbuf2("f_xr2", [128, 2 * 1024])
        t_w2l = [k.dma("sync", w2qs[:, ft * 1024:(ft + 1) * 1024],
                       w2q_h[ft * 128:(ft + 1) * 128, :], "wq2")
                 for ft in range(32)]
        t_hld_v = [None, None]
        t_hcast = None
        t_hqb_pe = [None, None]
        t_x2st = [None, None]
        t_xr2_v = [None, None]
        t_bank_v = {i: None for i in range(4)}
        for tb2 in range(16):
            b2 = tb2 % 2
            for hh in range(4):
                qb_ = hh % 2
                k.await_("gpsimd", t_hld_v[qb_], t_mm1)
                t_hl = k.dma("gpsimd", hld[:, qb_ * 2048:(qb_ + 1) * 2048],
                             hsp[tb2, :, hh * 2048:(hh + 1) * 2048],
                             f"hl{qb_}")
                sl = slice(qb_ * 2048, (qb_ + 1) * 2048)
                k.await_("vector", t_hl, t_s2, t_hqb_pe[b2])
                k.op("vector", lambda e, sl=sl: e.tensor_scalar(
                    hld[:, sl], hld[:, sl], smv[:, 3:4], None, ALU.mult))
                k.drain("vector")
                t_hm1 = k.sig("vector")
                k.await_("scalar", t_hm1, t_hcast)
                k.op("scalar", lambda e, sl=sl: e.copy(hqi[:, :], hld[:, sl]))
                k.drain("scalar")
                t_hc = k.op("scalar", lambda e: e.copy(hr0[:, :], hqi[:, :]),
                            sig=True)
                k.await_("vector", t_hc)
                k.op("vector", lambda e, sl=sl: e.tensor_sub(
                    hld[:, sl], hld[:, sl], hr0[:, :]))
                k.drain("vector")
                k.op("vector", lambda e, sl=sl, hh=hh, b2=b2:
                     e.scalar_tensor_tensor(
                         hqb[:, b2 * 8192 + hh * 2048:
                             b2 * 8192 + (hh + 1) * 2048], hld[:, sl], 0.5,
                         hr0[:, :], ALU.is_ge, ALU.add))
                k.drain("vector")
                t_hld_v[qb_] = k.sig("vector")
                t_hcast = t_hld_v[qb_]
            t_hq = t_hld_v[1]
            for tsub in range(2):
                tt = tb2 * 2 + tsub
                bq = tsub % 2
                k.await_("sync", t_xr2_v[bq])
                t_xr = k.dma("sync", xr2[:, bq * 1024:(bq + 1) * 1024],
                             xr_in[tt * 128:tt * 128 + 128, :], f"xr{bq}")
                toks = [None, None]
                for dmh in range(2):
                    bank = tsub * 2 + dmh
                    k.await_("tensor", t_hq, t_bank_v[bank], *t_w2l)
                    tokm = None
                    for ft in range(32):
                        tokm = k.op(
                            "tensor", lambda e, bank=bank, ft=ft, tsub=tsub,
                            dmh=dmh, b2=b2: e.matmul(
                                psb(bank),
                                hqb[:, b2 * 8192 + ft * 256 + tsub * 128:
                                    b2 * 8192 + ft * 256 + tsub * 128 + 128],
                                w2qs[:, ft * 1024 + dmh * 512:
                                     ft * 1024 + dmh * 512 + 512],
                                start=(ft == 0), stop=(ft == 31)),
                            sig=(ft == 31))
                    toks[dmh] = tokm
                t_hqb_pe[b2] = k.sig("tensor")
                for dmh in range(2):
                    bank = tsub * 2 + dmh
                    k.await_("vector", toks[dmh], t_xr, t_x2st[bq],
                             t_r1, t_r2)
                    dsl = slice(bq * 1024 + dmh * 512,
                                bq * 1024 + dmh * 512 + 512)
                    wsl = slice(dmh * 512, dmh * 512 + 512)
                    k.op("vector", lambda e, bank=bank, dsl=dsl, wsl=wsl:
                         e.scalar_tensor_tensor(
                             x2o[:, dsl], psb(bank), 1.0, s2rep[:, wsl],
                             ALU.bypass, ALU.mult))
                    k.drain("vector")
                    k.op("vector", lambda e, dsl=dsl, wsl=wsl: e.tensor_add(
                        x2o[:, dsl], x2o[:, dsl], b2rep[:, wsl]))
                    k.drain("vector")
                    k.op("vector", lambda e, dsl=dsl: e.tensor_add(
                        x2o[:, dsl], x2o[:, dsl], xr2[:, dsl]))
                    k.drain("vector")
                    t_bank_v[bank] = k.sig("vector")
                t_xr2_v[bq] = t_bank_v[tsub * 2 + 1]
                k.await_("gpsimd", t_bank_v[tsub * 2], t_bank_v[tsub * 2 + 1])
                t_x2st[bq] = k.dma(
                    "gpsimd", xw_out[tt * 128:tt * 128 + 128, :],
                    x2o[:, bq * 1024:(bq + 1) * 1024], f"x2s{bq}")
        barrier()
        p2.close()

    if nphases >= 2:
        build_ffn(xs1, xs2)

    def build_p3():
        SBW = 512
        p = ExitStack()

        def psbuf(name, shape, dt=F32):
            return p.enter_context(nc.sbuf_tensor(name, shape, dt))

        S_aug = psbuf("b_Saug", [128, 8 * 1025])
        xld3 = psbuf("b_xld", [128, 2 * 1024])
        xlb3 = psbuf("b_xlb", [128, 2 * 1024])
        xlnT3 = psbuf("b_xlnT", [128, 8 * SBW])
        qf = psbuf("b_qf", [128, 8 * SBW])
        kf = psbuf("b_kf", [128, 8 * SBW])
        qsc = psbuf("b_qsc", [128, 8 * 256])
        vau = psbuf("b_vau", [128, 4 * 1025])
        wvres = psbuf("b_wv", [128, 8 * 1024])
        wqk = psbuf("b_wqk", [128, 2 * 1024])
        ktld = psbuf("b_ktld", [128, 2 * 1024])
        A_sb = psbuf("b_A", [128, 2 * 256])
        ybuf = psbuf("b_y", [128, 2 * 1024])
        xr3 = psbuf("b_xr3", [128, 2 * 1024])

        for i in range(8):
            k.op("vector", lambda e, i=i: e.memset(
                S_aug[:, i * 1025:(i + 1) * 1025], 0.0))
        k.drain("vector")
        t_S0 = k.sig("vector")
        t_wv = [k.dma("sync", wvres[:, dt * 1024:(dt + 1) * 1024],
                      bWvT[dt * 128:(dt + 1) * 128, :], "wv")
               for dt in range(8)]

        t_b = {i: None for i in range(8)}
        t_tr3 = [None, None]
        t_xu3 = [None, None]
        t_xlb_pe3 = [None, None]
        t_wqk_pe = None
        t_qkv_w = [None, None]
        t_sb_pe = None
        t_vau_w = None
        t_ktld_w = None
        t_A_w = None
        t_y_done = [None, None]
        t_x3st = [None, None]
        t_Sev = [None] * 8
        t_slot3 = [None] * 4
        t_state_pe = t_S0
        for sb in range(8):
            t0 = sb * SBW
            for tt in range(4):
                gt = sb * 4 + tt
                b = tt % 2
                xs_ = xld3[:, b * 1024:(b + 1) * 1024]
                xb_ = xlb3[:, b * 1024:(b + 1) * 1024]
                k.await_("sync", t_xu3[b])
                t_x = k.dma("sync", xs_, xs2[t0 + tt * 128:t0 + tt * 128 + 128, :],
                            f"xa{b}")
                t_st_ = ln_stats(xs_, tt, False, t_x, t_slot3[tt])
                k.await_("scalar", t_xlb_pe3[b])
                t_ap = ln_apply(xb_, xs_, tt, t_st_, sig=True)
                t_slot3[tt] = t_ap
                t_xu3[b] = t_ap
                for dt in range(8):
                    j = tt * 8 + dt
                    bank = 6 + (j % 2)
                    k.await_("tensor", t_ap, t_tr3[j % 2], t_sb_pe, t_b[6])
                    t_tr = k.op("tensor", lambda e, bank=bank, dt=dt, xb_=xb_:
                                e.transpose(psb(bank, 128),
                                            xb_[:, dt * 128:(dt + 1) * 128],
                                            id_f[:, :]), sig=True)
                    t_xlb_pe3[b] = t_tr
                    k.await_("scalar", t_tr)
                    t_tr3[j % 2] = k.op(
                        "scalar", lambda e, bank=bank, dt=dt, tt=tt:
                        e.copy(xlnT3[:, dt * SBW + tt * 128:
                                     dt * SBW + tt * 128 + 128],
                               psb(bank, 128)), sig=True)
            t_ln_done = t_tr3[31 % 2]
            for i in range(8):
                k.await_("sync", t_wqk_pe)
                t_wl = [k.dma("sync",
                              wqk[:, j * 1024:(j + 1) * 1024]
                              .rearrange("p (dt e) -> p dt e", e=128),
                              W[:, i * 128:(i + 1) * 128]
                              .rearrange("(dt p) e -> p dt e", p=128), "wp")
                        for j, W in enumerate((bWqT, bWkT))]
                for (dst, wj, bias) in ((qf, 0, bqb), (kf, 1, bkb)):
                    bank = wj
                    k.await_("tensor", *t_wl, t_ln_done, t_b[bank])
                    tok = None
                    for dt in range(8):
                        tok = k.op("tensor", lambda e, bank=bank, wj=wj, dt=dt:
                                   e.matmul(psb(bank),
                                            wqk[:, wj * 1024 + dt * 128:
                                                wj * 1024 + dt * 128 + 128],
                                            xlnT3[:, dt * SBW:dt * SBW + SBW],
                                            start=(dt == 0), stop=(dt == 7)),
                                   sig=(dt == 7))
                    k.await_("scalar", tok, t_sb_pe)
                    t_b[bank] = k.op(
                        "scalar", lambda e, bank=bank, dst=dst, i=i,
                        bias=bias: e.activation(
                            dst[:, i * SBW:(i + 1) * SBW], psb(bank), AF.Relu,
                            bias=bias[:, i:i + 1]), sig=True)
                t_wqk_pe = k.sig("tensor")
            t_qk_done = t_b[1]
            k.op("vector", lambda e: e.memset(
                vau[:, :].rearrange("p (kt c) -> p kt c", c=1025)
                [:, :, 1024:1025], 1.0))
            k.drain("vector")
            t_ones3 = k.sig("vector")
            for kt in range(4):
                k.await_("tensor", t_b[2], t_b[3], t_sb_pe, *t_wv, t_ln_done)
                tokv = None
                for dt in range(8):
                    for nb in range(2):
                        tokv = k.op(
                            "tensor", lambda e, kt=kt, dt=dt, nb=nb:
                            e.matmul(psb(2 + nb),
                                     xlnT3[:, dt * SBW + kt * 128:
                                           dt * SBW + kt * 128 + 128],
                                     wvres[:, dt * 1024 + nb * 512:
                                           dt * 1024 + nb * 512 + 512],
                                     start=(dt == 0), stop=(dt == 7)),
                            sig=(dt == 7 and nb == 1))
                k.await_("vector", tokv, t_ones3, t_vau_w, t_state_pe)
                for nb in range(2):
                    k.op("vector", lambda e, kt=kt, nb=nb: e.tensor_add(
                        vau[:, kt * 1025 + nb * 512:kt * 1025 + nb * 512 + 512],
                        psb(2 + nb), bvrep[:, nb * 512:nb * 512 + 512]))
                k.drain("vector")
                t_vau_w = k.sig("vector")
                t_b[2] = t_vau_w
                t_b[3] = t_vau_w
            for cc in range(2):
                qoff = cc * 256
                k.await_("vector", t_qk_done, t_state_pe)
                for i in range(8):
                    k.op("vector", lambda e, i=i, qoff=qoff: e.tensor_mul(
                        qsc[:, i * 256:(i + 1) * 256],
                        qf[:, i * SBW + qoff:i * SBW + qoff + 256],
                        gvrep[:, :]))
                k.drain("vector")
                t_qsc = k.sig("vector")
                for i in range(8):
                    for kt2 in range(2):
                        j = i * 2 + kt2
                        bank = 6 + j % 2
                        k.await_("tensor", t_tr3[j % 2], t_qk_done, t_b[6])
                        t_tr = k.op(
                            "tensor", lambda e, bank=bank, i=i, kt2=kt2,
                            qoff=qoff: e.transpose(
                                psb(bank, 128),
                                kf[:, i * SBW + qoff + kt2 * 128:
                                   i * SBW + qoff + kt2 * 128 + 128],
                                id_f[:, :]), sig=True)
                        k.await_("scalar", t_tr, t_state_pe)
                        t_tr3[j % 2] = k.op(
                            "scalar", lambda e, bank=bank, i=i, kt2=kt2:
                            e.activation(
                                ktld[:, kt2 * 1024 + i * 128:
                                     kt2 * 1024 + i * 128 + 128],
                                psb(bank, 128), AF.Identity,
                                scale=gkv[:, kt2:kt2 + 1]), sig=True)
                t_kt_done = t_tr3[1]
                for jt in range(2):
                    bank = jt
                    k.await_("tensor", t_b[bank], t_qk_done)
                    tok = None
                    for i in range(8):
                        tok = k.op(
                            "tensor", lambda e, bank=bank, jt=jt, i=i,
                            qoff=qoff: e.matmul(
                                psb(bank, 256),
                                kf[:, i * SBW + qoff + jt * 128:
                                   i * SBW + qoff + jt * 128 + 128],
                                qf[:, i * SBW + qoff:i * SBW + qoff + 256],
                                start=(i == 0), stop=(i == 7)),
                            sig=(i == 7))
                    k.await_("vector", tok, t_A_w, t_state_pe)
                    k.op("vector", lambda e, bank=bank, jt=jt: e.tensor_mul(
                        A_sb[:, jt * 256:(jt + 1) * 256], psb(bank, 256),
                        Mt[:, jt * 256:(jt + 1) * 256]))
                    k.drain("vector")
                    t_A_w = k.sig("vector")
                    t_b[bank] = t_A_w
                for tsub in range(2):
                    gt = sb * 4 + cc * 2 + tsub
                    k.await_("tensor", t_qsc, t_A_w, t_kt_done,
                             t_b[4], t_b[5], t_b[6], *t_Sev)
                    grp = ((4, 512, 0), (5, 512, 512), (6, 1, 1024))
                    tok = None
                    for gi, (bank, w, so) in enumerate(grp):
                        for i in range(8):
                            tok = k.op(
                                "tensor", lambda e, bank=bank, w=w, so=so,
                                i=i, tsub=tsub: e.matmul(
                                    psb(bank, w),
                                    qsc[:, i * 256 + tsub * 128:
                                        i * 256 + tsub * 128 + 128],
                                    S_aug[:, i * 1025 + so:i * 1025 + so + w],
                                    start=(i == 0), stop=False,
                                    skip_group_check=True),
                                sig=False)
                        for jt in range(tsub + 1):
                            tok = k.op(
                                "tensor", lambda e, bank=bank, w=w, so=so,
                                jt=jt, tsub=tsub, cc=cc: e.matmul(
                                    psb(bank, w),
                                    A_sb[:, jt * 256 + tsub * 128:
                                         jt * 256 + tsub * 128 + 128],
                                    vau[:, (cc * 2 + jt) * 1025 + so:
                                        (cc * 2 + jt) * 1025 + so + w],
                                    start=False, stop=(jt == tsub),
                                    skip_group_check=True),
                                sig=(jt == tsub and gi == 2))
                    k.await_("vector", tok, t_y_done[tsub], t_x3st[tsub])
                    k.op("vector", lambda e: e.tensor_scalar(
                        s_tmp[:, 0:1], psb(6, 1), EPS_B, None, ALU.add))
                    k.drain("vector")
                    k.op("vector", lambda e: e.reciprocal(
                        s_tmp[:, 0:1], s_tmp[:, 0:1]))
                    k.drain("vector")
                    for nb in range(2):
                        k.op("vector", lambda e, nb=nb, tsub=tsub:
                             e.tensor_scalar(
                                 ybuf[:, tsub * 1024 + nb * 512:
                                      tsub * 1024 + nb * 512 + 512],
                                 psb(4 + nb), s_tmp[:, 0:1], None, ALU.mult))
                    k.drain("vector")
                    t_ynum = k.sig("vector")
                    t_b[4] = t_ynum
                    t_b[5] = t_ynum
                    t_b[6] = t_ynum
                    k.await_("sync", t_y_done[tsub])
                    t_xr = k.dma("sync", xr3[:, tsub * 1024:(tsub + 1) * 1024],
                                 xs2[gt * 128:gt * 128 + 128, :], f"xr{tsub}")
                    k.await_("vector", t_xr, t_x3st[tsub])
                    k.op("vector", lambda e, tsub=tsub: e.tensor_add(
                        ybuf[:, tsub * 1024:(tsub + 1) * 1024],
                        ybuf[:, tsub * 1024:(tsub + 1) * 1024],
                        xr3[:, tsub * 1024:(tsub + 1) * 1024]))
                    k.drain("vector")
                    t_y_done[tsub] = k.sig("vector")
                    k.await_("gpsimd", t_y_done[tsub])
                    t_x3st[tsub] = k.dma(
                        "gpsimd", xs3[gt * 128:gt * 128 + 128, :],
                        ybuf[:, tsub * 1024:(tsub + 1) * 1024], f"x3s{tsub}")
                for i in range(8):
                    k.await_("tensor", t_b[0], t_b[1], t_b[2], t_kt_done,
                             t_Sev[i])
                    grp = ((0, 512, 0), (1, 512, 512), (2, 1, 1024))
                    tok = None
                    for gi, (bank, w, so) in enumerate(grp):
                        tok = k.op(
                            "tensor", lambda e, bank=bank, w=w, so=so, i=i:
                            e.matmul(psb(bank, w), dgl[:, :],
                                     S_aug[:, i * 1025 + so:i * 1025 + so + w],
                                     start=True, stop=False,
                                     skip_group_check=True), sig=False)
                        for kt2 in range(2):
                            tok = k.op(
                                "tensor", lambda e, bank=bank, w=w, so=so,
                                i=i, kt2=kt2, cc=cc: e.matmul(
                                    psb(bank, w),
                                    ktld[:, kt2 * 1024 + i * 128:
                                         kt2 * 1024 + i * 128 + 128],
                                    vau[:, (cc * 2 + kt2) * 1025 + so:
                                        (cc * 2 + kt2) * 1025 + so + w],
                                    start=False, stop=(kt2 == 1),
                                    skip_group_check=True),
                                sig=(kt2 == 1 and gi == 2))
                    k.await_("scalar", tok)
                    k.op("scalar", lambda e, i=i: e.copy(
                        S_aug[:, i * 1025:i * 1025 + 512], psb(0)))
                    k.op("scalar", lambda e, i=i: e.copy(
                        S_aug[:, i * 1025 + 512:i * 1025 + 1024], psb(1)))
                    t_Sev[i] = k.op("scalar", lambda e, i=i: e.copy(
                        S_aug[:, i * 1025 + 1024:i * 1025 + 1025], psb(2, 1)),
                        sig=True)
                    t_b[0] = t_Sev[i]
                    t_b[1] = t_Sev[i]
                    t_b[2] = t_Sev[i]
                t_state_pe = k.sig("tensor")
            t_sb_pe = k.sig("tensor")
        barrier()
        p.close()

    if nphases >= 3:
        build_p3()
    if nphases >= 4:
        build_ffn(xs3, y_out, tag="2")

    blk = es.enter_context(nc.Block())

    def runner(eng_name):
        def fn(e):
            for thunk in k.ops[eng_name]:
                thunk(e)
        return fn

    blk.sync(runner("sync"))
    blk.vector(runner("vector"))
    blk.scalar(runner("scalar"))
    blk.tensor(runner("tensor"))
    blk.gpsimd(runner("gpsimd"))
    es.close()
    return nc




def host_inputs(inputs, b):
    import ml_dtypes
    x = np.ascontiguousarray(np.asarray(inputs["L0_attn_a_hidden_states"])[b])
    T = lambda a: np.ascontiguousarray(np.asarray(a).T)
    C = np.ascontiguousarray
    d = {
        "x": x,
        "aWqT": T(inputs["L0_attn_a_Wq"]),
        "aWkT": T(inputs["L0_attn_a_Wk"]),
        "aWvT": T(inputs["L0_attn_a_Wv"]),
        "aWoT": T(inputs["L0_attn_a_Wo"]),
        "bWqT": T(inputs["L0_attn_b_Wq"]),
        "bWkT": T(inputs["L0_attn_b_Wk"]),
        "bWvT": T(inputs["L0_attn_b_Wv"]),
        "bq": C(np.asarray(inputs["L0_attn_b_bq"]).reshape(1, D)),
        "bk": C(np.asarray(inputs["L0_attn_b_bk"]).reshape(1, D)),
        "bv": C(np.asarray(inputs["L0_attn_b_bv"]).reshape(1, D)),
        "gam": C(np.asarray(inputs["L0_attn_b_gamma"], dtype=np.float32)
                 .reshape(1, 1)),
        "w1": C(np.asarray(inputs["L0_ffn_w1"])),
        "w1T": T(inputs["L0_ffn_w1"]),
        "w2": C(np.asarray(inputs["L0_ffn_w2"])),
        "w2T": T(inputs["L0_ffn_w2"]),
        "b1": C(np.asarray(inputs["L0_ffn_b1"]).reshape(1, FF)),
        "b2": C(np.asarray(inputs["L0_ffn_b2"]).reshape(1, D)),
        "c_id": np.eye(128, dtype=np.float32),
        "c_idb": np.eye(128).astype(ml_dtypes.bfloat16),
        "c_tri": np.triu(np.ones((128, 128), dtype=np.float32)),
        "c_tmj": np.maximum(
            np.arange(256)[None, :] - np.arange(256)[:, None], 0
        ).astype(np.float32),
        "c_m01": (np.arange(256)[None, :] >= np.arange(256)[:, None])
        .astype(np.float32),
        "c_t1": np.broadcast_to(np.arange(1, LB + 1, dtype=np.float32),
                                (128, LB)).copy(),
        "c_lmj": (LB - 1.0 - (np.arange(2)[None, :] * 128
                              + np.arange(128)[:, None])).astype(np.float32),
    }
    return d


_NC = {}


def kernel(**inputs):
    if "nc" not in _NC:
        _NC["nc"] = build_nc(dbg=False)
    in_maps = [host_inputs(inputs, b) for b in range(NCORES)]
    res = run_bass_kernel_spmd(_NC["nc"], in_maps, core_ids=list(range(NCORES)))
    return np.stack([np.asarray(res.results[b]["y"]) for b in range(NCORES)],
                    axis=0)


# version 06
# speedup vs baseline: 1.0110x; 1.0110x over previous
import numpy as np
from contextlib import ExitStack

import concourse.bass as bass
import concourse.mybir as mybir
from concourse.bass_utils import run_bass_kernel_spmd

F32 = mybir.dt.float32
BF16 = mybir.dt.bfloat16
I32 = mybir.dt.int32
AF = mybir.ActivationFunctionType
ALU = mybir.AluOpType
AX = mybir.AxisListType.X

S, D, FF = 4096, 1024, 4096
P1SUB = "ATBC"
LNLVL = 9
P0LVL = 9
NH, HD = 16, 64
CH = 2048
LB = 256
SBK = 1024
EPS_LN = 1e-5
EPS_B = 1e-6
EPS_Q = 1e-8
NCORES = 8

ENGS = ("sync", "vector", "scalar", "tensor", "gpsimd")


class K:

    def __init__(self, nc, es):
        self.nc = nc
        self.es = es
        self.ops = {e: [] for e in ENGS}
        self.sem = {}
        self.cnt = {}
        self.chan_eng = {}
        for e in ENGS:
            self.sem["d_" + e] = es.enter_context(nc.semaphore("d_" + e))
        self.sem["cc"] = es.enter_context(nc.semaphore("cc"))

    def _bump(self, name, by):
        self.cnt[name] = self.cnt.get(name, 0) + by
        return (name, self.cnt[name])

    def op(self, eng, fn, sig=False):
        if not sig:
            self.ops[eng].append(fn)
            return None
        name = "d_" + eng
        tok = self._bump(name, 1)
        s = self.sem[name]
        self.ops[eng].append(lambda e, fn=fn, s=s: fn(e).then_inc(s, 1))
        return tok

    def sig(self, eng):
        nm = "d_" + eng
        v = self.cnt.get(nm, 0)
        if v:
            s = self.sem[nm]
            self.ops[eng].append(lambda e, s=s, v=v: e.wait_ge(s, v))
        return self.op(eng, lambda e: e.nop(nofuse=True), sig=True)

    def dma(self, eng, out, in_, ch, slow=False):
        name = "dc_" + ch
        if name not in self.sem:
            self.sem[name] = self.es.enter_context(
                self.nc.semaphore(name))
        self.chan_eng.setdefault(eng, set()).add(name)
        tok = self._bump(name, 16)
        s = self.sem[name]
        self.ops[eng].append(
            lambda e, o=out, i=in_, s=s, sl=slow: e.dma_start(
                out=o, in_=i, allow_slow_non_contiguous=sl)
            .then_inc(s, 16))
        return tok

    def await_(self, eng, *toks):
        best = {}
        for t in toks:
            if t is None:
                continue
            nm, v = t
            best[nm] = max(best.get(nm, 0), v)
        for nm, v in best.items():
            s = self.sem[nm]
            self.ops[eng].append(lambda e, s=s, v=v: e.wait_ge(s, v))

    def drain(self, eng):
        self.ops[eng].append(lambda e: e.drain())

    def cc_allreduce_max(self, ins_ap, outs_ap):
        tok = self._bump("cc", 1)
        s = self.sem["cc"]
        self.ops["gpsimd"].append(
            lambda e, i=ins_ap, o=outs_ap, s=s: e.collective_compute(
                "AllReduce", ALU.max, replica_groups=[list(range(NCORES))],
                ins=[i], outs=[o]).then_inc(s, 1))
        return tok


def build_nc(dbg=False, nphases=5):
    nc = bass.Bass("TRN2", target_bir_lowering=False)

    def din(name, shape, dt=F32):
        return nc.dram_tensor(name, shape, dt, kind="ExternalInput").ap()

    x_in = din("x", [S, D])
    aWqT, aWkT = din("aWqT", [D, D]), din("aWkT", [D, D])
    aWvT, aWoT = din("aWvT", [D, D]), din("aWoT", [D, D])
    bWqT, bWkT, bWvT = (din("bWqT", [D, D]), din("bWkT", [D, D]),
                        din("bWvT", [D, D]))
    bq_h, bk_h, bv_h = din("bq", [1, D]), din("bk", [1, D]), din("bv", [1, D])
    gam_h = din("gam", [1, 1])
    w1_h, w1T_h = din("w1", [FF, D]), din("w1T", [D, FF])
    w2_h, w2T_h = din("w2", [D, FF]), din("w2T", [FF, D])
    b1_h, b2_h = din("b1", [1, FF]), din("b2", [1, D])
    c_id = din("c_id", [128, 128])
    c_idb = din("c_idb", [128, 128], BF16)
    c_tri = din("c_tri", [128, 128])
    c_tmj = din("c_tmj", [256, 256])
    c_m01 = din("c_m01", [256, 256])
    c_t1 = din("c_t1", [128, LB])
    c_lmj = din("c_lmj", [128, 2])

    y_out = nc.dram_tensor("y", [S, D], F32, kind="ExternalOutput").ap()

    okind = "ExternalOutput" if dbg else "Internal"
    xs1 = nc.dram_tensor("xs1", [S, D], F32, kind=okind).ap()
    xs2 = nc.dram_tensor("xs2", [S, D], F32, kind=okind).ap()
    xs3 = nc.dram_tensor("xs3", [S, D], F32, kind=okind).ap()
    hsp = nc.dram_tensor("hsp", [16, 128, 8192], F32).ap()
    w1q_h = nc.dram_tensor("w1qs", [D, FF], BF16,
                           kind=okind).ap()
    w2q_h = nc.dram_tensor("w2qs", [FF, D], BF16, kind=okind).ap()
    sw1_h = nc.dram_tensor("sw1s", [1, FF], F32, kind=okind).ap()
    sw2_h = nc.dram_tensor("sw2s", [1, D], F32).ap()
    srow_h = nc.dram_tensor("srows", [8, 128], F32).ap()
    vrow_h = nc.dram_tensor("vrows", [4, FF], F32).ap()
    cc_i = nc.dram_tensor("cc_i", [1, 128], F32)
    cc_o = nc.dram_tensor("cc_o", [1, 128], F32)

    es = ExitStack()
    k = K(nc, es)

    def sbuf(name, shape, dt=F32):
        return es.enter_context(nc.sbuf_tensor(name, shape, dt))

    PS = es.enter_context(nc.psum_tensor("PS", [128, 4096], F32))

    def psb(bank, w=512, off=0, p0=0, p1=128):
        return PS[p0:p1, bank * 512 + off: bank * 512 + off + w]

    id_f = sbuf("id_f", [128, 128])
    id_b = sbuf("id_b", [128, 128], BF16)
    tri = sbuf("tri", [128, 128])
    bvrep = sbuf("bvrep", [128, D])
    bqb, bkb = sbuf("bqb", [128, 8]), sbuf("bkb", [128, 8])
    b1b, b2b = sbuf("b1b", [128, 32]), sbuf("b2b", [128, 8])
    sw1b, sw2b = sbuf("sw1b", [128, 32]), sbuf("sw2b", [128, 8])
    Mt = sbuf("Mt", [128, 2 * 256])
    gvrep = sbuf("gvrep", [128, LB])
    gkv = sbuf("gkv", [128, 2])
    gLv = sbuf("gLv", [128, 1])
    dgl = sbuf("dgl", [128, 128])
    lng = sbuf("lng", [128, 1])
    smv = sbuf("smv", [128, 16])
    st_mu = sbuf("st_mu", [128, 32])
    st_rs = sbuf("st_rs", [128, 32])
    st_mx = sbuf("st_mx", [128, 33])
    sq_scr = sbuf("sq_scr", [128, 1024])
    s_tmp = sbuf("s_tmp", [128, 4])

    ones_r = sbuf("ones_r", [1, 128])
    g11 = sbuf("g11", [1, 2])
    swrow = sbuf("swrow", [1, 1024])

    t_cb = {6: None, 7: None}

    def col2row(dst_row, src_sb_col_fn, m, t_pre=None):
        tev = None
        for j in range(m):
            bank = 6 + j % 2
            k.await_("tensor", t_pre, t_cb[bank])
            t_tr = k.op("tensor", lambda e, bank=bank, j=j: e.transpose(
                psb(bank, 128, p1=1), src_sb_col_fn(j), id_f[:, :]), sig=True)
            k.await_("scalar", t_tr, tev)
            tev = k.op("scalar", lambda e, bank=bank, j=j: e.copy(
                dst_row[0:1, j * 128:(j + 1) * 128], psb(bank, 128, p1=1)),
                sig=True)
            t_cb[bank] = tev
        return tev

    def bcastN(dst_sb_fn, src_row, n, t_pre=None, ev_eng="scalar"):
        tev = None
        for nb in range((n + 511) // 512):
            w = min(512, n - nb * 512)
            bank = 6 + nb % 2
            k.await_("tensor", t_pre, t_ones_r, t_cb[bank])
            t_mm = k.op("tensor", lambda e, bank=bank, nb=nb, w=w: e.matmul(
                psb(bank, w), ones_r[0:1, :],
                src_row[0:1, nb * 512:nb * 512 + w],
                start=True, stop=True), sig=True)
            k.await_(ev_eng, t_mm, tev)
            tev = k.op(ev_eng, lambda e, bank=bank, nb=nb, w=w: e.tensor_copy(
                dst_sb_fn(nb * 512, w), psb(bank, w))
                if ev_eng == "vector" else e.copy(
                    dst_sb_fn(nb * 512, w), psb(bank, w)), sig=True)
            t_cb[bank] = tev
        return tev

    k.op("vector", lambda e: e.memset(ones_r[:, :], 1.0))
    k.drain("vector")
    t_ones_r = k.sig("vector")

    t_ld = [
        k.dma("sync", id_f[:, :], c_id, "c0"),
        k.dma("sync", id_b[:, :], c_idb, "c0"),
        k.dma("sync", tri[:, :], c_tri, "c0"),

        k.dma("sync", bqb[:, :], bq_h.rearrange("o (c p) -> (o p) c", p=128), "c0", slow=True),
        k.dma("sync", bkb[:, :], bk_h.rearrange("o (c p) -> (o p) c", p=128), "c0", slow=True),
        k.dma("sync", b1b[:, :], b1_h.rearrange("o (c p) -> (o p) c", p=128), "c0", slow=True),
        k.dma("sync", b2b[:, :], b2_h.rearrange("o (c p) -> (o p) c", p=128), "c0", slow=True),
        k.dma("sync", g11[0:1, 0:1], gam_h, "c0"),
    ]
    t_bvr = k.dma("sync", swrow[0:1, 0:1024], bv_h, "mis0")
    t_bvr2 = bcastN(lambda o, w: bvrep[:, o:o + w], swrow, 1024, t_bvr,
                    "vector")

    k.await_("scalar", *t_ld)
    t_ln0 = k.op("scalar", lambda e: e.activation(g11[0:1, 0:1], g11[0:1, 0:1],
                                                  AF.Ln), sig=True)
    t_ln = bcastN(lambda o, w: lng[:, o:o + w], g11, 1, t_ln0)
    t_m = [k.dma("sync", sq_scr[:, 0:256], c_tmj[0:128, :], "tb0"),
           k.dma("sync", sq_scr[:, 256:512], c_tmj[128:256, :], "tb0")]
    k.await_("vector", t_ln, *t_m)
    k.op("vector", lambda e: e.tensor_scalar(sq_scr[:, 0:512],
                                             sq_scr[:, 0:512],
                                             lng[:, 0:1], None, ALU.mult))
    k.drain("vector")
    t_v = k.sig("vector")
    k.await_("scalar", t_v)
    t_exp = k.op("scalar", lambda e: e.activation(Mt[:, :], sq_scr[:, 0:512],
                                                  AF.Exp), sig=True)
    k.await_("sync", t_exp)
    t_m01 = [k.dma("sync", sq_scr[:, 0:256], c_m01[0:128, :], "tb1"),
             k.dma("sync", sq_scr[:, 256:512], c_m01[128:256, :], "tb1")]
    k.await_("vector", t_exp, *t_m01)
    k.op("vector", lambda e: e.tensor_mul(Mt[:, :], Mt[:, :],
                                          sq_scr[:, 0:512]))
    k.drain("vector")
    t_g = [k.dma("sync", gvrep[:, :], c_t1, "tb2"),
           k.dma("sync", gkv[:, :], c_lmj, "tb2")]
    k.await_("vector", *t_g)
    k.op("vector", lambda e: e.tensor_scalar(gvrep[:, :], gvrep[:, :],
                                             lng[:, 0:1], None, ALU.mult))
    k.op("vector", lambda e: e.tensor_scalar(gkv[:, :], gkv[:, :],
                                             lng[:, 0:1], None, ALU.mult))
    k.op("vector", lambda e: e.tensor_scalar(gLv[:, :], lng[:, 0:1],
                                             float(LB), None, ALU.mult))
    k.drain("vector")
    t_v2 = k.sig("vector")
    k.await_("scalar", t_v2)
    k.op("scalar", lambda e: e.activation(gvrep[:, :], gvrep[:, :], AF.Exp))
    k.op("scalar", lambda e: e.activation(gkv[:, :], gkv[:, :], AF.Exp))
    t_gl = k.op("scalar", lambda e: e.activation(gLv[:, :], gLv[:, :],
                                                 AF.Exp), sig=True)
    k.await_("vector", t_gl)
    k.op("vector", lambda e: e.tensor_scalar(dgl[:, :], id_f[:, :],
                                             gLv[:, 0:1], None, ALU.mult))
    k.drain("vector")
    t_tables = k.sig("vector")

    p0es = ExitStack()

    def sbuf0(name, shape, dt=F32):
        return p0es.enter_context(nc.sbuf_tensor(name, shape, dt))

    wnat = sbuf0("p0_wnat", [128, FF])
    wtb = [sbuf0(f"p0_wt{i}", [128, 512]) for i in range(2)]
    srep = sbuf0("p0_sr", [128, 512])
    wi32 = sbuf0("p0_wi", [128, 512], I32)
    wr0 = sbuf0("p0_wr0", [128, 512])
    wqb = [sbuf0(f"p0_wqb{i}", [128, 512], BF16) for i in range(2)]

    def wquant(w_nat, wT, sw_sb, sw_hbm, wq_hbm, nf, nd):
        nft = nf // 128
        for ft in range(nft):
            tl = k.dma("sync", wnat[:, 0:nd], w_nat[ft * 128:(ft + 1) * 128, :], "wn")
            k.await_("vector", tl)
            tr = k.op("vector", lambda e, ft=ft: e.reduce_max(
                sw_sb[:, ft:ft + 1], wnat[:, 0:nd], axis=AX,
                apply_absolute_value=True), sig=True)
            k.await_("sync", tr)
        k.drain("vector")
        k.op("vector", lambda e: e.tensor_scalar(
            sw_sb[:, 0:nft], sw_sb[:, 0:nft], EPS_Q, 1.0 / 127.0,
            ALU.max, ALU.mult))
        k.drain("vector")
        t_sw = k.sig("vector")
        t_sw_pe = t_sw
        k.await_("gpsimd", t_sw)
        t_st = k.dma("gpsimd", sw_hbm.rearrange("o (c p) -> (o p) c", p=128),
                     sw_sb[:, 0:nft], "wst", slow=True)
        ndt, nfb = nd // 128, nf // 512
        t_wq = wq_state["t_wq"]
        t_wtb_free = wq_state["t_wtb_free"]
        t_srep_free = wq_state["t_srep_free"]
        t_wcast = wq_state["t_srep_free"]
        for fb in range(nfb):
            t_row = col2row(swrow, lambda j, fb=fb: sw_sb[:, fb * 4 + j:
                                                          fb * 4 + j + 1],
                            4, t_sw_pe)
            k.await_("tensor", t_srep_free)
            t_bc = bcastN(lambda o, w: srep[:, o:o + w], swrow, 512, t_row,
                          "scalar")
            k.await_("vector", t_bc, t_srep_free)
            k.op("vector", lambda e: e.reciprocal(srep[:, :], srep[:, :]))
            k.drain("vector")
            for dt in range(ndt):
                b = dt % 2
                k.await_("sync", t_wtb_free[b])
                tw = k.dma("sync", wtb[b][:, :],
                           wT[dt * 128:(dt + 1) * 128,
                              fb * 512:(fb + 1) * 512], f"wt{b}")
                k.await_("vector", tw, t_wq[b])
                k.op("vector", lambda e, b=b: e.tensor_mul(
                    wtb[b][:, :], wtb[b][:, :], srep[:, :]))
                k.drain("vector")
                k.op("vector", lambda e, b=b: e.tensor_scalar(
                    wtb[b][:, :], wtb[b][:, :], 128.0, None, ALU.add))
                k.drain("vector")
                t_wm = k.sig("vector")
                k.await_("scalar", t_wm, t_wcast)
                k.op("scalar", lambda e, b=b: e.copy(wi32[:, :],
                                                     wtb[b][:, :]))
                k.drain("scalar")
                t_wc = k.op("scalar", lambda e: e.copy(wr0[:, :],
                                                       wi32[:, :]), sig=True)
                k.await_("vector", t_wc)
                k.op("vector", lambda e, b=b: e.tensor_sub(
                    wtb[b][:, :], wtb[b][:, :], wr0[:, :]))
                k.drain("vector")
                k.op("vector", lambda e, b=b: e.scalar_tensor_tensor(
                    wr0[:, :], wtb[b][:, :], 0.5, wr0[:, :],
                    ALU.is_ge, ALU.add))
                k.drain("vector")
                k.op("vector", lambda e: e.tensor_scalar(
                    wr0[:, :], wr0[:, :], 128.0, -128.0,
                    ALU.subtract, ALU.max))
                k.drain("vector")
                t_wtb_free[b] = k.sig("vector")
                tq = k.op("vector", lambda e, b=b: e.tensor_scalar(
                    wqb[b][:, :], wr0[:, :], 127.0, None, ALU.min), sig=True)
                t_wcast = tq
                k.await_("gpsimd", tq)
                t_wq[b] = k.dma("gpsimd",
                                wq_hbm[dt * 128:(dt + 1) * 128,
                                       fb * 512:(fb + 1) * 512], wqb[b][:, :],
                                f"ws{b}")
            t_srep_free = t_wtb_free[(ndt - 1) % 2]
            wq_state["t_srep_free"] = t_srep_free

    wq_state = {"t_wq": [None, None], "t_wtb_free": [None, None],
                "t_srep_free": None}
    if P0LVL >= 2 and P0LVL != 4:
        wquant(w1_h, w1T_h, sw1b, sw1_h, w1q_h, FF, D)
    if P0LVL >= 3:
        wquant(w2_h, w2T_h, sw2b, sw2_h, w2q_h, D, FF)

    _ln_prev = [None]
    t_x_global = [None]

    def ln_stats(xtile, idx, want_max, t_x, t_slot_free=None):
        t_x_global[0] = t_x
        if LNLVL < 1:
            return None
        """stats of xtile into st_mu/st_rs[:, idx]; scale/bias form:
        ln(x) = st_rs*x + st_mu  (st_mu = -mu*rstd)."""
        k.await_("scalar", t_x, _ln_prev[0], t_slot_free)
        t_sq = k.op("scalar", lambda e: e.activation(
            sq_scr[:, :], xtile, AF.Square), sig=True)
        if LNLVL < 2:
            return t_sq
        k.await_("vector", t_x, t_slot_free)
        k.op("vector", lambda e: e.reduce_sum(s_tmp[:, 0:1], xtile, axis=AX))
        if want_max:
            k.op("vector", lambda e, i=idx: e.reduce_max(
                st_mx[:, i:i + 1], xtile, axis=AX))
        k.await_("vector", t_sq)
        k.op("vector", lambda e: e.reduce_sum(s_tmp[:, 1:2], sq_scr[:, :],
                                              axis=AX))
        k.drain("vector")
        if LNLVL < 3:
            return k.sig("vector")
        k.op("vector", lambda e, i=idx: e.tensor_scalar(
            st_mu[:, i:i + 1], s_tmp[:, 0:1], 1.0 / D, None, ALU.mult))
        k.drain("vector")
        k.op("vector", lambda e: e.tensor_scalar(
            s_tmp[:, 1:2], s_tmp[:, 1:2], 1.0 / D, None, ALU.mult))
        k.drain("vector")
        k.op("vector", lambda e, i=idx: e.scalar_tensor_tensor(
            s_tmp[:, 2:3], st_mu[:, i:i + 1], 1.0, st_mu[:, i:i + 1],
            ALU.bypass, ALU.mult))
        k.drain("vector")
        k.op("vector", lambda e: e.tensor_sub(s_tmp[:, 2:3], s_tmp[:, 1:2],
                                              s_tmp[:, 2:3]))
        k.drain("vector")
        k.op("vector", lambda e: e.tensor_scalar(
            s_tmp[:, 2:3], s_tmp[:, 2:3], EPS_LN, None, ALU.add))
        k.drain("vector")
        t_var = k.sig("vector")
        if LNLVL < 4:
            return t_var
        k.await_("scalar", t_var)
        t_rs = k.op("scalar", lambda e, i=idx: e.activation(
            st_rs[:, i:i + 1], s_tmp[:, 2:3], AF.Sqrt), sig=True)
        k.await_("vector", t_rs)
        k.op("vector", lambda e, i=idx: e.reciprocal(
            st_rs[:, i:i + 1], st_rs[:, i:i + 1]))
        k.drain("vector")
        k.op("vector", lambda e, i=idx: e.scalar_tensor_tensor(
            st_mu[:, i:i + 1], st_mu[:, i:i + 1], -1.0, st_rs[:, i:i + 1],
            ALU.mult, ALU.mult))
        k.drain("vector")
        _ln_prev[0] = k.sig("vector")
        return _ln_prev[0]

    def ln_apply(dst, xtile, idx, t_stats, sig=False):
        if LNLVL < 0:
            return k.sig("scalar")
        k.await_("scalar", t_stats)
        if LNLVL < 5:
            k.await_("scalar", t_x_global[0])
            return k.op("scalar", lambda e: e.copy(dst, xtile), sig=sig)
        return k.op("scalar", lambda e, i=idx: e.activation(
            dst, xtile, AF.Identity, bias=st_mu[:, i:i + 1],
            scale=st_rs[:, i:i + 1]), sig=sig)

    def barrier():
        for e, chans in k.chan_eng.items():
            for nm in sorted(chans):
                v = k.cnt.get(nm, 0)
                if v:
                    s = k.sem[nm]
                    k.ops[e].append(lambda en, s=s, v=v: en.wait_ge(s, v))
        toks = [k.sig(e) for e in ENGS]
        for e in ENGS:
            k.await_(e, *toks)

    barrier()
    p0es.close()
    if nphases < 1:
        blk = es.enter_context(nc.Block())

        def runner0(eng_name):
            def fn(e):
                for thunk in k.ops[eng_name]:
                    thunk(e)
            return fn
        blk.sync(runner0("sync"))
        blk.vector(runner0("vector"))
        blk.scalar(runner0("scalar"))
        blk.tensor(runner0("tensor"))
        blk.gpsimd(runner0("gpsimd"))
        es.close()
        return nc

    def build_p1():
        p = ExitStack()

        def psbuf(name, shape, dt=F32):
            return p.enter_context(nc.sbuf_tensor(name, shape, dt))

        if "N" in P1SUB:
            p.close()
            return
        u64 = psbuf("a_u64", [128, 8 * CH])
        xlnT = u64[:, :]
        wo_sb = u64[:, 0:8 * 1024]
        yT = u64[:, 8 * 1024:8 * 1024 + 1024]
        x1o = u64[:, 9 * 1024 + 256:11 * 1024 + 256]
        ytm = psbuf("a_ytm", [128, 16 * 1024])
        Qp = psbuf("a_Qp", [128, CH])
        Kp = psbuf("a_Kp", [128, CH])
        vaug = psbuf("a_vaug", [128, 16 * 129 + 1])
        PT = psbuf("a_PT", [128, 4 * 512])
        wq3 = psbuf("a_wq3", [128, 3 * 1024])
        xld = psbuf("a_xld", [128, 2 * 1024])
        xlb = psbuf("a_xlb", [128, 2 * 1024])

        t_chunk_pe = None
        t_ytm_freed = None
        t_pair_pe = None
        t_bank_ev = {i: None for i in range(8)}
        t_avevac = [[None, None] for _ in range(4)]
        t_ptfree = [[None, None], [None, None]]
        t_x1st = {}
        t_xuse = [None, None]
        t_xlb_pe = [None, None]
        t_trd = [None, None]
        t_slot = [None] * 4
        for c in range(2):
            tc = c * CH
            k.await_("scalar", t_x1st.get(0), t_x1st.get(1))
            for tt in range(16 if "A" in P1SUB else 0):
                b = tt % 2
                xs_ = xld[:, b * 1024:(b + 1) * 1024]
                xb_ = xlb[:, b * 1024:(b + 1) * 1024]
                k.await_("sync", t_xuse[b])
                t_x = k.dma("sync", xs_,
                            x_in[tc + tt * 128:tc + tt * 128 + 128, :],
                            f"xa{b}")
                t_st_ = ln_stats(xs_, tt % 4, False, t_x, t_slot[tt % 4])
                k.await_("scalar", t_xlb_pe[b], t_chunk_pe)
                t_ap = ln_apply(xb_, xs_, tt % 4, t_st_, sig=True)
                t_slot[tt % 4] = t_ap
                t_xuse[b] = t_ap
                for dt in range(8 if "T" in P1SUB else 0):
                    j = tt * 8 + dt
                    bank = 6 + (j % 2)
                    k.await_("tensor", t_ap, t_trd[j % 2])
                    t_tr = k.op("tensor", lambda e, bank=bank, dt=dt, xb_=xb_:
                                e.transpose(psb(bank, 128),
                                            xb_[:, dt * 128:(dt + 1) * 128],
                                            id_f[:, :]), sig=True)
                    t_xlb_pe[b] = t_tr
                    k.await_("scalar", t_tr)
                    t_ev = k.op("scalar", lambda e, bank=bank, dt=dt, tt=tt:
                                e.copy(xlnT[:, dt * CH + tt * 128:
                                            dt * CH + tt * 128 + 128],
                                       psb(bank, 128)), sig=True)
                    t_trd[j % 2] = t_ev
            t_A_done = t_trd[(16 * 8 - 1) % 2] if "T" in P1SUB else None
            if "B" not in P1SUB:
                continue

            for hp in range(8):
                k.await_("sync", t_pair_pe, t_chunk_pe)
                t_w = [k.dma("sync",
                             wq3[:, j * 1024:(j + 1) * 1024]
                             .rearrange("p (dt e) -> p dt e", e=128),
                             W[:, hp * 128:(hp + 1) * 128]
                             .rearrange("(dt p) e -> p dt e", p=128), "wp")
                       for j, W in enumerate((aWqT, aWkT, aWvT))]
                t_qk = []
                for (dst, wj) in ((Qp, 0), (Kp, 1)):
                    for nb in range(4):
                        bank = nb % 2
                        k.await_("tensor", *t_w, t_bank_ev[bank], t_A_done,
                                 t_pair_pe)
                        tok = None
                        for dt in range(8):
                            tok = k.op("tensor", lambda e, bank=bank, wj=wj,
                                       dt=dt, nb=nb:
                                       e.matmul(psb(bank),
                                                wq3[:, wj * 1024 + dt * 128:
                                                    wj * 1024 + dt * 128 + 128],
                                                xlnT[:, dt * CH + nb * 512:
                                                     dt * CH + nb * 512 + 512],
                                                start=(dt == 0),
                                                stop=(dt == 7)),
                                       sig=(dt == 7))
                        k.await_("scalar", tok)
                        t_bank_ev[bank] = k.op(
                            "scalar", lambda e, bank=bank, dst=dst, nb=nb:
                            e.copy(dst[:, nb * 512:(nb + 1) * 512], psb(bank)),
                            sig=True)
                        t_qk.append(t_bank_ev[bank])
                k.await_("vector", t_pair_pe)
                k.op("vector", lambda e: e.memset(
                    vaug[:, 0:16 * 129]
                    .rearrange("p (kt c) -> p kt c", c=129)[:, :, 64:65], 1.0))
                k.drain("vector")
                t_ones = k.sig("vector")
                t_ve = None
                for kt in range(16):
                    k.await_("tensor", t_bank_ev[2], t_pair_pe)
                    tokv = None
                    for dt in range(8):
                        tokv = k.op("tensor", lambda e, kt=kt, dt=dt:
                                    e.matmul(psb(2, 128),
                                             xlnT[:, dt * CH + kt * 128:
                                                  dt * CH + kt * 128 + 128],
                                             wq3[:, 2 * 1024 + dt * 128:
                                                 2 * 1024 + dt * 128 + 128],
                                             start=(dt == 0), stop=(dt == 7)),
                                    sig=(dt == 7))
                    k.await_("scalar", tokv, t_ones)
                    t_ve = k.op(
                        "scalar", lambda e, kt=kt:
                        e.copy(vaug[:, kt * 129:kt * 129 + 130]
                               .rearrange("p (a c) -> p a c", c=65)
                               [:, :, 0:64],
                               psb(2, 128).rearrange("p (a c) -> p a c",
                                                     c=64)),
                        sig=True)
                    t_bank_ev[2] = t_ve
                k.await_("tensor", t_ve, t_ones, *t_qk)
                for qb in range(4):
                    ktmax = 4 * qb + 4
                    t_avstop = [[None, None] for _ in range(4)]
                    for kt in range(ktmax):
                        off = max(0, kt - 4 * qb) * 128
                        Nq = 512 - off
                        for g in range(2):
                            sb_ = g * 2 + kt % 2
                            pt0 = (g * 2 + kt % 2) * 512
                            k.await_("tensor", t_bank_ev[sb_])
                            t_s = k.op(
                                "tensor", lambda e, sb_=sb_, off=off,
                                Nq=Nq, g=g, kt=kt, qb=qb:
                                e.matmul(psb(sb_, Nq, off),
                                         Kp[g * 64:(g + 1) * 64,
                                            kt * 128:kt * 128 + 128],
                                         Qp[g * 64:(g + 1) * 64,
                                            qb * 512 + off:qb * 512 + 512],
                                         start=True, stop=True), sig=True)
                            k.await_("scalar", t_s, t_ptfree[g][kt % 2])
                            t_e = k.op("scalar", lambda e, pt0=pt0, off=off,
                                       Nq=Nq, sb_=sb_:
                                       e.activation(
                                           PT[:, pt0 + off:pt0 + 512],
                                           psb(sb_, Nq, off), AF.Exp,
                                           scale=0.125), sig=True)
                            t_bank_ev[sb_] = t_e
                            t_gate = t_e
                            if kt >= 4 * qb:
                                k.await_("gpsimd", t_e)
                                t_gate = k.op(
                                    "gpsimd", lambda e, pt0=pt0, off=off:
                                    e.tensor_mul(
                                        PT[:, pt0 + off:pt0 + off + 128],
                                        PT[:, pt0 + off:pt0 + off + 128],
                                        tri[:, :]), sig=True)
                            qs_lo = off // 128
                            k.await_("tensor", t_gate)
                            if kt == 0 and g == 0:
                                k.await_("tensor",
                                         *[t for qs in range(4)
                                           for t in t_avevac[qs]])
                            for qs in range(qs_lo, 4):
                                avb = 4 + qs // 2
                                avo = (qs % 2) * 256
                                first = (kt == 0 and g == 0 and qs % 2 == 0)
                                last = (kt == 4 * qb + qs)
                                tk = k.op("tensor", lambda e, avb=avb, avo=avo,
                                          g=g, pt0=pt0, qs=qs, kt=kt,
                                          first=first, last=last:
                                          e.matmul(
                                              psb(avb, 65, avo + 65 * g),
                                              PT[:, pt0 + qs * 128:
                                                  pt0 + qs * 128 + 128],
                                              vaug[:, kt * 129 + g * 64:
                                                   kt * 129 + g * 64 + 65],
                                              start=first, stop=last,
                                              skip_group_check=True),
                                          sig=last)
                                if last:
                                    t_avstop[qs][g] = tk
                            t_ptfree[g][kt % 2] = k.sig("tensor")
                    for qs in range(4):
                        tsub = qb * 4 + qs
                        avb = 4 + qs // 2
                        avo = (qs % 2) * 256
                        k.await_("vector", t_avstop[qs][0], t_avstop[qs][1],
                                 t_ytm_freed)
                        k.op("vector", lambda e, avb=avb, avo=avo:
                             e.reciprocal(s_tmp[:, 0:1],
                                          psb(avb, 1, avo + 64)))
                        k.drain("vector")
                        k.op("vector", lambda e, avb=avb, avo=avo:
                             e.reciprocal(s_tmp[:, 1:2],
                                          psb(avb, 1, avo + 65)))
                        k.drain("vector")
                        k.op("vector", lambda e, avb=avb, avo=avo, tsub=tsub,
                             hp=hp: e.tensor_scalar(
                                 ytm[:, tsub * 1024 + hp * 128:
                                     tsub * 1024 + hp * 128 + 64],
                                 psb(avb, 64, avo), s_tmp[:, 0:1], None,
                                 ALU.mult))
                        k.op("vector", lambda e, avb=avb, avo=avo, tsub=tsub,
                             hp=hp: e.tensor_scalar(
                                 ytm[:, tsub * 1024 + hp * 128 + 64:
                                     tsub * 1024 + hp * 128 + 128],
                                 psb(avb, 64, avo + 66), s_tmp[:, 1:2], None,
                                 ALU.mult))
                        k.drain("vector")
                        tv = k.sig("vector")
                        t_avevac[qs] = [tv, tv]
                t_pair_pe = k.sig("tensor")
            t_B_vec = k.sig("vector")
            if "C" not in P1SUB:
                t_chunk_pe = k.sig("tensor")
                t_ytm_freed = t_chunk_pe
                continue

            k.await_("sync", t_pair_pe, t_B_vec)
            t_wo = [k.dma("sync", wo_sb[:, ec * 1024:(ec + 1) * 1024],
                          aWoT[ec * 128:(ec + 1) * 128, :], "wo")
                    for ec in range(8)]
            t_yT_free = None
            t_res_use = list(t_xuse)
            t_trd2 = list(t_trd)
            k.await_("tensor", t_B_vec)
            for tsub in range(16):
                for ec in range(8):
                    bank = 6 + ec % 2
                    k.await_("tensor", t_trd2[ec % 2])
                    t_tr = k.op("tensor", lambda e, bank=bank, tsub=tsub,
                                ec=ec: e.transpose(
                                    psb(bank, 128),
                                    ytm[:, tsub * 1024 + ec * 128:
                                        tsub * 1024 + ec * 128 + 128],
                                    id_f[:, :]), sig=True)
                    k.await_("scalar", t_tr, t_yT_free)
                    t_trd2[ec % 2] = k.op(
                        "scalar", lambda e, bank=bank, ec=ec:
                        e.copy(yT[:, ec * 128:(ec + 1) * 128], psb(bank, 128)),
                        sig=True)
                b = tsub % 2
                k.await_("sync", t_res_use[b])
                t_xr = k.dma("sync", xld[:, b * 1024:(b + 1) * 1024],
                             x_in[tc + tsub * 128:tc + tsub * 128 + 128, :],
                             f"xa{b}")
                t_o = [None, None]
                k.await_("tensor", t_bank_ev[0], t_bank_ev[1], t_trd2[0],
                         t_trd2[1], *t_wo)
                for nb in range(2):
                    bank = nb
                    for ec in range(8):
                        tko = k.op("tensor", lambda e, bank=bank, ec=ec,
                                   nb=nb: e.matmul(
                                       psb(bank),
                                       yT[:, ec * 128:(ec + 1) * 128],
                                       wo_sb[:, ec * 1024 + nb * 512:
                                             ec * 1024 + nb * 512 + 512],
                                       start=(ec == 0), stop=(ec == 7)),
                                   sig=(ec == 7))
                    t_o[nb] = tko
                t_yT_free = k.sig("tensor")
                for nb in range(2):
                    k.await_("vector", t_o[nb], t_xr, t_x1st.get(b))
                    k.op("vector", lambda e, nb=nb, b=b: e.tensor_add(
                        x1o[:, b * 1024 + nb * 512:b * 1024 + nb * 512 + 512],
                        psb(nb),
                        xld[:, b * 1024 + nb * 512:b * 1024 + nb * 512 + 512]))
                    k.drain("vector")
                    t_bank_ev[nb] = k.sig("vector")
                t_res_use[b] = t_bank_ev[1]
                k.await_("gpsimd", t_bank_ev[0], t_bank_ev[1])
                t_x1st[b] = k.dma(
                    "gpsimd", xs1[tc + tsub * 128:tc + tsub * 128 + 128, :],
                    x1o[:, b * 1024:(b + 1) * 1024], f"x1s{b}")
            t_xuse = list(t_res_use)
            t_chunk_pe = k.sig("tensor")
            t_ytm_freed = t_chunk_pe
        if "Z" not in P1SUB:
            barrier()
        p.close()

    if nphases >= 1:
        build_p1()

    sc2 = sbuf("sc2", [1, 4])
    ccs = sbuf("ccs", [1, 128])
    f_ssw1 = sbuf("f_ssw1", [128, 32])
    f_b1q = sbuf("f_b1q", [128, 32])
    f_ssw2 = sbuf("f_ssw2", [128, 8])
    f_b2q = sbuf("f_b2q", [128, 8])
    f_i32s = sbuf("f_i32s", [128, 64], I32)
    hmax = sbuf("hmax", [128, 1])

    def global_max_scale(src_col, slot, t_ready):
        k.await_("tensor", t_ready)
        t_tr = k.op("tensor", lambda e: e.transpose(
            psb(7, 128, p1=1), src_col, id_f[:, :]), sig=True)
        k.await_("vector", t_tr)
        k.op("vector", lambda e: e.reduce_max(sc2[0:1, 0:1],
                                              psb(7, 128, p1=1), axis=AX))
        k.drain("vector")
        k.op("vector", lambda e: e.tensor_scalar(
            ccs[0:1, :], ones_r[0:1, :], sc2[0:1, 0:1], None, ALU.mult))
        k.drain("vector")
        t_v = k.sig("vector")
        k.await_("gpsimd", t_v)
        t_ci = k.dma("gpsimd", cc_i.ap(), ccs[0:1, :], "cc0")
        k.await_("gpsimd", t_ci)
        t_cc = k.cc_allreduce_max(cc_i.ap(), cc_o.ap())
        k.await_("gpsimd", t_cc)
        t_co = k.dma("gpsimd", sc2[0:1, 0:1], cc_o.ap()[0:1, 0:1], "cc0")
        k.await_("vector", t_co)
        k.op("vector", lambda e: e.tensor_scalar(
            sc2[0:1, 0:1], sc2[0:1, 0:1], EPS_Q, 1.0 / 255.0,
            ALU.max, ALU.mult))
        k.drain("vector")
        k.op("vector", lambda e: e.reciprocal(sc2[0:1, 1:2], sc2[0:1, 0:1]))
        k.drain("vector")
        t_v2 = k.sig("vector")
        t_bc = bcastN(lambda o, w, slot=slot: smv[:, slot + o:slot + o + w],
                      sc2, 2, t_v2)
        return t_bc

    def bias_quant(b_sb, sw_sb, ssw_dst, bq_dst, n, slot):
        k.op("vector", lambda e: e.tensor_scalar(
            ssw_dst[:, 0:n], sw_sb[:, 0:n], smv[:, slot:slot + 1], None,
            ALU.mult))
        k.drain("vector")
        k.op("vector", lambda e: e.reciprocal(sq_scr[:, 0:n], ssw_dst[:, 0:n]))
        k.drain("vector")
        k.op("vector", lambda e: e.tensor_mul(sq_scr[:, 32:32 + n],
                                              b_sb[:, 0:n], sq_scr[:, 0:n]))
        k.drain("vector")
        k.op("vector", lambda e: e.tensor_scalar(
            sq_scr[:, 32:32 + n], sq_scr[:, 32:32 + n], 32768.0, None,
            ALU.add))
        k.drain("vector")
        k.op("vector", lambda e: e.tensor_copy(f_i32s[:, 0:n],
                                               sq_scr[:, 32:32 + n]))
        k.drain("vector")
        k.op("vector", lambda e: e.tensor_copy(sq_scr[:, 64:64 + n],
                                               f_i32s[:, 0:n]))
        k.drain("vector")
        k.op("vector", lambda e: e.tensor_sub(sq_scr[:, 32:32 + n],
                                              sq_scr[:, 32:32 + n],
                                              sq_scr[:, 64:64 + n]))
        k.drain("vector")
        k.op("vector", lambda e: e.scalar_tensor_tensor(
            sq_scr[:, 64:64 + n], sq_scr[:, 32:32 + n], 0.5,
            sq_scr[:, 64:64 + n], ALU.is_ge, ALU.add))
        k.drain("vector")
        k.op("vector", lambda e: e.tensor_scalar(
            sq_scr[:, 64:64 + n], sq_scr[:, 64:64 + n], 32768.0, None,
            ALU.subtract))
        k.drain("vector")
        k.op("vector", lambda e: e.tensor_mul(bq_dst[:, 0:n],
                                              sq_scr[:, 64:64 + n],
                                              ssw_dst[:, 0:n]))
        k.drain("vector")
        return k.sig("vector")

    def build_ffn(xr_in, xw_out, tag=""):
        p = ExitStack()

        def psbuf(name, shape, dt=F32):
            return p.enter_context(nc.sbuf_tensor(name + tag, shape, dt))

        xld2 = psbuf("f_xld", [128, 2 * 1024])
        t_xu = [None, None]
        t_mx = None
        for tt in range(32):
            b = tt % 2
            xs_ = xld2[:, b * 1024:(b + 1) * 1024]
            k.await_("sync", t_xu[b])
            t_x = k.dma("sync", xs_, xr_in[tt * 128:tt * 128 + 128, :], f"xa{b}")
            t_st_ = ln_stats(xs_, tt, True, t_x)
            k.op("vector", lambda e, tt=tt: e.tensor_scalar(
                st_mx[:, tt:tt + 1], st_mx[:, tt:tt + 1],
                st_rs[:, tt:tt + 1], st_mu[:, tt:tt + 1],
                ALU.mult, ALU.add))
            k.drain("vector")
            t_xu[b] = t_st_
        k.op("vector", lambda e: e.reduce_max(st_mx[:, 32:33],
                                              st_mx[:, 0:32], axis=AX))
        k.drain("vector")
        t_mx = k.sig("vector")
        t_s1 = global_max_scale(st_mx[:, 32:33], 0, t_mx)
        k.await_("vector", t_s1)
        t_b1q = bias_quant(b1b, sw1b, f_ssw1, f_b1q, 32, 0)

        w1qs = psbuf("f_w1qs", [128, 8 * FF], BF16)
        xqT = psbuf("f_xqT", [128, 8 * 512], BF16)
        xqf = psbuf("f_xqf", [128, 2 * 1024])
        xqi = psbuf("f_xqi", [128, 1024], I32)
        xr0 = psbuf("f_xr0", [128, 1024])
        xqb = psbuf("f_xqb", [128, 2 * 1024])
        hsb = psbuf("f_hsb", [128, 2 * 512])
        t_w1l = [k.dma("sync", w1qs[:, dt * FF:(dt + 1) * FF],
                       w1q_h[dt * 128:(dt + 1) * 128, :], "wq1")
                 for dt in range(8)]
        k.op("vector", lambda e: e.memset(hmax[:, :], 0.0))
        k.drain("vector")
        t_hm = k.sig("vector")
        t_xqT_pe = None
        t_xqb_pe = [None, None]
        t_xqi_free = None
        t_hsb_st = [None, None]
        t_hsb_mx = [None, None]
        t_trq = [None, None]
        t_bank_ev2 = {0: None, 1: None}
        k.await_("scalar", t_mx)
        for tb in range(8):
            for tt4 in range(4):
                tt = tb * 4 + tt4
                b = tt % 2
                xs_ = xld2[:, b * 1024:(b + 1) * 1024]
                xq_ = xqf[:, b * 1024:(b + 1) * 1024]
                xb_ = xqb[:, b * 1024:(b + 1) * 1024]
                k.await_("sync", t_xu[b])
                t_x = k.dma("sync", xs_, xr_in[tt * 128:tt * 128 + 128, :], f"xa{b}")
                t_ap = ln_apply(xq_, xs_, tt, t_x, sig=True)
                t_xu[b] = t_ap
                k.await_("vector", t_ap, t_s1)
                k.op("vector", lambda e, xq_=xq_: e.tensor_scalar(
                    xq_, xq_, smv[:, 1:2], None, ALU.mult))
                k.drain("vector")
                k.await_("vector", t_xqi_free)
                k.op("vector", lambda e, xq_=xq_: e.tensor_copy(
                    xqi[:, :], xq_))
                k.drain("vector")
                k.op("vector", lambda e: e.tensor_copy(xr0[:, :], xqi[:, :]))
                k.drain("vector")
                t_xqi_free = k.sig("vector")
                k.op("vector", lambda e, xq_=xq_: e.tensor_sub(
                    xq_, xq_, xr0[:, :]))
                k.drain("vector")
                k.op("vector", lambda e, xq_=xq_: e.scalar_tensor_tensor(
                    xr0[:, :], xq_, 0.5, xr0[:, :], ALU.is_ge, ALU.add))
                k.drain("vector")
                k.await_("vector", t_xqb_pe[b])
                k.op("vector", lambda e, xb_=xb_: e.tensor_scalar(
                    xb_, xr0[:, :], 0.0, None, ALU.max))
                k.drain("vector")
                t_q = k.sig("vector")
                for dt in range(8):
                    j = tt4 * 8 + dt
                    bank = 6 + j % 2
                    k.await_("tensor", t_q, t_trq[j % 2])
                    t_tr = k.op("tensor", lambda e, bank=bank, xb_=xb_,
                                dt=dt: e.transpose(
                                    psb(bank, 128),
                                    xb_[:, dt * 128:(dt + 1) * 128],
                                    id_f[:, :]), sig=True)
                    t_xqb_pe[b] = t_tr
                    k.await_("scalar", t_tr)
                    t_trq[j % 2] = k.op(
                        "scalar", lambda e, bank=bank, dt=dt, tt4=tt4:
                        e.copy(xqT[:, dt * 512 + tt4 * 128:
                                   dt * 512 + tt4 * 128 + 128],
                               psb(bank, 128)), sig=True)
            t_xq_done = t_trq[31 % 2]
            k.await_("tensor", t_xq_done, *t_w1l, t_b1q)
            for ft in range(32):
                bank = ft % 2
                k.await_("tensor", t_bank_ev2[bank])
                tokm = None
                for dt in range(8):
                    tokm = k.op("tensor", lambda e, bank=bank, ft=ft, dt=dt:
                                e.matmul(psb(bank),
                                         w1qs[:, dt * FF + ft * 128:
                                              dt * FF + ft * 128 + 128],
                                         xqT[:, dt * 512:(dt + 1) * 512],
                                         start=(dt == 0), stop=(dt == 7)),
                                sig=(dt == 7))
                k.await_("scalar", tokm, t_hsb_st[ft % 2], t_hsb_mx[ft % 2])
                t_h = k.op("scalar", lambda e, bank=bank, ft=ft:
                           e.activation(hsb[:, (ft % 2) * 512:
                                            (ft % 2) * 512 + 512],
                                        psb(bank), AF.Relu,
                                        bias=f_b1q[:, ft:ft + 1],
                                        scale=f_ssw1[:, ft:ft + 1]), sig=True)
                t_bank_ev2[bank] = t_h
                k.await_("vector", t_h, t_hm)
                k.op("vector", lambda e, ft=ft: e.reduce_max(
                    s_tmp[:, 3:4], hsb[:, (ft % 2) * 512:(ft % 2) * 512 + 512],
                    axis=AX))
                k.drain("vector")
                k.op("vector", lambda e: e.tensor_max(
                    hmax[:, :], hmax[:, :], s_tmp[:, 3:4]))
                k.drain("vector")
                t_hsb_mx[ft % 2] = k.sig("vector")
                k.await_("gpsimd", t_h)
                for s_ in range(2):
                    t_hsb_st[ft % 2] = k.dma(
                        "gpsimd",
                        hsp[tb * 2 + s_, :, ft * 256:(ft + 1) * 256],
                        hsb[:, (ft % 2) * 512 + s_ * 256:
                            (ft % 2) * 512 + s_ * 256 + 256], f"hs{ft % 2}")
            t_xqT_pe = k.sig("tensor")
        t_mm1 = k.sig("vector")
        barrier()
        p.close()

        t_s2 = global_max_scale(hmax[:, :], 2, None)
        k.await_("vector", t_s2)
        t_b2q = bias_quant(b2b, sw2b, f_ssw2, f_b2q, 8, 2)
        p2 = ExitStack()

        def psbuf2(name, shape, dt=F32):
            return p2.enter_context(nc.sbuf_tensor(name + tag, shape, dt))

        s2rep = psbuf2("f_s2rep", [128, 1024])
        b2rep = psbuf2("f_b2rep", [128, 1024])
        t_row1 = col2row(swrow, lambda j: f_ssw2[:, j:j + 1], 8, t_b2q)
        t_r1 = bcastN(lambda o, w: s2rep[:, o:o + w], swrow, 1024, t_row1)
        t_row2 = col2row(swrow, lambda j: f_b2q[:, j:j + 1], 8, t_r1)
        t_r2 = bcastN(lambda o, w: b2rep[:, o:o + w], swrow, 1024, t_row2)

        w2qs = psbuf2("f_w2qs", [128, 32 * 1024], BF16)
        hld = psbuf2("f_hld", [128, 2 * 2048])
        hqi = psbuf2("f_hqi", [128, 2048], I32)
        hr0 = psbuf2("f_hr0", [128, 2048])
        hqb = psbuf2("f_hqb", [128, 2 * 8192], BF16)
        x2o = psbuf2("f_x2o", [128, 2 * 1024])
        xr2 = psbuf2("f_xr2", [128, 2 * 1024])
        t_w2l = [k.dma("sync", w2qs[:, ft * 1024:(ft + 1) * 1024],
                       w2q_h[ft * 128:(ft + 1) * 128, :], "wq2")
                 for ft in range(32)]
        t_hld_v = [None, None]
        t_hcast = None
        t_hqb_pe = [None, None]
        t_x2st = [None, None]
        t_xr2_v = [None, None]
        t_bank_v = {i: None for i in range(4)}
        for tb2 in range(16):
            b2 = tb2 % 2
            for hh in range(4):
                qb_ = hh % 2
                k.await_("gpsimd", t_hld_v[qb_], t_mm1)
                t_hl = k.dma("gpsimd", hld[:, qb_ * 2048:(qb_ + 1) * 2048],
                             hsp[tb2, :, hh * 2048:(hh + 1) * 2048],
                             f"hl{qb_}")
                sl = slice(qb_ * 2048, (qb_ + 1) * 2048)
                k.await_("vector", t_hl, t_s2, t_hqb_pe[b2])
                k.op("vector", lambda e, sl=sl: e.tensor_scalar(
                    hld[:, sl], hld[:, sl], smv[:, 3:4], None, ALU.mult))
                k.drain("vector")
                t_hm1 = k.sig("vector")
                k.await_("scalar", t_hm1, t_hcast)
                k.op("scalar", lambda e, sl=sl: e.copy(hqi[:, :], hld[:, sl]))
                k.drain("scalar")
                t_hc = k.op("scalar", lambda e: e.copy(hr0[:, :], hqi[:, :]),
                            sig=True)
                k.await_("vector", t_hc)
                k.op("vector", lambda e, sl=sl: e.tensor_sub(
                    hld[:, sl], hld[:, sl], hr0[:, :]))
                k.drain("vector")
                k.op("vector", lambda e, sl=sl, hh=hh, b2=b2:
                     e.scalar_tensor_tensor(
                         hqb[:, b2 * 8192 + hh * 2048:
                             b2 * 8192 + (hh + 1) * 2048], hld[:, sl], 0.5,
                         hr0[:, :], ALU.is_ge, ALU.add))
                k.drain("vector")
                t_hld_v[qb_] = k.sig("vector")
                t_hcast = t_hld_v[qb_]
            t_hq = t_hld_v[1]
            for tsub in range(2):
                tt = tb2 * 2 + tsub
                bq = tsub % 2
                k.await_("sync", t_xr2_v[bq])
                t_xr = k.dma("sync", xr2[:, bq * 1024:(bq + 1) * 1024],
                             xr_in[tt * 128:tt * 128 + 128, :], f"xr{bq}")
                toks = [None, None]
                for dmh in range(2):
                    bank = tsub * 2 + dmh
                    k.await_("tensor", t_hq, t_bank_v[bank], *t_w2l)
                    tokm = None
                    for ft in range(32):
                        tokm = k.op(
                            "tensor", lambda e, bank=bank, ft=ft, tsub=tsub,
                            dmh=dmh, b2=b2: e.matmul(
                                psb(bank),
                                hqb[:, b2 * 8192 + ft * 256 + tsub * 128:
                                    b2 * 8192 + ft * 256 + tsub * 128 + 128],
                                w2qs[:, ft * 1024 + dmh * 512:
                                     ft * 1024 + dmh * 512 + 512],
                                start=(ft == 0), stop=(ft == 31)),
                            sig=(ft == 31))
                    toks[dmh] = tokm
                t_hqb_pe[b2] = k.sig("tensor")
                for dmh in range(2):
                    bank = tsub * 2 + dmh
                    k.await_("vector", toks[dmh], t_xr, t_x2st[bq],
                             t_r1, t_r2)
                    dsl = slice(bq * 1024 + dmh * 512,
                                bq * 1024 + dmh * 512 + 512)
                    wsl = slice(dmh * 512, dmh * 512 + 512)
                    k.op("vector", lambda e, bank=bank, dsl=dsl, wsl=wsl:
                         e.scalar_tensor_tensor(
                             x2o[:, dsl], psb(bank), 1.0, s2rep[:, wsl],
                             ALU.bypass, ALU.mult))
                    k.drain("vector")
                    k.op("vector", lambda e, dsl=dsl, wsl=wsl: e.tensor_add(
                        x2o[:, dsl], x2o[:, dsl], b2rep[:, wsl]))
                    k.drain("vector")
                    k.op("vector", lambda e, dsl=dsl: e.tensor_add(
                        x2o[:, dsl], x2o[:, dsl], xr2[:, dsl]))
                    k.drain("vector")
                    t_bank_v[bank] = k.sig("vector")
                t_xr2_v[bq] = t_bank_v[tsub * 2 + 1]
                k.await_("gpsimd", t_bank_v[tsub * 2], t_bank_v[tsub * 2 + 1])
                t_x2st[bq] = k.dma(
                    "gpsimd", xw_out[tt * 128:tt * 128 + 128, :],
                    x2o[:, bq * 1024:(bq + 1) * 1024], f"x2s{bq}")
        barrier()
        p2.close()

    if nphases >= 2:
        build_ffn(xs1, xs2)

    def build_p3():
        SBW = 512
        p = ExitStack()

        def psbuf(name, shape, dt=F32):
            return p.enter_context(nc.sbuf_tensor(name, shape, dt))

        S_aug = psbuf("b_Saug", [128, 8 * 1025])
        xld3 = psbuf("b_xld", [128, 2 * 1024])
        xlb3 = psbuf("b_xlb", [128, 2 * 1024])
        xlnT3 = psbuf("b_xlnT", [128, 8 * SBW])
        qf = psbuf("b_qf", [128, 8 * SBW])
        kf = psbuf("b_kf", [128, 8 * SBW])
        qsc = psbuf("b_qsc", [128, 8 * 256])
        vau = psbuf("b_vau", [128, 4 * 1025])
        wvres = psbuf("b_wv", [128, 8 * 1024])
        wqk = psbuf("b_wqk", [128, 2 * 1024])
        ktld = psbuf("b_ktld", [128, 2 * 1024])
        A_sb = psbuf("b_A", [128, 2 * 256])
        ybuf = psbuf("b_y", [128, 2 * 1024])
        xr3 = psbuf("b_xr3", [128, 2 * 1024])

        for i in range(8):
            k.op("vector", lambda e, i=i: e.memset(
                S_aug[:, i * 1025:(i + 1) * 1025], 0.0))
        k.drain("vector")
        t_S0 = k.sig("vector")
        t_wv = [k.dma("sync", wvres[:, dt * 1024:(dt + 1) * 1024],
                      bWvT[dt * 128:(dt + 1) * 128, :], "wv")
               for dt in range(8)]

        t_b = {i: None for i in range(8)}
        t_tr3 = [None, None]
        t_xu3 = [None, None]
        t_xlb_pe3 = [None, None]
        t_wqk_pe = None
        t_qkv_w = [None, None]
        t_sb_pe = None
        t_vau_w = None
        t_ktld_w = None
        t_A_w = None
        t_y_done = [None, None]
        t_x3st = [None, None]
        t_Sev = [None] * 8
        t_slot3 = [None] * 4
        t_state_pe = t_S0
        for sb in range(8):
            t0 = sb * SBW
            for tt in range(4):
                gt = sb * 4 + tt
                b = tt % 2
                xs_ = xld3[:, b * 1024:(b + 1) * 1024]
                xb_ = xlb3[:, b * 1024:(b + 1) * 1024]
                k.await_("sync", t_xu3[b])
                t_x = k.dma("sync", xs_, xs2[t0 + tt * 128:t0 + tt * 128 + 128, :],
                            f"xa{b}")
                t_st_ = ln_stats(xs_, tt, False, t_x, t_slot3[tt])
                k.await_("scalar", t_xlb_pe3[b])
                t_ap = ln_apply(xb_, xs_, tt, t_st_, sig=True)
                t_slot3[tt] = t_ap
                t_xu3[b] = t_ap
                for dt in range(8):
                    j = tt * 8 + dt
                    bank = 6 + (j % 2)
                    k.await_("tensor", t_ap, t_tr3[j % 2], t_sb_pe, t_b[6])
                    t_tr = k.op("tensor", lambda e, bank=bank, dt=dt, xb_=xb_:
                                e.transpose(psb(bank, 128),
                                            xb_[:, dt * 128:(dt + 1) * 128],
                                            id_f[:, :]), sig=True)
                    t_xlb_pe3[b] = t_tr
                    k.await_("scalar", t_tr)
                    t_tr3[j % 2] = k.op(
                        "scalar", lambda e, bank=bank, dt=dt, tt=tt:
                        e.copy(xlnT3[:, dt * SBW + tt * 128:
                                     dt * SBW + tt * 128 + 128],
                               psb(bank, 128)), sig=True)
            t_ln_done = t_tr3[31 % 2]
            for i in range(8):
                k.await_("sync", t_wqk_pe)
                t_wl = [k.dma("sync",
                              wqk[:, j * 1024:(j + 1) * 1024]
                              .rearrange("p (dt e) -> p dt e", e=128),
                              W[:, i * 128:(i + 1) * 128]
                              .rearrange("(dt p) e -> p dt e", p=128), "wp")
                        for j, W in enumerate((bWqT, bWkT))]
                for (dst, wj, bias) in ((qf, 0, bqb), (kf, 1, bkb)):
                    bank = wj
                    k.await_("tensor", *t_wl, t_ln_done, t_b[bank])
                    tok = None
                    for dt in range(8):
                        tok = k.op("tensor", lambda e, bank=bank, wj=wj, dt=dt:
                                   e.matmul(psb(bank),
                                            wqk[:, wj * 1024 + dt * 128:
                                                wj * 1024 + dt * 128 + 128],
                                            xlnT3[:, dt * SBW:dt * SBW + SBW],
                                            start=(dt == 0), stop=(dt == 7)),
                                   sig=(dt == 7))
                    k.await_("scalar", tok, t_sb_pe)
                    t_b[bank] = k.op(
                        "scalar", lambda e, bank=bank, dst=dst, i=i,
                        bias=bias: e.activation(
                            dst[:, i * SBW:(i + 1) * SBW], psb(bank), AF.Relu,
                            bias=bias[:, i:i + 1]), sig=True)
                t_wqk_pe = k.sig("tensor")
            t_qk_done = t_b[1]
            k.op("vector", lambda e: e.memset(
                vau[:, :].rearrange("p (kt c) -> p kt c", c=1025)
                [:, :, 1024:1025], 1.0))
            k.drain("vector")
            t_ones3 = k.sig("vector")
            for kt in range(4):
                k.await_("tensor", t_b[2], t_b[3], t_sb_pe, *t_wv, t_ln_done)
                tokv = None
                for dt in range(8):
                    for nb in range(2):
                        tokv = k.op(
                            "tensor", lambda e, kt=kt, dt=dt, nb=nb:
                            e.matmul(psb(2 + nb),
                                     xlnT3[:, dt * SBW + kt * 128:
                                           dt * SBW + kt * 128 + 128],
                                     wvres[:, dt * 1024 + nb * 512:
                                           dt * 1024 + nb * 512 + 512],
                                     start=(dt == 0), stop=(dt == 7)),
                            sig=(dt == 7 and nb == 1))
                k.await_("vector", tokv, t_ones3, t_vau_w, t_state_pe)
                for nb in range(2):
                    k.op("vector", lambda e, kt=kt, nb=nb: e.tensor_add(
                        vau[:, kt * 1025 + nb * 512:kt * 1025 + nb * 512 + 512],
                        psb(2 + nb), bvrep[:, nb * 512:nb * 512 + 512]))
                k.drain("vector")
                t_vau_w = k.sig("vector")
                t_b[2] = t_vau_w
                t_b[3] = t_vau_w
            for cc in range(2):
                qoff = cc * 256
                k.await_("vector", t_qk_done, t_state_pe)
                for i in range(8):
                    k.op("vector", lambda e, i=i, qoff=qoff: e.tensor_mul(
                        qsc[:, i * 256:(i + 1) * 256],
                        qf[:, i * SBW + qoff:i * SBW + qoff + 256],
                        gvrep[:, :]))
                k.drain("vector")
                t_qsc = k.sig("vector")
                for i in range(8):
                    for kt2 in range(2):
                        j = i * 2 + kt2
                        bank = 6 + j % 2
                        k.await_("tensor", t_tr3[j % 2], t_qk_done, t_b[6])
                        t_tr = k.op(
                            "tensor", lambda e, bank=bank, i=i, kt2=kt2,
                            qoff=qoff: e.transpose(
                                psb(bank, 128),
                                kf[:, i * SBW + qoff + kt2 * 128:
                                   i * SBW + qoff + kt2 * 128 + 128],
                                id_f[:, :]), sig=True)
                        k.await_("scalar", t_tr, t_state_pe)
                        t_tr3[j % 2] = k.op(
                            "scalar", lambda e, bank=bank, i=i, kt2=kt2:
                            e.activation(
                                ktld[:, kt2 * 1024 + i * 128:
                                     kt2 * 1024 + i * 128 + 128],
                                psb(bank, 128), AF.Identity,
                                scale=gkv[:, kt2:kt2 + 1]), sig=True)
                t_kt_done = t_tr3[1]
                for jt in range(2):
                    bank = jt
                    k.await_("tensor", t_b[bank], t_qk_done)
                    tok = None
                    for i in range(8):
                        tok = k.op(
                            "tensor", lambda e, bank=bank, jt=jt, i=i,
                            qoff=qoff: e.matmul(
                                psb(bank, 256),
                                kf[:, i * SBW + qoff + jt * 128:
                                   i * SBW + qoff + jt * 128 + 128],
                                qf[:, i * SBW + qoff:i * SBW + qoff + 256],
                                start=(i == 0), stop=(i == 7)),
                            sig=(i == 7))
                    k.await_("vector", tok, t_A_w, t_state_pe)
                    k.op("vector", lambda e, bank=bank, jt=jt: e.tensor_mul(
                        A_sb[:, jt * 256:(jt + 1) * 256], psb(bank, 256),
                        Mt[:, jt * 256:(jt + 1) * 256]))
                    k.drain("vector")
                    t_A_w = k.sig("vector")
                    t_b[bank] = t_A_w
                for tsub in range(2):
                    gt = sb * 4 + cc * 2 + tsub
                    k.await_("tensor", t_qsc, t_A_w, t_kt_done,
                             t_b[4], t_b[5], t_b[6], *t_Sev)
                    grp = ((4, 512, 0), (5, 512, 512), (6, 1, 1024))
                    tok = None
                    for gi, (bank, w, so) in enumerate(grp):
                        for i in range(8):
                            tok = k.op(
                                "tensor", lambda e, bank=bank, w=w, so=so,
                                i=i, tsub=tsub: e.matmul(
                                    psb(bank, w),
                                    qsc[:, i * 256 + tsub * 128:
                                        i * 256 + tsub * 128 + 128],
                                    S_aug[:, i * 1025 + so:i * 1025 + so + w],
                                    start=(i == 0), stop=False,
                                    skip_group_check=True),
                                sig=False)
                        for jt in range(tsub + 1):
                            tok = k.op(
                                "tensor", lambda e, bank=bank, w=w, so=so,
                                jt=jt, tsub=tsub, cc=cc: e.matmul(
                                    psb(bank, w),
                                    A_sb[:, jt * 256 + tsub * 128:
                                         jt * 256 + tsub * 128 + 128],
                                    vau[:, (cc * 2 + jt) * 1025 + so:
                                        (cc * 2 + jt) * 1025 + so + w],
                                    start=False, stop=(jt == tsub),
                                    skip_group_check=True),
                                sig=(jt == tsub and gi == 2))
                    k.await_("vector", tok, t_y_done[tsub], t_x3st[tsub])
                    k.op("vector", lambda e: e.tensor_scalar(
                        s_tmp[:, 0:1], psb(6, 1), EPS_B, None, ALU.add))
                    k.drain("vector")
                    k.op("vector", lambda e: e.reciprocal(
                        s_tmp[:, 0:1], s_tmp[:, 0:1]))
                    k.drain("vector")
                    for nb in range(2):
                        k.op("vector", lambda e, nb=nb, tsub=tsub:
                             e.tensor_scalar(
                                 ybuf[:, tsub * 1024 + nb * 512:
                                      tsub * 1024 + nb * 512 + 512],
                                 psb(4 + nb), s_tmp[:, 0:1], None, ALU.mult))
                    k.drain("vector")
                    t_ynum = k.sig("vector")
                    t_b[4] = t_ynum
                    t_b[5] = t_ynum
                    t_b[6] = t_ynum
                    k.await_("sync", t_y_done[tsub])
                    t_xr = k.dma("sync", xr3[:, tsub * 1024:(tsub + 1) * 1024],
                                 xs2[gt * 128:gt * 128 + 128, :], f"xr{tsub}")
                    k.await_("vector", t_xr, t_x3st[tsub])
                    k.op("vector", lambda e, tsub=tsub: e.tensor_add(
                        ybuf[:, tsub * 1024:(tsub + 1) * 1024],
                        ybuf[:, tsub * 1024:(tsub + 1) * 1024],
                        xr3[:, tsub * 1024:(tsub + 1) * 1024]))
                    k.drain("vector")
                    t_y_done[tsub] = k.sig("vector")
                    k.await_("gpsimd", t_y_done[tsub])
                    t_x3st[tsub] = k.dma(
                        "gpsimd", xs3[gt * 128:gt * 128 + 128, :],
                        ybuf[:, tsub * 1024:(tsub + 1) * 1024], f"x3s{tsub}")
                for i in range(8):
                    k.await_("tensor", t_b[0], t_b[1], t_b[2], t_kt_done,
                             t_Sev[i])
                    grp = ((0, 512, 0), (1, 512, 512), (2, 1, 1024))
                    tok = None
                    for gi, (bank, w, so) in enumerate(grp):
                        for kt2 in range(2):
                            tok = k.op(
                                "tensor", lambda e, bank=bank, w=w, so=so,
                                i=i, kt2=kt2, cc=cc: e.matmul(
                                    psb(bank, w),
                                    ktld[:, kt2 * 1024 + i * 128:
                                         kt2 * 1024 + i * 128 + 128],
                                    vau[:, (cc * 2 + kt2) * 1025 + so:
                                        (cc * 2 + kt2) * 1025 + so + w],
                                    start=(kt2 == 0), stop=(kt2 == 1),
                                    skip_group_check=True),
                                sig=(kt2 == 1 and gi == 2))
                    k.await_("vector", tok)
                    for bank, w, so in grp:
                        k.op("vector", lambda e, i=i, bank=bank, w=w, so=so:
                             e.scalar_tensor_tensor(
                                 S_aug[:, i * 1025 + so:i * 1025 + so + w],
                                 S_aug[:, i * 1025 + so:i * 1025 + so + w],
                                 gLv[:, 0:1], psb(bank, w),
                                 ALU.mult, ALU.add))
                        k.drain("vector")
                    t_Sev[i] = k.sig("vector")
                    t_b[0] = t_Sev[i]
                    t_b[1] = t_Sev[i]
                    t_b[2] = t_Sev[i]
                t_state_pe = k.sig("tensor")
            t_sb_pe = k.sig("tensor")
        barrier()
        p.close()

    if nphases >= 3:
        build_p3()
    if nphases >= 4:
        build_ffn(xs3, y_out, tag="2")

    blk = es.enter_context(nc.Block())

    def runner(eng_name):
        def fn(e):
            for thunk in k.ops[eng_name]:
                thunk(e)
        return fn

    blk.sync(runner("sync"))
    blk.vector(runner("vector"))
    blk.scalar(runner("scalar"))
    blk.tensor(runner("tensor"))
    blk.gpsimd(runner("gpsimd"))
    es.close()
    return nc




def host_inputs(inputs, b):
    import ml_dtypes
    x = np.ascontiguousarray(np.asarray(inputs["L0_attn_a_hidden_states"])[b])
    T = lambda a: np.ascontiguousarray(np.asarray(a).T)
    C = np.ascontiguousarray
    d = {
        "x": x,
        "aWqT": T(inputs["L0_attn_a_Wq"]),
        "aWkT": T(inputs["L0_attn_a_Wk"]),
        "aWvT": T(inputs["L0_attn_a_Wv"]),
        "aWoT": T(inputs["L0_attn_a_Wo"]),
        "bWqT": T(inputs["L0_attn_b_Wq"]),
        "bWkT": T(inputs["L0_attn_b_Wk"]),
        "bWvT": T(inputs["L0_attn_b_Wv"]),
        "bq": C(np.asarray(inputs["L0_attn_b_bq"]).reshape(1, D)),
        "bk": C(np.asarray(inputs["L0_attn_b_bk"]).reshape(1, D)),
        "bv": C(np.asarray(inputs["L0_attn_b_bv"]).reshape(1, D)),
        "gam": C(np.asarray(inputs["L0_attn_b_gamma"], dtype=np.float32)
                 .reshape(1, 1)),
        "w1": C(np.asarray(inputs["L0_ffn_w1"])),
        "w1T": T(inputs["L0_ffn_w1"]),
        "w2": C(np.asarray(inputs["L0_ffn_w2"])),
        "w2T": T(inputs["L0_ffn_w2"]),
        "b1": C(np.asarray(inputs["L0_ffn_b1"]).reshape(1, FF)),
        "b2": C(np.asarray(inputs["L0_ffn_b2"]).reshape(1, D)),
        "c_id": np.eye(128, dtype=np.float32),
        "c_idb": np.eye(128).astype(ml_dtypes.bfloat16),
        "c_tri": np.triu(np.ones((128, 128), dtype=np.float32)),
        "c_tmj": np.maximum(
            np.arange(256)[None, :] - np.arange(256)[:, None], 0
        ).astype(np.float32),
        "c_m01": (np.arange(256)[None, :] >= np.arange(256)[:, None])
        .astype(np.float32),
        "c_t1": np.broadcast_to(np.arange(1, LB + 1, dtype=np.float32),
                                (128, LB)).copy(),
        "c_lmj": (LB - 1.0 - (np.arange(2)[None, :] * 128
                              + np.arange(128)[:, None])).astype(np.float32),
    }
    return d


_NC = {}


def kernel(**inputs):
    if "nc" not in _NC:
        _NC["nc"] = build_nc(dbg=False)
    in_maps = [host_inputs(inputs, b) for b in range(NCORES)]
    res = run_bass_kernel_spmd(_NC["nc"], in_maps, core_ids=list(range(NCORES)))
    return np.stack([np.asarray(res.results[b]["y"]) for b in range(NCORES)],
                    axis=0)


# version 09
# speedup vs baseline: 1.1775x; 1.1647x over previous
import numpy as np
from contextlib import ExitStack

import concourse.bass as bass
import concourse.mybir as mybir
from concourse.bass_utils import run_bass_kernel_spmd

F32 = mybir.dt.float32
BF16 = mybir.dt.bfloat16
I32 = mybir.dt.int32
AF = mybir.ActivationFunctionType
ALU = mybir.AluOpType
AX = mybir.AxisListType.X

S, D, FF = 4096, 1024, 4096
P1SUB = "ATBC"
LNLVL = 9
P0LVL = 9
NH, HD = 16, 64
CH = 2048
LB = 256
SBK = 1024
EPS_LN = 1e-5
EPS_B = 1e-6
EPS_Q = 1e-8
NCORES = 8

ENGS = ("sync", "vector", "scalar", "tensor", "gpsimd")


class K:

    def __init__(self, nc, es):
        self.nc = nc
        self.es = es
        self.ops = {e: [] for e in ENGS}
        self.sem = {}
        self.cnt = {}
        self.chan_eng = {}
        for e in ENGS:
            self.sem["d_" + e] = es.enter_context(nc.semaphore("d_" + e))
        self.sem["cc"] = es.enter_context(nc.semaphore("cc"))

    def _bump(self, name, by):
        self.cnt[name] = self.cnt.get(name, 0) + by
        return (name, self.cnt[name])

    def op(self, eng, fn, sig=False):
        if not sig:
            self.ops[eng].append(fn)
            return None
        name = "d_" + eng
        tok = self._bump(name, 1)
        s = self.sem[name]
        self.ops[eng].append(lambda e, fn=fn, s=s: fn(e).then_inc(s, 1))
        return tok

    def sig(self, eng):
        nm = "d_" + eng
        v = self.cnt.get(nm, 0)
        if v:
            s = self.sem[nm]
            self.ops[eng].append(lambda e, s=s, v=v: e.wait_ge(s, v))
        return self.op(eng, lambda e: e.nop(nofuse=True), sig=True)

    def dma(self, eng, out, in_, ch, slow=False):
        name = "dc_" + ch
        if name not in self.sem:
            self.sem[name] = self.es.enter_context(
                self.nc.semaphore(name))
        self.chan_eng.setdefault(eng, set()).add(name)
        tok = self._bump(name, 16)
        s = self.sem[name]
        self.ops[eng].append(
            lambda e, o=out, i=in_, s=s, sl=slow: e.dma_start(
                out=o, in_=i, allow_slow_non_contiguous=sl)
            .then_inc(s, 16))
        return tok

    def await_(self, eng, *toks):
        best = {}
        for t in toks:
            if t is None:
                continue
            nm, v = t
            best[nm] = max(best.get(nm, 0), v)
        for nm, v in best.items():
            s = self.sem[nm]
            self.ops[eng].append(lambda e, s=s, v=v: e.wait_ge(s, v))

    def drain(self, eng):
        self.ops[eng].append(lambda e: e.drain())

    def cc_allreduce_max(self, ins_ap, outs_ap):
        tok = self._bump("cc", 1)
        s = self.sem["cc"]
        self.ops["gpsimd"].append(
            lambda e, i=ins_ap, o=outs_ap, s=s: e.collective_compute(
                "AllReduce", ALU.max, replica_groups=[list(range(NCORES))],
                ins=[i], outs=[o]).then_inc(s, 1))
        return tok


def build_nc(dbg=False, nphases=5):
    nc = bass.Bass("TRN2", target_bir_lowering=False)

    def din(name, shape, dt=F32):
        return nc.dram_tensor(name, shape, dt, kind="ExternalInput").ap()

    x_in = din("x", [S, D])
    aWqT, aWkT = din("aWqT", [D, D]), din("aWkT", [D, D])
    aWvT, aWoT = din("aWvT", [D, D]), din("aWoT", [D, D])
    bWqT, bWkT, bWvT = (din("bWqT", [D, D]), din("bWkT", [D, D]),
                        din("bWvT", [D, D]))
    bq_h, bk_h, bv_h = din("bq", [1, D]), din("bk", [1, D]), din("bv", [1, D])
    gam_h = din("gam", [1, 1])
    w1_h, w1T_h = din("w1", [FF, D]), din("w1T", [D, FF])
    w2_h, w2T_h = din("w2", [D, FF]), din("w2T", [FF, D])
    b1_h, b2_h = din("b1", [1, FF]), din("b2", [1, D])
    c_id = din("c_id", [128, 128])
    c_idb = din("c_idb", [128, 128], BF16)
    c_tri = din("c_tri", [128, 128])
    c_tmj = din("c_tmj", [256, 256])
    c_m01 = din("c_m01", [256, 256])
    c_t1 = din("c_t1", [128, LB])
    c_lmj = din("c_lmj", [128, 2])

    y_out = nc.dram_tensor("y", [S, D], F32, kind="ExternalOutput").ap()

    okind = "ExternalOutput" if dbg else "Internal"
    xs1 = nc.dram_tensor("xs1", [S, D], F32, kind=okind).ap()
    xs2 = nc.dram_tensor("xs2", [S, D], F32, kind=okind).ap()
    xs3 = nc.dram_tensor("xs3", [S, D], F32, kind=okind).ap()
    hsp = nc.dram_tensor("hsp", [16, 128, 8192], F32).ap()
    w1q_h = nc.dram_tensor("w1qs", [D, FF], BF16,
                           kind=okind).ap()
    w2q_h = nc.dram_tensor("w2qs", [FF, D], BF16, kind=okind).ap()
    sw1_h = nc.dram_tensor("sw1s", [1, FF], F32, kind=okind).ap()
    sw2_h = nc.dram_tensor("sw2s", [1, D], F32).ap()
    srow_h = nc.dram_tensor("srows", [8, 128], F32).ap()
    vrow_h = nc.dram_tensor("vrows", [4, FF], F32).ap()
    cc_i = nc.dram_tensor("cc_i", [1, 128], F32)
    cc_o = nc.dram_tensor("cc_o", [1, 128], F32)

    es = ExitStack()
    k = K(nc, es)

    def sbuf(name, shape, dt=F32):
        return es.enter_context(nc.sbuf_tensor(name, shape, dt))

    PS = es.enter_context(nc.psum_tensor("PS", [128, 4096], F32))

    def psb(bank, w=512, off=0, p0=0, p1=128):
        return PS[p0:p1, bank * 512 + off: bank * 512 + off + w]

    id_f = sbuf("id_f", [128, 128])
    id_b = sbuf("id_b", [128, 128], BF16)
    tri = sbuf("tri", [128, 128])
    bvrep = sbuf("bvrep", [128, D])
    bqb, bkb = sbuf("bqb", [128, 8]), sbuf("bkb", [128, 8])
    b1b, b2b = sbuf("b1b", [128, 32]), sbuf("b2b", [128, 8])
    sw1b, sw2b = sbuf("sw1b", [128, 32]), sbuf("sw2b", [128, 8])
    Mt = sbuf("Mt", [128, 2 * 256])
    gvrep = sbuf("gvrep", [128, LB])
    gkv = sbuf("gkv", [128, 2])
    gLv = sbuf("gLv", [128, 1])
    dgl = sbuf("dgl", [128, 128])
    lng = sbuf("lng", [128, 1])
    smv = sbuf("smv", [128, 16])
    st_mu = sbuf("st_mu", [128, 32])
    st_rs = sbuf("st_rs", [128, 32])
    st_mx = sbuf("st_mx", [128, 33])
    sq_scr = sbuf("sq_scr", [128, 1024])
    s_tmp = sbuf("s_tmp", [128, 4])

    ones_r = sbuf("ones_r", [1, 128])
    g11 = sbuf("g11", [1, 2])
    swrow = sbuf("swrow", [1, 1024])

    t_cb = {6: None, 7: None}

    def col2row(dst_row, src_sb_col_fn, m, t_pre=None):
        tev = None
        for j in range(m):
            bank = 6 + j % 2
            k.await_("tensor", t_pre, t_cb[bank])
            t_tr = k.op("tensor", lambda e, bank=bank, j=j: e.transpose(
                psb(bank, 128, p1=1), src_sb_col_fn(j), id_f[:, :]), sig=True)
            k.await_("scalar", t_tr, tev)
            tev = k.op("scalar", lambda e, bank=bank, j=j: e.copy(
                dst_row[0:1, j * 128:(j + 1) * 128], psb(bank, 128, p1=1)),
                sig=True)
            t_cb[bank] = tev
        return tev

    def bcastN(dst_sb_fn, src_row, n, t_pre=None, ev_eng="scalar"):
        tev = None
        for nb in range((n + 511) // 512):
            w = min(512, n - nb * 512)
            bank = 6 + nb % 2
            k.await_("tensor", t_pre, t_ones_r, t_cb[bank])
            t_mm = k.op("tensor", lambda e, bank=bank, nb=nb, w=w: e.matmul(
                psb(bank, w), ones_r[0:1, :],
                src_row[0:1, nb * 512:nb * 512 + w],
                start=True, stop=True), sig=True)
            k.await_(ev_eng, t_mm, tev)
            tev = k.op(ev_eng, lambda e, bank=bank, nb=nb, w=w: e.tensor_copy(
                dst_sb_fn(nb * 512, w), psb(bank, w))
                if ev_eng == "vector" else e.copy(
                    dst_sb_fn(nb * 512, w), psb(bank, w)), sig=True)
            t_cb[bank] = tev
        return tev

    k.op("vector", lambda e: e.memset(ones_r[:, :], 1.0))
    k.drain("vector")
    t_ones_r = k.sig("vector")

    t_ld = [
        k.dma("sync", id_f[:, :], c_id, "c0"),
        k.dma("sync", id_b[:, :], c_idb, "c0"),
        k.dma("sync", tri[:, :], c_tri, "c0"),

        k.dma("sync", bqb[:, :], bq_h.rearrange("o (c p) -> (o p) c", p=128), "c0", slow=True),
        k.dma("sync", bkb[:, :], bk_h.rearrange("o (c p) -> (o p) c", p=128), "c0", slow=True),
        k.dma("sync", b1b[:, :], b1_h.rearrange("o (c p) -> (o p) c", p=128), "c0", slow=True),
        k.dma("sync", b2b[:, :], b2_h.rearrange("o (c p) -> (o p) c", p=128), "c0", slow=True),
        k.dma("sync", g11[0:1, 0:1], gam_h, "c0"),
    ]
    t_bvr = k.dma("sync", swrow[0:1, 0:1024], bv_h, "mis0")
    t_bvr2 = bcastN(lambda o, w: bvrep[:, o:o + w], swrow, 1024, t_bvr,
                    "vector")

    k.await_("scalar", *t_ld)
    t_ln0 = k.op("scalar", lambda e: e.activation(g11[0:1, 0:1], g11[0:1, 0:1],
                                                  AF.Ln), sig=True)
    t_ln = bcastN(lambda o, w: lng[:, o:o + w], g11, 1, t_ln0)
    t_m = [k.dma("sync", sq_scr[:, 0:256], c_tmj[0:128, :], "tb0"),
           k.dma("sync", sq_scr[:, 256:512], c_tmj[128:256, :], "tb0")]
    k.await_("vector", t_ln, *t_m)
    k.op("vector", lambda e: e.tensor_scalar(sq_scr[:, 0:512],
                                             sq_scr[:, 0:512],
                                             lng[:, 0:1], None, ALU.mult))
    k.drain("vector")
    t_v = k.sig("vector")
    k.await_("scalar", t_v)
    t_exp = k.op("scalar", lambda e: e.activation(Mt[:, :], sq_scr[:, 0:512],
                                                  AF.Exp), sig=True)
    k.await_("sync", t_exp)
    t_m01 = [k.dma("sync", sq_scr[:, 0:256], c_m01[0:128, :], "tb1"),
             k.dma("sync", sq_scr[:, 256:512], c_m01[128:256, :], "tb1")]
    k.await_("vector", t_exp, *t_m01)
    k.op("vector", lambda e: e.tensor_mul(Mt[:, :], Mt[:, :],
                                          sq_scr[:, 0:512]))
    k.drain("vector")
    t_g = [k.dma("sync", gvrep[:, :], c_t1, "tb2"),
           k.dma("sync", gkv[:, :], c_lmj, "tb2")]
    k.await_("vector", *t_g)
    k.op("vector", lambda e: e.tensor_scalar(gvrep[:, :], gvrep[:, :],
                                             lng[:, 0:1], None, ALU.mult))
    k.op("vector", lambda e: e.tensor_scalar(gkv[:, :], gkv[:, :],
                                             lng[:, 0:1], None, ALU.mult))
    k.op("vector", lambda e: e.tensor_scalar(gLv[:, :], lng[:, 0:1],
                                             float(LB), None, ALU.mult))
    k.drain("vector")
    t_v2 = k.sig("vector")
    k.await_("scalar", t_v2)
    k.op("scalar", lambda e: e.activation(gvrep[:, :], gvrep[:, :], AF.Exp))
    k.op("scalar", lambda e: e.activation(gkv[:, :], gkv[:, :], AF.Exp))
    t_gl = k.op("scalar", lambda e: e.activation(gLv[:, :], gLv[:, :],
                                                 AF.Exp), sig=True)
    k.await_("vector", t_gl)
    k.op("vector", lambda e: e.tensor_scalar(dgl[:, :], id_f[:, :],
                                             gLv[:, 0:1], None, ALU.mult))
    k.drain("vector")
    t_tables = k.sig("vector")

    p0es = ExitStack()

    def sbuf0(name, shape, dt=F32):
        return p0es.enter_context(nc.sbuf_tensor(name, shape, dt))

    wnat = sbuf0("p0_wnat", [128, 2 * FF])
    wtb = [sbuf0(f"p0_wt{i}", [128, 512]) for i in range(2)]
    srep = sbuf0("p0_sr", [128, 512])
    wi32 = sbuf0("p0_wi", [128, 2 * 512], I32)
    wr0 = sbuf0("p0_wr0", [128, 2 * 512])
    wqb = [sbuf0(f"p0_wqb{i}", [128, 512], BF16) for i in range(2)]

    def wquant(w_nat, wT, sw_sb, sw_hbm, wq_hbm, nf, nd):
        nft = nf // 128
        tls = [None] * nft
        for ft in range(min(2, nft)):
            tls[ft] = k.dma("scalar", wnat[:, (ft % 2) * FF:(ft % 2) * FF + nd],
                            w_nat[ft * 128:(ft + 1) * 128, :], f"wn{ft % 2}")
        for ft in range(nft):
            b = ft % 2
            k.await_("vector", tls[ft])
            k.op("vector", lambda e, ft=ft, b=b: e.reduce_max(
                sw_sb[:, ft:ft + 1], wnat[:, b * FF:b * FF + nd], axis=AX,
                apply_absolute_value=True))
            k.drain("vector")
            t_red = k.sig("vector")
            if ft + 2 < nft:
                k.await_("scalar", t_red)
                tls[ft + 2] = k.dma(
                    "scalar", wnat[:, b * FF:b * FF + nd],
                    w_nat[(ft + 2) * 128:(ft + 3) * 128, :], f"wn{b}")
        k.drain("vector")
        k.op("vector", lambda e: e.tensor_scalar(
            sw_sb[:, 0:nft], sw_sb[:, 0:nft], EPS_Q, 1.0 / 127.0,
            ALU.max, ALU.mult))
        k.drain("vector")
        t_sw = k.sig("vector")
        t_sw_pe = t_sw
        k.await_("gpsimd", t_sw)
        t_st = k.dma("gpsimd", sw_hbm.rearrange("o (c p) -> (o p) c", p=128),
                     sw_sb[:, 0:nft], "wst", slow=True)
        ndt, nfb = nd // 128, nf // 512
        t_wq = wq_state["t_wq"]
        t_wtbA = wq_state["t_wtbA"]
        t_wtbC = wq_state["t_wtbC"]
        t_wB = wq_state["t_wB"]
        t_srep_free = wq_state["t_srep_free"]
        for fb in range(nfb):
            t_row = col2row(swrow, lambda j, fb=fb: sw_sb[:, fb * 4 + j:
                                                          fb * 4 + j + 1],
                            4, t_sw_pe)
            k.await_("tensor", t_srep_free)
            t_bc = bcastN(lambda o, w: srep[:, o:o + w], swrow, 512, t_row,
                          "scalar")
            k.await_("vector", t_bc, t_srep_free)
            k.op("vector", lambda e: e.reciprocal(srep[:, :], srep[:, :]))
            k.drain("vector")
            t_srep_rdy = k.sig("vector")
            for dt in range(ndt + 1):
                if dt < ndt:
                    b = dt % 2
                    k.await_("sync", t_wtbC[b])
                    tw = k.dma("sync", wtb[b][:, :],
                               wT[dt * 128:(dt + 1) * 128,
                                  fb * 512:(fb + 1) * 512], f"wt{b}")
                    k.await_("vector", tw, t_srep_rdy)
                    k.op("vector", lambda e, b=b: e.tensor_mul(
                        wtb[b][:, :], wtb[b][:, :], srep[:, :]))
                    k.drain("vector")
                    k.op("vector", lambda e, b=b: e.tensor_scalar(
                        wtb[b][:, :], wtb[b][:, :], 128.0, None, ALU.add))
                    k.drain("vector")
                    t_wtbA[b] = k.sig("vector")
                    k.await_("scalar", t_wtbA[b], t_wtbC[b])
                    k.op("scalar", lambda e, b=b: e.copy(
                        wi32[:, b * 512:(b + 1) * 512], wtb[b][:, :]))
                    k.drain("scalar")
                    t_wB[b] = k.op("scalar", lambda e, b=b: e.copy(
                        wr0[:, b * 512:(b + 1) * 512],
                        wi32[:, b * 512:(b + 1) * 512]), sig=True)
                if dt >= 1:
                    dtc = dt - 1
                    b = dtc % 2
                    k.await_("vector", t_wB[b], t_wq[b])
                    k.op("vector", lambda e, b=b: e.tensor_sub(
                        wtb[b][:, :], wtb[b][:, :],
                        wr0[:, b * 512:(b + 1) * 512]))
                    k.drain("vector")
                    k.op("vector", lambda e, b=b: e.scalar_tensor_tensor(
                        wr0[:, b * 512:(b + 1) * 512], wtb[b][:, :], 0.5,
                        wr0[:, b * 512:(b + 1) * 512], ALU.is_ge, ALU.add))
                    k.drain("vector")
                    k.op("vector", lambda e, b=b: e.tensor_scalar(
                        wr0[:, b * 512:(b + 1) * 512],
                        wr0[:, b * 512:(b + 1) * 512], 128.0, -128.0,
                        ALU.subtract, ALU.max))
                    k.drain("vector")
                    k.op("vector", lambda e, b=b: e.tensor_scalar(
                        wqb[b][:, :], wr0[:, b * 512:(b + 1) * 512], 127.0,
                        None, ALU.min))
                    k.drain("vector")
                    t_wtbC[b] = k.sig("vector")
                    k.await_("gpsimd", t_wtbC[b])
                    t_wq[b] = k.dma("gpsimd",
                                    wq_hbm[dtc * 128:(dtc + 1) * 128,
                                           fb * 512:(fb + 1) * 512],
                                    wqb[b][:, :], f"ws{b}")
            t_srep_free = t_wtbA[(ndt - 1) % 2]
            wq_state["t_srep_free"] = t_srep_free

    wq_state = {"t_wq": [None, None], "t_wtbA": [None, None],
                "t_wtbC": [None, None], "t_wB": [None, None],
                "t_srep_free": None}
    if P0LVL >= 2 and P0LVL != 4:
        wquant(w1_h, w1T_h, sw1b, sw1_h, w1q_h, FF, D)
    if P0LVL >= 3:
        wquant(w2_h, w2T_h, sw2b, sw2_h, w2q_h, D, FF)

    _ln_prev = [None]
    t_x_global = [None]

    def ln_stats(xtile, idx, want_max, t_x, t_slot_free=None):
        t_x_global[0] = t_x
        if LNLVL < 1:
            return None
        """stats of xtile into st_mu/st_rs[:, idx]; scale/bias form:
        ln(x) = st_rs*x + st_mu  (st_mu = -mu*rstd)."""
        k.await_("scalar", t_x, _ln_prev[0], t_slot_free)
        t_sq = k.op("scalar", lambda e: e.activation(
            sq_scr[:, :], xtile, AF.Square), sig=True)
        if LNLVL < 2:
            return t_sq
        k.await_("vector", t_x, t_slot_free)
        k.op("vector", lambda e: e.reduce_sum(s_tmp[:, 0:1], xtile, axis=AX))
        if want_max:
            k.op("vector", lambda e, i=idx: e.reduce_max(
                st_mx[:, i:i + 1], xtile, axis=AX))
        k.await_("vector", t_sq)
        k.op("vector", lambda e: e.reduce_sum(s_tmp[:, 1:2], sq_scr[:, :],
                                              axis=AX))
        k.drain("vector")
        if LNLVL < 3:
            return k.sig("vector")
        k.op("vector", lambda e, i=idx: e.tensor_scalar(
            st_mu[:, i:i + 1], s_tmp[:, 0:1], 1.0 / D, None, ALU.mult))
        k.drain("vector")
        k.op("vector", lambda e: e.tensor_scalar(
            s_tmp[:, 1:2], s_tmp[:, 1:2], 1.0 / D, None, ALU.mult))
        k.drain("vector")
        k.op("vector", lambda e, i=idx: e.scalar_tensor_tensor(
            s_tmp[:, 2:3], st_mu[:, i:i + 1], 1.0, st_mu[:, i:i + 1],
            ALU.bypass, ALU.mult))
        k.drain("vector")
        k.op("vector", lambda e: e.tensor_sub(s_tmp[:, 2:3], s_tmp[:, 1:2],
                                              s_tmp[:, 2:3]))
        k.drain("vector")
        k.op("vector", lambda e: e.tensor_scalar(
            s_tmp[:, 2:3], s_tmp[:, 2:3], EPS_LN, None, ALU.add))
        k.drain("vector")
        t_var = k.sig("vector")
        if LNLVL < 4:
            return t_var
        k.await_("scalar", t_var)
        t_rs = k.op("scalar", lambda e, i=idx: e.activation(
            st_rs[:, i:i + 1], s_tmp[:, 2:3], AF.Sqrt), sig=True)
        k.await_("vector", t_rs)
        k.op("vector", lambda e, i=idx: e.reciprocal(
            st_rs[:, i:i + 1], st_rs[:, i:i + 1]))
        k.drain("vector")
        k.op("vector", lambda e, i=idx: e.scalar_tensor_tensor(
            st_mu[:, i:i + 1], st_mu[:, i:i + 1], -1.0, st_rs[:, i:i + 1],
            ALU.mult, ALU.mult))
        k.drain("vector")
        _ln_prev[0] = k.sig("vector")
        return _ln_prev[0]

    def ln_apply(dst, xtile, idx, t_stats, sig=False):
        if LNLVL < 0:
            return k.sig("scalar")
        k.await_("scalar", t_stats)
        if LNLVL < 5:
            k.await_("scalar", t_x_global[0])
            return k.op("scalar", lambda e: e.copy(dst, xtile), sig=sig)
        return k.op("scalar", lambda e, i=idx: e.activation(
            dst, xtile, AF.Identity, bias=st_mu[:, i:i + 1],
            scale=st_rs[:, i:i + 1]), sig=sig)

    def barrier():
        for e, chans in k.chan_eng.items():
            for nm in sorted(chans):
                v = k.cnt.get(nm, 0)
                if v:
                    s = k.sem[nm]
                    k.ops[e].append(lambda en, s=s, v=v: en.wait_ge(s, v))
        toks = [k.sig(e) for e in ENGS]
        for e in ENGS:
            k.await_(e, *toks)

    barrier()
    p0es.close()
    if nphases < 1:
        blk = es.enter_context(nc.Block())

        def runner0(eng_name):
            def fn(e):
                for thunk in k.ops[eng_name]:
                    thunk(e)
            return fn
        blk.sync(runner0("sync"))
        blk.vector(runner0("vector"))
        blk.scalar(runner0("scalar"))
        blk.tensor(runner0("tensor"))
        blk.gpsimd(runner0("gpsimd"))
        es.close()
        return nc

    def build_p1():
        p = ExitStack()

        def psbuf(name, shape, dt=F32):
            return p.enter_context(nc.sbuf_tensor(name, shape, dt))

        if "N" in P1SUB:
            p.close()
            return
        u64 = psbuf("a_u64", [128, 8 * CH])
        xlnT = u64[:, :]
        wo_sb = u64[:, 0:8 * 1024]
        yT = u64[:, 8 * 1024:8 * 1024 + 1024]
        x1o = u64[:, 9 * 1024 + 256:11 * 1024 + 256]
        ytm = psbuf("a_ytm", [128, 16 * 1024])
        Qp = psbuf("a_Qp", [128, CH])
        Kp = psbuf("a_Kp", [128, CH])
        vaug = psbuf("a_vaug", [128, 16 * 129 + 1])
        PT = psbuf("a_PT", [128, 4 * 512])
        wq3 = psbuf("a_wq3", [128, 3 * 1024])
        xld = psbuf("a_xld", [128, 2 * 1024])
        xlb = psbuf("a_xlb", [128, 2 * 1024])

        t_chunk_pe = None
        t_ytm_freed = None
        t_pair_pe = None
        t_bank_ev = {i: None for i in range(8)}
        t_avevac = [[None, None] for _ in range(4)]
        t_ptfree = [[None, None], [None, None]]
        t_x1st = {}
        t_xuse = [None, None]
        t_xlb_pe = [None, None]
        t_trd = [None, None]
        t_slot = [None] * 4
        for c in range(2):
            tc = c * CH
            k.await_("scalar", t_x1st.get(0), t_x1st.get(1))
            for tt in range(16 if "A" in P1SUB else 0):
                b = tt % 2
                xs_ = xld[:, b * 1024:(b + 1) * 1024]
                xb_ = xlb[:, b * 1024:(b + 1) * 1024]
                k.await_("sync", t_xuse[b])
                t_x = k.dma("sync", xs_,
                            x_in[tc + tt * 128:tc + tt * 128 + 128, :],
                            f"xa{b}")
                t_st_ = ln_stats(xs_, tt % 4, False, t_x, t_slot[tt % 4])
                k.await_("scalar", t_xlb_pe[b], t_chunk_pe)
                t_ap = ln_apply(xb_, xs_, tt % 4, t_st_, sig=True)
                t_slot[tt % 4] = t_ap
                t_xuse[b] = t_ap
                for dt in range(8 if "T" in P1SUB else 0):
                    j = tt * 8 + dt
                    bank = 6 + (j % 2)
                    k.await_("tensor", t_ap, t_trd[j % 2])
                    t_tr = k.op("tensor", lambda e, bank=bank, dt=dt, xb_=xb_:
                                e.transpose(psb(bank, 128),
                                            xb_[:, dt * 128:(dt + 1) * 128],
                                            id_f[:, :]), sig=True)
                    t_xlb_pe[b] = t_tr
                    k.await_("scalar", t_tr)
                    t_ev = k.op("scalar", lambda e, bank=bank, dt=dt, tt=tt:
                                e.copy(xlnT[:, dt * CH + tt * 128:
                                            dt * CH + tt * 128 + 128],
                                       psb(bank, 128)), sig=True)
                    t_trd[j % 2] = t_ev
            t_A_done = t_trd[(16 * 8 - 1) % 2] if "T" in P1SUB else None
            if "B" not in P1SUB:
                continue

            for hp in range(8):
                k.await_("sync", t_pair_pe, t_chunk_pe)
                t_w = [k.dma("sync",
                             wq3[:, j * 1024:(j + 1) * 1024]
                             .rearrange("p (dt e) -> p dt e", e=128),
                             W[:, hp * 128:(hp + 1) * 128]
                             .rearrange("(dt p) e -> p dt e", p=128), "wp")
                       for j, W in enumerate((aWqT, aWkT, aWvT))]
                t_qk = []
                for (dst, wj) in ((Qp, 0), (Kp, 1)):
                    for nb in range(4):
                        bank = nb % 2
                        k.await_("tensor", *t_w, t_bank_ev[bank], t_A_done,
                                 t_pair_pe)
                        tok = None
                        for dt in range(8):
                            tok = k.op("tensor", lambda e, bank=bank, wj=wj,
                                       dt=dt, nb=nb:
                                       e.matmul(psb(bank),
                                                wq3[:, wj * 1024 + dt * 128:
                                                    wj * 1024 + dt * 128 + 128],
                                                xlnT[:, dt * CH + nb * 512:
                                                     dt * CH + nb * 512 + 512],
                                                start=(dt == 0),
                                                stop=(dt == 7)),
                                       sig=(dt == 7))
                        k.await_("scalar", tok)
                        t_bank_ev[bank] = k.op(
                            "scalar", lambda e, bank=bank, dst=dst, nb=nb:
                            e.copy(dst[:, nb * 512:(nb + 1) * 512], psb(bank)),
                            sig=True)
                        t_qk.append(t_bank_ev[bank])
                k.await_("vector", t_pair_pe)
                k.op("vector", lambda e: e.memset(
                    vaug[:, 0:16 * 129]
                    .rearrange("p (kt c) -> p kt c", c=129)[:, :, 64:65], 1.0))
                k.drain("vector")
                t_ones = k.sig("vector")
                t_ve = None
                for kt in range(16):
                    k.await_("tensor", t_bank_ev[2], t_pair_pe)
                    tokv = None
                    for dt in range(8):
                        tokv = k.op("tensor", lambda e, kt=kt, dt=dt:
                                    e.matmul(psb(2, 128),
                                             xlnT[:, dt * CH + kt * 128:
                                                  dt * CH + kt * 128 + 128],
                                             wq3[:, 2 * 1024 + dt * 128:
                                                 2 * 1024 + dt * 128 + 128],
                                             start=(dt == 0), stop=(dt == 7)),
                                    sig=(dt == 7))
                    k.await_("scalar", tokv, t_ones)
                    t_ve = k.op(
                        "scalar", lambda e, kt=kt:
                        e.copy(vaug[:, kt * 129:kt * 129 + 130]
                               .rearrange("p (a c) -> p a c", c=65)
                               [:, :, 0:64],
                               psb(2, 128).rearrange("p (a c) -> p a c",
                                                     c=64)),
                        sig=True)
                    t_bank_ev[2] = t_ve
                k.await_("tensor", t_ve, t_ones, *t_qk)
                for qb in range(4):
                    ktmax = 4 * qb + 4
                    t_avstop = [[None, None] for _ in range(4)]
                    pend = [None, None]

                    def emit_av(g, kt, t_gate):
                        off = max(0, kt - 4 * qb) * 128
                        pt0 = (g * 2 + kt % 2) * 512
                        k.await_("tensor", t_gate)
                        if kt == 0 and g == 0:
                            k.await_("tensor", *[t for qs in range(4)
                                                 for t in t_avevac[qs]])
                        for qs in range(off // 128, 4):
                            avb = 4 + qs // 2
                            avo = (qs % 2) * 256
                            first = (kt == 0 and g == 0 and qs % 2 == 0)
                            last = (kt == 4 * qb + qs)
                            tk = k.op("tensor", lambda e, avb=avb, avo=avo,
                                      g=g, pt0=pt0, qs=qs, kt=kt,
                                      first=first, last=last:
                                      e.matmul(
                                          psb(avb, 65, avo + 65 * g),
                                          PT[:, pt0 + qs * 128:
                                              pt0 + qs * 128 + 128],
                                          vaug[:, kt * 129 + g * 64:
                                               kt * 129 + g * 64 + 65],
                                          start=first, stop=last,
                                          skip_group_check=True),
                                      sig=last)
                            if last:
                                t_avstop[qs][g] = tk
                        t_ptfree[g][kt % 2] = k.sig("tensor")

                    for kt in range(ktmax):
                        off = max(0, kt - 4 * qb) * 128
                        Nq = 512 - off
                        for g in range(2):
                            sb_ = g * 2 + kt % 2
                            pt0 = (g * 2 + kt % 2) * 512
                            if pend[g] is not None:
                                emit_av(g, *pend[g])
                                pend[g] = None
                            if kt == 0 and g == 1:
                                k.op("tensor", lambda e: e.drain())
                            k.await_("tensor", t_bank_ev[sb_])
                            t_s = k.op(
                                "tensor", lambda e, sb_=sb_, off=off,
                                Nq=Nq, g=g, kt=kt, qb=qb:
                                e.matmul(psb(sb_, Nq, off),
                                         Kp[g * 64:(g + 1) * 64,
                                            kt * 128:kt * 128 + 128],
                                         Qp[g * 64:(g + 1) * 64,
                                            qb * 512 + off:qb * 512 + 512],
                                         start=True, stop=True), sig=True)
                            k.await_("scalar", t_s, t_ptfree[g][kt % 2])
                            t_e = k.op("scalar", lambda e, pt0=pt0, off=off,
                                       Nq=Nq, sb_=sb_:
                                       e.activation(
                                           PT[:, pt0 + off:pt0 + 512],
                                           psb(sb_, Nq, off), AF.Exp,
                                           scale=0.125), sig=True)
                            t_bank_ev[sb_] = t_e
                            t_gate = t_e
                            if kt >= 4 * qb:
                                k.await_("gpsimd", t_e)
                                t_gate = k.op(
                                    "gpsimd", lambda e, pt0=pt0, off=off:
                                    e.tensor_mul(
                                        PT[:, pt0 + off:pt0 + off + 128],
                                        PT[:, pt0 + off:pt0 + off + 128],
                                        tri[:, :]), sig=True)
                            pend[g] = (kt, t_gate)
                    for g in range(2):
                        if pend[g] is not None:
                            emit_av(g, *pend[g])
                            pend[g] = None
                    for qs in range(4):
                        tsub = qb * 4 + qs
                        avb = 4 + qs // 2
                        avo = (qs % 2) * 256
                        k.await_("vector", t_avstop[qs][0], t_avstop[qs][1],
                                 t_ytm_freed)
                        k.op("vector", lambda e, avb=avb, avo=avo:
                             e.reciprocal(s_tmp[:, 0:1],
                                          psb(avb, 1, avo + 64)))
                        k.drain("vector")
                        k.op("vector", lambda e, avb=avb, avo=avo:
                             e.reciprocal(s_tmp[:, 1:2],
                                          psb(avb, 1, avo + 65)))
                        k.drain("vector")
                        k.op("vector", lambda e, avb=avb, avo=avo, tsub=tsub,
                             hp=hp: e.tensor_scalar(
                                 ytm[:, tsub * 1024 + hp * 128:
                                     tsub * 1024 + hp * 128 + 64],
                                 psb(avb, 64, avo), s_tmp[:, 0:1], None,
                                 ALU.mult))
                        k.op("vector", lambda e, avb=avb, avo=avo, tsub=tsub,
                             hp=hp: e.tensor_scalar(
                                 ytm[:, tsub * 1024 + hp * 128 + 64:
                                     tsub * 1024 + hp * 128 + 128],
                                 psb(avb, 64, avo + 66), s_tmp[:, 1:2], None,
                                 ALU.mult))
                        k.drain("vector")
                        tv = k.sig("vector")
                        t_avevac[qs] = [tv, tv]
                t_pair_pe = k.sig("tensor")
            t_B_vec = k.sig("vector")
            if "C" not in P1SUB:
                t_chunk_pe = k.sig("tensor")
                t_ytm_freed = t_chunk_pe
                continue

            k.await_("sync", t_pair_pe, t_B_vec)
            t_wo = [k.dma("sync", wo_sb[:, ec * 1024:(ec + 1) * 1024],
                          aWoT[ec * 128:(ec + 1) * 128, :], "wo")
                    for ec in range(8)]
            t_yT_free = None
            t_res_use = list(t_xuse)
            t_trd2 = list(t_trd)
            k.await_("tensor", t_B_vec)
            for tsub in range(16):
                for ec in range(8):
                    bank = 6 + ec % 2
                    k.await_("tensor", t_trd2[ec % 2])
                    t_tr = k.op("tensor", lambda e, bank=bank, tsub=tsub,
                                ec=ec: e.transpose(
                                    psb(bank, 128),
                                    ytm[:, tsub * 1024 + ec * 128:
                                        tsub * 1024 + ec * 128 + 128],
                                    id_f[:, :]), sig=True)
                    k.await_("scalar", t_tr, t_yT_free)
                    t_trd2[ec % 2] = k.op(
                        "scalar", lambda e, bank=bank, ec=ec:
                        e.copy(yT[:, ec * 128:(ec + 1) * 128], psb(bank, 128)),
                        sig=True)
                b = tsub % 2
                k.await_("sync", t_res_use[b])
                t_xr = k.dma("sync", xld[:, b * 1024:(b + 1) * 1024],
                             x_in[tc + tsub * 128:tc + tsub * 128 + 128, :],
                             f"xa{b}")
                t_o = [None, None]
                k.await_("tensor", t_bank_ev[0], t_bank_ev[1], t_trd2[0],
                         t_trd2[1], *t_wo)
                for nb in range(2):
                    bank = nb
                    for ec in range(8):
                        tko = k.op("tensor", lambda e, bank=bank, ec=ec,
                                   nb=nb: e.matmul(
                                       psb(bank),
                                       yT[:, ec * 128:(ec + 1) * 128],
                                       wo_sb[:, ec * 1024 + nb * 512:
                                             ec * 1024 + nb * 512 + 512],
                                       start=(ec == 0), stop=(ec == 7)),
                                   sig=(ec == 7))
                    t_o[nb] = tko
                t_yT_free = k.sig("tensor")
                for nb in range(2):
                    k.await_("vector", t_o[nb], t_xr, t_x1st.get(b))
                    k.op("vector", lambda e, nb=nb, b=b: e.tensor_add(
                        x1o[:, b * 1024 + nb * 512:b * 1024 + nb * 512 + 512],
                        psb(nb),
                        xld[:, b * 1024 + nb * 512:b * 1024 + nb * 512 + 512]))
                    k.drain("vector")
                    t_bank_ev[nb] = k.sig("vector")
                t_res_use[b] = t_bank_ev[1]
                k.await_("gpsimd", t_bank_ev[0], t_bank_ev[1])
                t_x1st[b] = k.dma(
                    "gpsimd", xs1[tc + tsub * 128:tc + tsub * 128 + 128, :],
                    x1o[:, b * 1024:(b + 1) * 1024], f"x1s{b}")
            t_xuse = list(t_res_use)
            t_chunk_pe = k.sig("tensor")
            t_ytm_freed = t_chunk_pe
        if "Z" not in P1SUB:
            barrier()
        p.close()

    if nphases >= 1:
        build_p1()

    sc2 = sbuf("sc2", [1, 4])
    ccs = sbuf("ccs", [1, 128])
    f_ssw1 = sbuf("f_ssw1", [128, 32])
    f_b1q = sbuf("f_b1q", [128, 32])
    f_ssw2 = sbuf("f_ssw2", [128, 8])
    f_b2q = sbuf("f_b2q", [128, 8])
    f_i32s = sbuf("f_i32s", [128, 64], I32)
    hmax = sbuf("hmax", [128, 1])

    def global_max_scale(src_col, slot, t_ready):
        k.await_("tensor", t_ready)
        t_tr = k.op("tensor", lambda e: e.transpose(
            psb(7, 128, p1=1), src_col, id_f[:, :]), sig=True)
        k.await_("vector", t_tr)
        k.op("vector", lambda e: e.reduce_max(sc2[0:1, 0:1],
                                              psb(7, 128, p1=1), axis=AX))
        k.drain("vector")
        k.op("vector", lambda e: e.tensor_scalar(
            ccs[0:1, :], ones_r[0:1, :], sc2[0:1, 0:1], None, ALU.mult))
        k.drain("vector")
        t_v = k.sig("vector")
        k.await_("gpsimd", t_v)
        t_ci = k.dma("gpsimd", cc_i.ap(), ccs[0:1, :], "cc0")
        k.await_("gpsimd", t_ci)
        t_cc = k.cc_allreduce_max(cc_i.ap(), cc_o.ap())
        k.await_("gpsimd", t_cc)
        t_co = k.dma("gpsimd", sc2[0:1, 0:1], cc_o.ap()[0:1, 0:1], "cc0")
        k.await_("vector", t_co)
        k.op("vector", lambda e: e.tensor_scalar(
            sc2[0:1, 0:1], sc2[0:1, 0:1], EPS_Q, 1.0 / 255.0,
            ALU.max, ALU.mult))
        k.drain("vector")
        k.op("vector", lambda e: e.reciprocal(sc2[0:1, 1:2], sc2[0:1, 0:1]))
        k.drain("vector")
        t_v2 = k.sig("vector")
        t_bc = bcastN(lambda o, w, slot=slot: smv[:, slot + o:slot + o + w],
                      sc2, 2, t_v2)
        return t_bc

    def bias_quant(b_sb, sw_sb, ssw_dst, bq_dst, n, slot):
        k.op("vector", lambda e: e.tensor_scalar(
            ssw_dst[:, 0:n], sw_sb[:, 0:n], smv[:, slot:slot + 1], None,
            ALU.mult))
        k.drain("vector")
        k.op("vector", lambda e: e.reciprocal(sq_scr[:, 0:n], ssw_dst[:, 0:n]))
        k.drain("vector")
        k.op("vector", lambda e: e.tensor_mul(sq_scr[:, 32:32 + n],
                                              b_sb[:, 0:n], sq_scr[:, 0:n]))
        k.drain("vector")
        k.op("vector", lambda e: e.tensor_scalar(
            sq_scr[:, 32:32 + n], sq_scr[:, 32:32 + n], 32768.0, None,
            ALU.add))
        k.drain("vector")
        k.op("vector", lambda e: e.tensor_copy(f_i32s[:, 0:n],
                                               sq_scr[:, 32:32 + n]))
        k.drain("vector")
        k.op("vector", lambda e: e.tensor_copy(sq_scr[:, 64:64 + n],
                                               f_i32s[:, 0:n]))
        k.drain("vector")
        k.op("vector", lambda e: e.tensor_sub(sq_scr[:, 32:32 + n],
                                              sq_scr[:, 32:32 + n],
                                              sq_scr[:, 64:64 + n]))
        k.drain("vector")
        k.op("vector", lambda e: e.scalar_tensor_tensor(
            sq_scr[:, 64:64 + n], sq_scr[:, 32:32 + n], 0.5,
            sq_scr[:, 64:64 + n], ALU.is_ge, ALU.add))
        k.drain("vector")
        k.op("vector", lambda e: e.tensor_scalar(
            sq_scr[:, 64:64 + n], sq_scr[:, 64:64 + n], 32768.0, None,
            ALU.subtract))
        k.drain("vector")
        k.op("vector", lambda e: e.tensor_mul(bq_dst[:, 0:n],
                                              sq_scr[:, 64:64 + n],
                                              ssw_dst[:, 0:n]))
        k.drain("vector")
        return k.sig("vector")

    def build_ffn(xr_in, xw_out, tag=""):
        p = ExitStack()

        def psbuf(name, shape, dt=F32):
            return p.enter_context(nc.sbuf_tensor(name + tag, shape, dt))

        xld2 = psbuf("f_xld", [128, 4 * 1024])
        t_xu = [None, None, None, None]
        t_mx = None
        for tt in range(32):
            b = tt % 2
            xs_ = xld2[:, b * 1024:(b + 1) * 1024]
            k.await_("sync", t_xu[b])
            t_x = k.dma("sync", xs_, xr_in[tt * 128:tt * 128 + 128, :], f"xa{b}")
            t_st_ = ln_stats(xs_, tt, True, t_x)
            k.op("vector", lambda e, tt=tt: e.tensor_scalar(
                st_mx[:, tt:tt + 1], st_mx[:, tt:tt + 1],
                st_rs[:, tt:tt + 1], st_mu[:, tt:tt + 1],
                ALU.mult, ALU.add))
            k.drain("vector")
            t_xu[b] = t_st_
        k.op("vector", lambda e: e.reduce_max(st_mx[:, 32:33],
                                              st_mx[:, 0:32], axis=AX))
        k.drain("vector")
        t_mx = k.sig("vector")
        t_s1 = global_max_scale(st_mx[:, 32:33], 0, t_mx)
        k.await_("vector", t_s1)
        t_b1q = bias_quant(b1b, sw1b, f_ssw1, f_b1q, 32, 0)

        w1qs = psbuf("f_w1qs", [128, 8 * FF], BF16)
        xqT = psbuf("f_xqT", [128, 2 * 8 * 512], BF16)
        xqf = psbuf("f_xqf", [128, 4 * 1024])
        xqi = psbuf("f_xqi", [128, 1024], I32)
        xr0 = psbuf("f_xr0", [128, 1024])
        xqb = psbuf("f_xqb", [128, 4 * 1024])
        hsb = psbuf("f_hsb", [128, 8 * 512])
        rs2 = psbuf("f_rs2", [128, 32])
        mu2 = psbuf("f_mu2", [128, 32])
        t_w1l = [k.dma("sync", w1qs[:, dt * FF:(dt + 1) * FF],
                       w1q_h[dt * 128:(dt + 1) * 128, :], "wq1")
                 for dt in range(8)]
        k.op("vector", lambda e: e.memset(hmax[:, :], 0.0))
        k.drain("vector")
        t_hm = k.sig("vector")
        k.await_("vector", t_s1)
        k.op("vector", lambda e: e.tensor_scalar(
            rs2[:, 0:32], st_rs[:, 0:32], smv[:, 1:2], None, ALU.mult))
        k.drain("vector")
        k.op("vector", lambda e: e.tensor_scalar(
            mu2[:, 0:32], st_mu[:, 0:32], smv[:, 1:2], None, ALU.mult))
        k.drain("vector")
        t_rs2 = k.sig("vector")
        t_q4 = [None] * 4
        t_xq_done = [None, None]
        t_xqb_pe = [None] * 4
        t_trq = {6: None, 7: None}
        t_hsb_st = [None] * 8
        t_hsb_mx = [None] * 8
        t_bank_ev2 = {0: None, 1: None}

        def fb_quant_tile(tb, tt4):
            tt = tb * 4 + tt4
            s4 = tt % 4
            xs_ = xld2[:, s4 * 1024:(s4 + 1) * 1024]
            xq_ = xqf[:, s4 * 1024:(s4 + 1) * 1024]
            xb_ = xqb[:, s4 * 1024:(s4 + 1) * 1024]
            k.await_("sync", t_xu[s4])
            t_x = k.dma("sync", xs_, xr_in[tt * 128:tt * 128 + 128, :],
                        f"xa{s4}")
            k.await_("vector", t_x, t_rs2, t_xqb_pe[s4])
            k.op("vector", lambda e, xq_=xq_, xs_=xs_, tt=tt: e.tensor_scalar(
                xq_, xs_, rs2[:, tt:tt + 1], mu2[:, tt:tt + 1],
                ALU.mult, ALU.add))
            k.drain("vector")
            t_xu[s4] = k.sig("vector")
            k.op("vector", lambda e, xq_=xq_: e.tensor_copy(xqi[:, :], xq_))
            k.drain("vector")
            k.op("vector", lambda e: e.tensor_copy(xr0[:, :], xqi[:, :]))
            k.drain("vector")
            k.op("vector", lambda e, xq_=xq_: e.tensor_sub(
                xq_, xq_, xr0[:, :]))
            k.drain("vector")
            k.op("vector", lambda e, xq_=xq_: e.scalar_tensor_tensor(
                xr0[:, :], xq_, 0.5, xr0[:, :], ALU.is_ge, ALU.add))
            k.drain("vector")
            k.op("vector", lambda e, xb_=xb_: e.tensor_scalar(
                xb_, xr0[:, :], 0.0, None, ALU.max))
            k.drain("vector")
            t_q4[s4] = k.sig("vector")

        def fb_transposes(tb):
            half = tb % 2
            for tt4 in range(4):
                t_last = None
                for pb in range(2):
                    bank = 6 + pb
                    k.await_("tensor", t_q4[tt4], t_trq[bank])
                    t_tr = None
                    for dt4 in range(4):
                        dt = pb * 4 + dt4
                        t_tr = k.op("tensor", lambda e, bank=bank, dt4=dt4,
                                    dt=dt, tt4=tt4: e.transpose(
                                        psb(bank, 128, off=dt4 * 128),
                                        xqb[:, tt4 * 1024 + dt * 128:
                                            tt4 * 1024 + dt * 128 + 128],
                                        id_f[:, :]), sig=(dt4 == 3))
                    t_last = t_tr
                    k.await_("scalar", t_tr)
                    t_cp = k.op("scalar", lambda e, bank=bank, half=half,
                                pb=pb, tt4=tt4: e.copy(
                                    xqT[:, half * 4096 + pb * 2048:
                                        half * 4096 + pb * 2048 + 2048]
                                    .rearrange("p (g e) -> p g e", g=4)
                                    [:, :, tt4 * 128:tt4 * 128 + 128],
                                    psb(bank, 512).rearrange(
                                        "p (g e) -> p g e", g=4)), sig=True)
                    t_trq[bank] = t_cp
                t_xqb_pe[tt4] = t_last
            t_xq_done[half] = t_trq[7]

        def fb_mm1(tb):
            half = tb % 2
            k.await_("tensor", t_xq_done[half], *t_w1l, t_b1q)
            for ft in range(32):
                bank = ft % 2
                sl8 = ft % 8
                k.await_("tensor", t_bank_ev2[bank])
                tokm = None
                for dt in range(8):
                    tokm = k.op("tensor", lambda e, bank=bank, ft=ft, dt=dt,
                                half=half:
                                e.matmul(psb(bank),
                                         w1qs[:, dt * FF + ft * 128:
                                              dt * FF + ft * 128 + 128],
                                         xqT[:, half * 4096 + dt * 512:
                                             half * 4096 + (dt + 1) * 512],
                                         start=(dt == 0), stop=(dt == 7)),
                                sig=(dt == 7))
                k.await_("scalar", tokm, t_hsb_st[sl8], t_hsb_mx[sl8])
                t_h = k.op("scalar", lambda e, bank=bank, ft=ft, sl8=sl8:
                           e.activation(hsb[:, sl8 * 512:sl8 * 512 + 512],
                                        psb(bank), AF.Relu,
                                        bias=f_b1q[:, ft:ft + 1],
                                        scale=f_ssw1[:, ft:ft + 1]), sig=True)
                t_bank_ev2[bank] = t_h
                k.await_("vector", t_h, t_hm)
                k.op("vector", lambda e, sl8=sl8: e.reduce_max(
                    s_tmp[:, 3:4], hsb[:, sl8 * 512:sl8 * 512 + 512],
                    axis=AX))
                k.drain("vector")
                k.op("vector", lambda e: e.tensor_max(
                    hmax[:, :], hmax[:, :], s_tmp[:, 3:4]))
                k.drain("vector")
                t_hsb_mx[sl8] = k.sig("vector")
                k.await_("gpsimd", t_h)
                for s_ in range(2):
                    t_hsb_st[sl8] = k.dma(
                        "gpsimd",
                        hsp[tb * 2 + s_, :, ft * 256:(ft + 1) * 256],
                        hsb[:, sl8 * 512 + s_ * 256:
                            sl8 * 512 + s_ * 256 + 256], f"hs{ft % 2}")

        for tb in range(9):
            if tb < 8:
                for tt4 in range(4):
                    fb_quant_tile(tb, tt4)
            if tb >= 1:
                fb_mm1(tb - 1)
            if tb < 8:
                fb_transposes(tb)
        barrier()
        p.close()

        t_s2 = global_max_scale(hmax[:, :], 2, None)
        k.await_("vector", t_s2)
        t_b2q = bias_quant(b2b, sw2b, f_ssw2, f_b2q, 8, 2)
        p2 = ExitStack()

        def psbuf2(name, shape, dt=F32):
            return p2.enter_context(nc.sbuf_tensor(name + tag, shape, dt))

        s2rep = psbuf2("f_s2rep", [128, 1024])
        b2rep = psbuf2("f_b2rep", [128, 1024])
        t_row1 = col2row(swrow, lambda j: f_ssw2[:, j:j + 1], 8, t_b2q)
        t_r1 = bcastN(lambda o, w: s2rep[:, o:o + w], swrow, 1024, t_row1)
        t_row2 = col2row(swrow, lambda j: f_b2q[:, j:j + 1], 8, t_r1)
        t_r2 = bcastN(lambda o, w: b2rep[:, o:o + w], swrow, 1024, t_row2)

        w2qs = psbuf2("f_w2qs", [128, 32 * 1024], BF16)
        hld = psbuf2("f_hld", [128, 2 * 2048])
        hqi = psbuf2("f_hqi", [128, 2 * 2048], I32)
        hr0 = psbuf2("f_hr0", [128, 2 * 2048])
        hqb = psbuf2("f_hqb", [128, 2 * 8192], BF16)
        x2o = psbuf2("f_x2o", [128, 2 * 1024])
        xr2 = psbuf2("f_xr2", [128, 2 * 1024])
        t_w2l = [k.dma("sync", w2qs[:, ft * 1024:(ft + 1) * 1024],
                       w2q_h[ft * 128:(ft + 1) * 128, :], "wq2")
                 for ft in range(32)]
        t_ldC = [None, None]
        t_A = [None] * 4
        t_B = [None, None]
        t_C_by = {}
        t_hqb_pe = [None, None]
        t_x2st = [None, None]
        t_xr2_v = [None, None]
        t_bank_v = {i: None for i in range(4)}

        def fd_load(tb2, hh):
            qb_ = hh % 2
            k.await_("gpsimd", t_ldC[qb_])
            return k.dma("gpsimd", hld[:, qb_ * 2048:(qb_ + 1) * 2048],
                         hsp[tb2, :, hh * 2048:(hh + 1) * 2048], f"hl{qb_}")

        def fd_A(hh, t_hl):
            qb_ = hh % 2
            sl = slice(qb_ * 2048, (qb_ + 1) * 2048)
            k.await_("vector", t_hl, t_s2)
            k.op("vector", lambda e, sl=sl: e.tensor_scalar(
                hld[:, sl], hld[:, sl], smv[:, 3:4], None, ALU.mult))
            k.drain("vector")
            t_A[hh] = k.sig("vector")

        def fd_B(hh):
            qb_ = hh % 2
            sl = slice(qb_ * 2048, (qb_ + 1) * 2048)
            k.await_("scalar", t_A[hh], t_ldC[qb_])
            k.op("scalar", lambda e, sl=sl: e.copy(hqi[:, sl], hld[:, sl]))
            k.drain("scalar")
            t_B[qb_] = k.op("scalar", lambda e, sl=sl: e.copy(
                hr0[:, sl], hqi[:, sl]), sig=True)

        def fd_C(tb2, hh):
            qb_ = hh % 2
            b2 = tb2 % 2
            sl = slice(qb_ * 2048, (qb_ + 1) * 2048)
            k.await_("vector", t_B[qb_],
                     t_hqb_pe[b2] if hh == 0 else None)
            k.op("vector", lambda e, sl=sl: e.tensor_sub(
                hld[:, sl], hld[:, sl], hr0[:, sl]))
            k.drain("vector")
            k.op("vector", lambda e, sl=sl, hh=hh, b2=b2:
                 e.scalar_tensor_tensor(
                     hqb[:, b2 * 8192 + hh * 2048:
                         b2 * 8192 + (hh + 1) * 2048], hld[:, sl], 0.5,
                     hr0[:, sl], ALU.is_ge, ALU.add))
            k.drain("vector")
            tok = k.sig("vector")
            t_C_by[tb2][hh] = tok
            t_ldC[qb_] = tok

        def fd_quant(tb2):
            t_C_by[tb2] = [None] * 4
            t_hl = [fd_load(tb2, 0), fd_load(tb2, 1), None, None]
            fd_A(0, t_hl[0])
            fd_B(0)
            fd_A(1, t_hl[1])
            fd_B(1)
            fd_C(tb2, 0)
            t_hl[2] = fd_load(tb2, 2)
            fd_A(2, t_hl[2])
            fd_B(2)
            fd_C(tb2, 1)
            t_hl[3] = fd_load(tb2, 3)
            fd_A(3, t_hl[3])
            fd_B(3)
            fd_C(tb2, 2)
            fd_C(tb2, 3)

        def fd_mm2(tb2):
            b2 = tb2 % 2
            tCq = t_C_by.pop(tb2)
            for tsub in range(2):
                tt = tb2 * 2 + tsub
                bq = tsub % 2
                k.await_("sync", t_xr2_v[bq])
                t_xr = k.dma("sync", xr2[:, bq * 1024:(bq + 1) * 1024],
                             xr_in[tt * 128:tt * 128 + 128, :], f"xr{bq}")
                toks = [None, None]
                for dmh in range(2):
                    bank = tsub * 2 + dmh
                    k.await_("tensor", *tCq, t_bank_v[bank], *t_w2l)
                    tokm = None
                    for ft in range(32):
                        tokm = k.op(
                            "tensor", lambda e, bank=bank, ft=ft, tsub=tsub,
                            dmh=dmh, b2=b2: e.matmul(
                                psb(bank),
                                hqb[:, b2 * 8192 + ft * 256 + tsub * 128:
                                    b2 * 8192 + ft * 256 + tsub * 128 + 128],
                                w2qs[:, ft * 1024 + dmh * 512:
                                     ft * 1024 + dmh * 512 + 512],
                                start=(ft == 0), stop=(ft == 31)),
                            sig=(ft == 31))
                    toks[dmh] = tokm
                if tsub == 1:
                    t_hqb_pe[b2] = k.sig("tensor")
                for dmh in range(2):
                    bank = tsub * 2 + dmh
                    k.await_("vector", toks[dmh], t_xr, t_x2st[bq],
                             t_r1, t_r2)
                    dsl = slice(bq * 1024 + dmh * 512,
                                bq * 1024 + dmh * 512 + 512)
                    wsl = slice(dmh * 512, dmh * 512 + 512)
                    k.op("vector", lambda e, bank=bank, dsl=dsl, wsl=wsl:
                         e.scalar_tensor_tensor(
                             x2o[:, dsl], psb(bank), 1.0, s2rep[:, wsl],
                             ALU.bypass, ALU.mult))
                    k.drain("vector")
                    k.op("vector", lambda e, dsl=dsl, wsl=wsl: e.tensor_add(
                        x2o[:, dsl], x2o[:, dsl], b2rep[:, wsl]))
                    k.drain("vector")
                    k.op("vector", lambda e, dsl=dsl: e.tensor_add(
                        x2o[:, dsl], x2o[:, dsl], xr2[:, dsl]))
                    k.drain("vector")
                    t_bank_v[bank] = k.sig("vector")
                t_xr2_v[bq] = t_bank_v[tsub * 2 + 1]
                k.await_("gpsimd", t_bank_v[tsub * 2], t_bank_v[tsub * 2 + 1])
                t_x2st[bq] = k.dma(
                    "gpsimd", xw_out[tt * 128:tt * 128 + 128, :],
                    x2o[:, bq * 1024:(bq + 1) * 1024], f"x2s{bq}")

        for tb2 in range(17):
            if tb2 < 16:
                fd_quant(tb2)
            if tb2 >= 1:
                fd_mm2(tb2 - 1)
        barrier()
        p2.close()

    if nphases >= 2:
        build_ffn(xs1, xs2)

    def build_p3():
        SBW = 512
        p = ExitStack()

        def psbuf(name, shape, dt=F32):
            return p.enter_context(nc.sbuf_tensor(name, shape, dt))

        S_aug = psbuf("b_Saug", [128, 8 * 1025])
        xld3 = psbuf("b_xld", [128, 2 * 1024])
        xlb3 = psbuf("b_xlb", [128, 2 * 1024])
        xlnT3 = psbuf("b_xlnT", [128, 8 * SBW])
        qf = psbuf("b_qf", [128, 8 * SBW])
        kf = psbuf("b_kf", [128, 8 * SBW])
        qsc = psbuf("b_qsc", [128, 8 * 256])
        vau = psbuf("b_vau", [128, 4 * 1025])
        wvres = psbuf("b_wv", [128, 8 * 1024])
        wqk = psbuf("b_wqk", [128, 2 * 1024])
        ktld = psbuf("b_ktld", [128, 2 * 1024])
        A_sb = psbuf("b_A", [128, 2 * 256])
        ybuf = psbuf("b_y", [128, 2 * 1024])
        xr3 = psbuf("b_xr3", [128, 2 * 1024])

        for i in range(8):
            k.op("vector", lambda e, i=i: e.memset(
                S_aug[:, i * 1025:(i + 1) * 1025], 0.0))
        k.drain("vector")
        t_S0 = k.sig("vector")
        t_wv = [k.dma("sync", wvres[:, dt * 1024:(dt + 1) * 1024],
                      bWvT[dt * 128:(dt + 1) * 128, :], "wv")
               for dt in range(8)]

        t_b = {i: None for i in range(8)}
        t_tr3 = [None, None]
        t_xu3 = [None, None]
        t_xlb_pe3 = [None, None]
        t_wqk_pe = None
        t_qkv_w = [None, None]
        t_sb_pe = None
        t_vau_w = None
        t_ktld_w = None
        t_A_w = None
        t_y_done = [None, None]
        t_x3st = [None, None]
        t_Sev = [None] * 8
        t_slot3 = [None] * 4
        t_state_pe = t_S0
        for sb in range(8):
            t0 = sb * SBW
            for tt in range(4):
                gt = sb * 4 + tt
                b = tt % 2
                xs_ = xld3[:, b * 1024:(b + 1) * 1024]
                xb_ = xlb3[:, b * 1024:(b + 1) * 1024]
                k.await_("sync", t_xu3[b])
                t_x = k.dma("sync", xs_, xs2[t0 + tt * 128:t0 + tt * 128 + 128, :],
                            f"xa{b}")
                t_st_ = ln_stats(xs_, tt, False, t_x, t_slot3[tt])
                k.await_("scalar", t_xlb_pe3[b])
                t_ap = ln_apply(xb_, xs_, tt, t_st_, sig=True)
                t_slot3[tt] = t_ap
                t_xu3[b] = t_ap
                for dt in range(8):
                    j = tt * 8 + dt
                    bank = 6 + (j % 2)
                    k.await_("tensor", t_ap, t_tr3[j % 2], t_sb_pe, t_b[6])
                    t_tr = k.op("tensor", lambda e, bank=bank, dt=dt, xb_=xb_:
                                e.transpose(psb(bank, 128),
                                            xb_[:, dt * 128:(dt + 1) * 128],
                                            id_f[:, :]), sig=True)
                    t_xlb_pe3[b] = t_tr
                    k.await_("scalar", t_tr)
                    t_tr3[j % 2] = k.op(
                        "scalar", lambda e, bank=bank, dt=dt, tt=tt:
                        e.copy(xlnT3[:, dt * SBW + tt * 128:
                                     dt * SBW + tt * 128 + 128],
                               psb(bank, 128)), sig=True)
            t_ln_done = t_tr3[31 % 2]
            for i in range(8):
                k.await_("sync", t_wqk_pe)
                t_wl = [k.dma("sync",
                              wqk[:, j * 1024:(j + 1) * 1024]
                              .rearrange("p (dt e) -> p dt e", e=128),
                              W[:, i * 128:(i + 1) * 128]
                              .rearrange("(dt p) e -> p dt e", p=128), "wp")
                        for j, W in enumerate((bWqT, bWkT))]
                for (dst, wj, bias) in ((qf, 0, bqb), (kf, 1, bkb)):
                    bank = wj
                    k.await_("tensor", *t_wl, t_ln_done, t_b[bank])
                    tok = None
                    for dt in range(8):
                        tok = k.op("tensor", lambda e, bank=bank, wj=wj, dt=dt:
                                   e.matmul(psb(bank),
                                            wqk[:, wj * 1024 + dt * 128:
                                                wj * 1024 + dt * 128 + 128],
                                            xlnT3[:, dt * SBW:dt * SBW + SBW],
                                            start=(dt == 0), stop=(dt == 7)),
                                   sig=(dt == 7))
                    k.await_("scalar", tok, t_sb_pe)
                    t_b[bank] = k.op(
                        "scalar", lambda e, bank=bank, dst=dst, i=i,
                        bias=bias: e.activation(
                            dst[:, i * SBW:(i + 1) * SBW], psb(bank), AF.Relu,
                            bias=bias[:, i:i + 1]), sig=True)
                t_wqk_pe = k.sig("tensor")
            t_qk_done = t_b[1]
            k.op("vector", lambda e: e.memset(
                vau[:, :].rearrange("p (kt c) -> p kt c", c=1025)
                [:, :, 1024:1025], 1.0))
            k.drain("vector")
            t_ones3 = k.sig("vector")
            for kt in range(4):
                k.await_("tensor", t_b[2], t_b[3], t_sb_pe, *t_wv, t_ln_done)
                tokv = None
                for dt in range(8):
                    for nb in range(2):
                        tokv = k.op(
                            "tensor", lambda e, kt=kt, dt=dt, nb=nb:
                            e.matmul(psb(2 + nb),
                                     xlnT3[:, dt * SBW + kt * 128:
                                           dt * SBW + kt * 128 + 128],
                                     wvres[:, dt * 1024 + nb * 512:
                                           dt * 1024 + nb * 512 + 512],
                                     start=(dt == 0), stop=(dt == 7)),
                            sig=(dt == 7 and nb == 1))
                k.await_("vector", tokv, t_ones3, t_vau_w, t_state_pe)
                for nb in range(2):
                    k.op("vector", lambda e, kt=kt, nb=nb: e.tensor_add(
                        vau[:, kt * 1025 + nb * 512:kt * 1025 + nb * 512 + 512],
                        psb(2 + nb), bvrep[:, nb * 512:nb * 512 + 512]))
                k.drain("vector")
                t_vau_w = k.sig("vector")
                t_b[2] = t_vau_w
                t_b[3] = t_vau_w
            for cc in range(2):
                qoff = cc * 256
                k.await_("vector", t_qk_done, t_state_pe)
                for i in range(8):
                    k.op("vector", lambda e, i=i, qoff=qoff: e.tensor_mul(
                        qsc[:, i * 256:(i + 1) * 256],
                        qf[:, i * SBW + qoff:i * SBW + qoff + 256],
                        gvrep[:, :]))
                k.drain("vector")
                t_qsc = k.sig("vector")
                for i in range(8):
                    for kt2 in range(2):
                        j = i * 2 + kt2
                        bank = 6 + j % 2
                        k.await_("tensor", t_tr3[j % 2], t_qk_done, t_b[6])
                        t_tr = k.op(
                            "tensor", lambda e, bank=bank, i=i, kt2=kt2,
                            qoff=qoff: e.transpose(
                                psb(bank, 128),
                                kf[:, i * SBW + qoff + kt2 * 128:
                                   i * SBW + qoff + kt2 * 128 + 128],
                                id_f[:, :]), sig=True)
                        k.await_("scalar", t_tr, t_state_pe)
                        t_tr3[j % 2] = k.op(
                            "scalar", lambda e, bank=bank, i=i, kt2=kt2:
                            e.activation(
                                ktld[:, kt2 * 1024 + i * 128:
                                     kt2 * 1024 + i * 128 + 128],
                                psb(bank, 128), AF.Identity,
                                scale=gkv[:, kt2:kt2 + 1]), sig=True)
                t_kt_done = t_tr3[1]
                for jt in range(2):
                    bank = jt
                    k.await_("tensor", t_b[bank], t_qk_done)
                    tok = None
                    for i in range(8):
                        tok = k.op(
                            "tensor", lambda e, bank=bank, jt=jt, i=i,
                            qoff=qoff: e.matmul(
                                psb(bank, 256),
                                kf[:, i * SBW + qoff + jt * 128:
                                   i * SBW + qoff + jt * 128 + 128],
                                qf[:, i * SBW + qoff:i * SBW + qoff + 256],
                                start=(i == 0), stop=(i == 7)),
                            sig=(i == 7))
                    k.await_("vector", tok, t_A_w, t_state_pe)
                    k.op("vector", lambda e, bank=bank, jt=jt: e.tensor_mul(
                        A_sb[:, jt * 256:(jt + 1) * 256], psb(bank, 256),
                        Mt[:, jt * 256:(jt + 1) * 256]))
                    k.drain("vector")
                    t_A_w = k.sig("vector")
                    t_b[bank] = t_A_w
                for tsub in range(2):
                    gt = sb * 4 + cc * 2 + tsub
                    k.await_("tensor", t_qsc, t_A_w, t_kt_done,
                             t_b[4], t_b[5], t_b[6], *t_Sev)
                    grp = ((4, 512, 0), (5, 512, 512), (6, 1, 1024))
                    tok = None
                    for gi, (bank, w, so) in enumerate(grp):
                        for i in range(8):
                            tok = k.op(
                                "tensor", lambda e, bank=bank, w=w, so=so,
                                i=i, tsub=tsub: e.matmul(
                                    psb(bank, w),
                                    qsc[:, i * 256 + tsub * 128:
                                        i * 256 + tsub * 128 + 128],
                                    S_aug[:, i * 1025 + so:i * 1025 + so + w],
                                    start=(i == 0), stop=False,
                                    skip_group_check=True),
                                sig=False)
                        for jt in range(tsub + 1):
                            tok = k.op(
                                "tensor", lambda e, bank=bank, w=w, so=so,
                                jt=jt, tsub=tsub, cc=cc: e.matmul(
                                    psb(bank, w),
                                    A_sb[:, jt * 256 + tsub * 128:
                                         jt * 256 + tsub * 128 + 128],
                                    vau[:, (cc * 2 + jt) * 1025 + so:
                                        (cc * 2 + jt) * 1025 + so + w],
                                    start=False, stop=(jt == tsub),
                                    skip_group_check=True),
                                sig=(jt == tsub and gi == 2))
                    k.await_("vector", tok, t_y_done[tsub], t_x3st[tsub])
                    k.op("vector", lambda e: e.tensor_scalar(
                        s_tmp[:, 0:1], psb(6, 1), EPS_B, None, ALU.add))
                    k.drain("vector")
                    k.op("vector", lambda e: e.reciprocal(
                        s_tmp[:, 0:1], s_tmp[:, 0:1]))
                    k.drain("vector")
                    for nb in range(2):
                        k.op("vector", lambda e, nb=nb, tsub=tsub:
                             e.tensor_scalar(
                                 ybuf[:, tsub * 1024 + nb * 512:
                                      tsub * 1024 + nb * 512 + 512],
                                 psb(4 + nb), s_tmp[:, 0:1], None, ALU.mult))
                    k.drain("vector")
                    t_ynum = k.sig("vector")
                    t_b[4] = t_ynum
                    t_b[5] = t_ynum
                    t_b[6] = t_ynum
                    k.await_("sync", t_y_done[tsub])
                    t_xr = k.dma("sync", xr3[:, tsub * 1024:(tsub + 1) * 1024],
                                 xs2[gt * 128:gt * 128 + 128, :], f"xr{tsub}")
                    k.await_("vector", t_xr, t_x3st[tsub])
                    k.op("vector", lambda e, tsub=tsub: e.tensor_add(
                        ybuf[:, tsub * 1024:(tsub + 1) * 1024],
                        ybuf[:, tsub * 1024:(tsub + 1) * 1024],
                        xr3[:, tsub * 1024:(tsub + 1) * 1024]))
                    k.drain("vector")
                    t_y_done[tsub] = k.sig("vector")
                    k.await_("gpsimd", t_y_done[tsub])
                    t_x3st[tsub] = k.dma(
                        "gpsimd", xs3[gt * 128:gt * 128 + 128, :],
                        ybuf[:, tsub * 1024:(tsub + 1) * 1024], f"x3s{tsub}")
                for i in range(8):
                    k.await_("tensor", t_b[0], t_b[1], t_b[2], t_kt_done,
                             t_Sev[i])
                    grp = ((0, 512, 0), (1, 512, 512), (2, 1, 1024))
                    tok = None
                    for gi, (bank, w, so) in enumerate(grp):
                        for kt2 in range(2):
                            tok = k.op(
                                "tensor", lambda e, bank=bank, w=w, so=so,
                                i=i, kt2=kt2, cc=cc: e.matmul(
                                    psb(bank, w),
                                    ktld[:, kt2 * 1024 + i * 128:
                                         kt2 * 1024 + i * 128 + 128],
                                    vau[:, (cc * 2 + kt2) * 1025 + so:
                                        (cc * 2 + kt2) * 1025 + so + w],
                                    start=(kt2 == 0), stop=(kt2 == 1),
                                    skip_group_check=True),
                                sig=(kt2 == 1 and gi == 2))
                    k.await_("vector", tok)
                    for bank, w, so in grp:
                        k.op("vector", lambda e, i=i, bank=bank, w=w, so=so:
                             e.scalar_tensor_tensor(
                                 S_aug[:, i * 1025 + so:i * 1025 + so + w],
                                 S_aug[:, i * 1025 + so:i * 1025 + so + w],
                                 gLv[:, 0:1], psb(bank, w),
                                 ALU.mult, ALU.add))
                        k.drain("vector")
                    t_Sev[i] = k.sig("vector")
                    t_b[0] = t_Sev[i]
                    t_b[1] = t_Sev[i]
                    t_b[2] = t_Sev[i]
                t_state_pe = k.sig("tensor")
            t_sb_pe = k.sig("tensor")
        barrier()
        p.close()

    if nphases >= 3:
        build_p3()
    if nphases >= 4:
        build_ffn(xs3, y_out, tag="2")

    blk = es.enter_context(nc.Block())

    def runner(eng_name):
        def fn(e):
            for thunk in k.ops[eng_name]:
                thunk(e)
        return fn

    blk.sync(runner("sync"))
    blk.vector(runner("vector"))
    blk.scalar(runner("scalar"))
    blk.tensor(runner("tensor"))
    blk.gpsimd(runner("gpsimd"))
    es.close()
    return nc




def host_inputs(inputs, b):
    import ml_dtypes
    x = np.ascontiguousarray(np.asarray(inputs["L0_attn_a_hidden_states"])[b])
    T = lambda a: np.ascontiguousarray(np.asarray(a).T)
    C = np.ascontiguousarray
    d = {
        "x": x,
        "aWqT": T(inputs["L0_attn_a_Wq"]),
        "aWkT": T(inputs["L0_attn_a_Wk"]),
        "aWvT": T(inputs["L0_attn_a_Wv"]),
        "aWoT": T(inputs["L0_attn_a_Wo"]),
        "bWqT": T(inputs["L0_attn_b_Wq"]),
        "bWkT": T(inputs["L0_attn_b_Wk"]),
        "bWvT": T(inputs["L0_attn_b_Wv"]),
        "bq": C(np.asarray(inputs["L0_attn_b_bq"]).reshape(1, D)),
        "bk": C(np.asarray(inputs["L0_attn_b_bk"]).reshape(1, D)),
        "bv": C(np.asarray(inputs["L0_attn_b_bv"]).reshape(1, D)),
        "gam": C(np.asarray(inputs["L0_attn_b_gamma"], dtype=np.float32)
                 .reshape(1, 1)),
        "w1": C(np.asarray(inputs["L0_ffn_w1"])),
        "w1T": T(inputs["L0_ffn_w1"]),
        "w2": C(np.asarray(inputs["L0_ffn_w2"])),
        "w2T": T(inputs["L0_ffn_w2"]),
        "b1": C(np.asarray(inputs["L0_ffn_b1"]).reshape(1, FF)),
        "b2": C(np.asarray(inputs["L0_ffn_b2"]).reshape(1, D)),
        "c_id": np.eye(128, dtype=np.float32),
        "c_idb": np.eye(128).astype(ml_dtypes.bfloat16),
        "c_tri": np.triu(np.ones((128, 128), dtype=np.float32)),
        "c_tmj": np.maximum(
            np.arange(256)[None, :] - np.arange(256)[:, None], 0
        ).astype(np.float32),
        "c_m01": (np.arange(256)[None, :] >= np.arange(256)[:, None])
        .astype(np.float32),
        "c_t1": np.broadcast_to(np.arange(1, LB + 1, dtype=np.float32),
                                (128, LB)).copy(),
        "c_lmj": (LB - 1.0 - (np.arange(2)[None, :] * 128
                              + np.arange(128)[:, None])).astype(np.float32),
    }
    return d


_NC = {}


def kernel(**inputs):
    if "nc" not in _NC:
        _NC["nc"] = build_nc(dbg=False)
    in_maps = [host_inputs(inputs, b) for b in range(NCORES)]
    res = run_bass_kernel_spmd(_NC["nc"], in_maps, core_ids=list(range(NCORES)))
    return np.stack([np.asarray(res.results[b]["y"]) for b in range(NCORES)],
                    axis=0)
